# version 36
# baseline (speedup 1.0000x reference)
import numpy as np
import ml_dtypes
from contextlib import ExitStack
import concourse.bass as bass
import concourse.mybir as mybir
from concourse.bass_utils import run_bass_kernel_spmd

F32 = mybir.dt.float32
BF16 = mybir.dt.bfloat16
AF = mybir.ActivationFunctionType
ALU = mybir.AluOpType
AX = mybir.AxisListType.X
BF = ml_dtypes.bfloat16

B, N, D = 2, 8192, 1024
FCH = 8
TPC = 2048
CH = 512
NCH = TPC // CH
HEADS, DH, WIN, INNER, HID = 8, 64, 512, 512, 2730
QKS = 8.0
HALO = 514
TW = 1 + HALO + TPC
QCOL0 = 515
NEG = -1e9
EPS = 1e-5
ATTN_PIPE = True
ATTN_STAGES = "ABCDE"
W1BLK = [128] * 21 + [42]
W1OFF = np.cumsum([0] + W1BLK).tolist()
NBLK = len(W1BLK)

ENGS = ("pe", "act", "dve", "pool", "sp")


class Prog:

    QR = 12

    def __init__(self, nc, es):
        names = ["pe", "act", "dve", "pool", "sp", "cc1", "cc2", "cc3"]
        for q in ("sp", "pool", "act"):
            names += [f"q_{q}{i}" for i in range(self.QR)]
        self.nc = nc
        self.qrot = {"sp": 0, "pool": 0, "act": 0}
        self.sems = {s: es.enter_context(nc.semaphore(s)) for s in names}
        self.cnt = {s: 0 for s in names}
        self.plan = {e: [] for e in ENGS}
        self.lastw = {e: {} for e in ENGS}
        self.wr = {}
        self.rd = {}

    def _filter(self, eng, deps):
        out = []
        for d in deps:
            if d is None:
                continue
            s, v = d
            if self.lastw[eng].get(s, 0) < v:
                out.append((s, v))
                self.lastw[eng][s] = v
        return out

    def _rw_deps(self, reads, writes):
        deps = []
        for k in reads:
            deps.extend(self.wr.get(k, ()))
        for k in writes:
            deps.extend(self.wr.get(k, ()))
            deps.extend(self.rd.get(k, ()))
        return deps

    def _rw_post(self, tok, reads, writes):
        for k in reads:
            self.rd.setdefault(k, []).append(tok)
        for k in writes:
            self.wr.setdefault(k, []).append(tok)

    def step(self, eng, fn, reads=(), writes=(), extra=(), sem="auto", amt=1):
        deps = self._rw_deps(reads, writes) + list(extra)
        waits = self._filter(eng, deps)
        s = eng if sem == "auto" else sem
        self.cnt[s] += amt
        tok = (s, self.cnt[s])
        self.plan[eng].append((waits, fn, [(s, amt)]))
        self._rw_post(tok, reads, writes)
        return tok

    def dma(self, q, out, in_, reads=(), writes=(), extra=(), slow=False):
        i = self.qrot[q]
        self.qrot[q] = (i + 1) % self.QR
        s = f"q_{q}{i}"
        extra = list(extra) + [(s, self.cnt[s])]
        return self.step(
            q, lambda e: e.dma_start(out=out, in_=in_,
                                     allow_slow_non_contiguous=slow),
            reads, writes, extra, sem=s, amt=16)

    def mmg(self, parts, reads=(), writes=(), extra=()):
        deps = self._rw_deps(reads, writes) + list(extra)
        waits = self._filter("pe", deps)
        n = len(parts)
        for i, (o, l, r, tp) in enumerate(parts):
            st, sp = (i == 0), (i == n - 1)
            def f(e, o=o, l=l, r=r, st=st, sp=sp, tp=tp):
                return e.matmul(o, l, r, start=st, stop=sp, tile_position=tp)
            if sp:
                self.cnt["pe"] += 1
                tok = ("pe", self.cnt["pe"])
                self.plan["pe"].append((waits if i == 0 else [], f, [("pe", 1)]))
            else:
                self.plan["pe"].append((waits if i == 0 else [], f, []))
        self._rw_post(tok, reads, writes)
        return tok

    def mm1(self, out, lhsT, rhs, start, stop, reads=(), writes=(), extra=(),
            tp=None):
        def f(e):
            return e.matmul(out, lhsT, rhs, start=start, stop=stop,
                            tile_position=tp)
        return self.step("pe", f, reads, writes, extra)

    def mseq(self, parts, reads=(), writes=(), extra=()):
        deps = self._rw_deps(reads, writes) + list(extra)
        waits = self._filter("pe", deps)
        self.cnt["pe"] += 1
        tok = ("pe", self.cnt["pe"])
        n = len(parts)
        for i, (o, l, r, st, sp, tp) in enumerate(parts):
            def f(e, o=o, l=l, r=r, st=st, sp=sp, tp=tp):
                return e.matmul(o, l, r, start=st, stop=sp, tile_position=tp)
            self.plan["pe"].append((waits if i == 0 else [], f,
                                    [("pe", 1)] if i == n - 1 else []))
        self._rw_post(tok, reads, writes)
        return tok

    def cp(self, eng, out, in_, reads=(), writes=(), extra=()):
        return self.step(eng, lambda e: e.tensor_copy(out, in_),
                         reads, writes, extra)

    def act(self, out, in_, func, reads=(), writes=(), extra=(), bias=0.0,
            scale=1.0, accum_out=None):
        def f(e):
            return e.activation(out, in_, func, bias=bias, scale=scale,
                                accum_out=accum_out)
        return self.step("act", f, reads, writes, extra)

    def tt(self, eng, out, in0, in1, op, reads=(), writes=(), extra=()):
        return self.step(eng, lambda e: e.tensor_tensor(out, in0, in1, op),
                         reads, writes, extra)

    def ts(self, eng, out, in0, s1, op0, reads=(), writes=(), extra=(),
           s2=None, op1=ALU.bypass):
        def f(e):
            return e.tensor_scalar(out, in0, s1, s2, op0, op1)
        return self.step(eng, f, reads, writes, extra)

    def stt(self, eng, out, in0, scalar, in1, op0, op1, reads=(), writes=(),
            extra=()):
        def f(e):
            return e.scalar_tensor_tensor(out, in0, scalar, in1, op0, op1)
        return self.step(eng, f, reads, writes, extra)

    def red(self, out, in_, op, reads=(), writes=(), extra=(), negate=False):
        def f(e):
            return e.tensor_reduce(out, in_, AX, op, negate=negate)
        return self.step("dve", f, reads, writes, extra)

    def ttr(self, out, in0, in1, scalar, accum_out, op0, op1, reads=(),
            writes=(), extra=(), scale=1.0):
        def f(e):
            return e.tensor_tensor_reduce(out=out, in0=in0, in1=in1,
                                          scale=scale, scalar=scalar,
                                          op0=op0, op1=op1,
                                          accum_out=accum_out)
        return self.step("dve", f, reads, writes, extra)

    def recip(self, out, in_, reads=(), writes=(), extra=()):
        return self.step("dve", lambda e: e.reciprocal(out, in_),
                         reads, writes, extra)

    def memset(self, eng, ap, val, reads=(), writes=(), extra=()):
        return self.step(eng, lambda e: e.memset(ap, val), reads, writes, extra)

    def coll(self, kind, op, groups, ins, outs, sem, reads=(), writes=()):
        def f(e):
            return e.collective_compute(kind, op, replica_groups=groups,
                                        ins=ins, outs=outs)
        self.step("pool", f, reads, writes, sem=sem)
        return self.step("pool", lambda e: e.engine_nop(), (), writes)

    def barrier(self):
        toks = [(s, self.cnt[s]) for s in self.cnt if not s.startswith("cc")]
        for e in ENGS:
            w = self._filter(e, toks)
            if w:
                self.plan[e].append((w, None, []))
        self.wr = {}
        self.rd = {}

    def emit(self):
        nc, sems = self.nc, self.sems

        def replay(name):
            def f(eng):
                for waits, fn, incs in self.plan[name]:
                    for s, v in waits:
                        eng.wait_ge(sems[s], v)
                    if fn is not None:
                        inst = fn(eng)
                        for s, a in incs:
                            inst.then_inc(sems[s], a)
            return f

        with nc.Block() as block:
            block.tensor(replay("pe"))
            block.scalar(replay("act"))
            block.vector(replay("dve"))
            block.gpsimd(replay("pool"))
            block.sync(replay("sp"))



def host_shared():
    t = {}
    t["identb"] = np.eye(128, dtype=BF)
    t["onesb"] = np.ones((128, 1), BF)
    t["onesf"] = np.ones((1, 128), np.float32)
    hm2 = np.zeros((128, 128), np.float32)
    for g in range(4):
        hm2[g * 32:(g + 1) * 32, g * 32:(g + 1) * 32] = 1.0
    t["hm2b"] = hm2.astype(BF)
    j = np.arange(CH, dtype=np.float32)
    t["qexp"] = np.broadcast_to(j, (128, CH)).copy()
    t["kexp"] = np.broadcast_to(CH - 1 - j, (128, CH)).copy()
    p = np.arange(128, dtype=np.float32)[:, None]
    ecl, d01 = [], []
    for qt in range(NCH):
        E = (128 * qt + p) - 1.0 - j[None, :]
        ecl.append(np.maximum(E, 0.0))
        d01.append((E >= 0).astype(np.float32))
    t["eclamp"] = np.concatenate(ecl, 1)
    t["d01"] = np.concatenate(d01, 1)
    t["chexp"] = np.broadcast_to(np.arange(NCH, dtype=np.float32) * CH,
                                 (128, NCH)).copy()
    return t


def host_percore(core):
    b, s = core // 4, core % 4
    t0 = s * TPC
    out = {}
    ce = np.zeros((128, 4), np.float32)
    cv = np.zeros((128, 4), np.float32)
    for cp in range(s):
        ce[:, cp] = TPC * (s - 1 - cp)
        cv[:, cp] = 1.0
    out["coefexp"], out["coefval"] = ce, cv
    hs = np.zeros((128, 4), np.float32)
    if s > 0:
        hs[:, s - 1] = 1.0
    out["hsel"] = hs
    out["validp"] = np.full((128, 1), 1.0 if s > 0 else 0.0, np.float32)
    toks = t0 - 513 + np.arange(2561)
    half = np.arange(0, DH, 2, dtype=np.float64)
    inv_freq = 1.0 / 10000.0 ** (half / DH)
    base = (half + 0.4 * DH) / (1.4 * DH)
    tt = toks.astype(np.float64)
    freqs = tt[:, None] * inv_freq[None, :]
    power = (tt - N // 2) / float(WIN // 2)
    scale = base[None, :] ** power[:, None]
    cos, sin = np.cos(freqs), np.sin(freqs)
    neg = toks < 0
    csq = cos * scale * QKS; ssq = sin * scale * QKS
    csk = cos / scale; ssk = sin / scale
    for a in (csq, ssq, csk, ssk):
        a[neg] = 0.0

    def tile(a):
        return np.tile(a.T.astype(np.float32), (4, 1))
    out["csq"], out["ssq"] = tile(csq), tile(ssq)
    out["csk"], out["ssk"] = tile(csk), tile(ssk)
    pp = np.arange(128)[:, None]
    mm = np.arange(640)[None, :]
    band = (mm >= pp) & (mm <= pp + 512)
    slots = []
    for qt in range(4):
        if s == 0:
            keyabs = t0 + 128 * qt - 512 + mm
            msk = band & (keyabs >= 0)
        else:
            msk = band
        slots.append(np.where(msk, 0.0, NEG))
    slots.append(np.where(band, 0.0, NEG))
    mb = np.concatenate(slots, 0)
    out["maskb"] = np.tile(mb, (1, 2)).astype(BF)
    sp = np.full((128, 640), NEG, np.float32)
    sp[127, 0:513] = 0.0
    out["maskspec"] = np.tile(sp, (1, 2)).astype(BF)
    return out


def host_inputs(inputs, core):
    b, s = core // 4, core % 4
    t0 = s * TPC
    g = lambda k: np.asarray(inputs[k])
    m = {}
    m["xT"] = np.ascontiguousarray(g("L0_attn_x")[b, t0:t0 + TPC, :].T)
    for w in ("Wq", "Wk", "Wv", "Wo"):
        m[w] = np.ascontiguousarray(g("L0_attn_" + w))
    for bias, nm in (("bq", "L0_attn_bq"), ("bk", "L0_attn_bk"),
                     ("bv", "L0_attn_bv"), ("bo", "L0_attn_bo")):
        m[bias] = np.ascontiguousarray(g(nm).reshape(FCH, 128).T)
    m["dparam"] = np.broadcast_to(
        g("L0_attn_decay_param").reshape(1, 1), (128, 1)).copy()
    perm = []
    for grp in range(2):
        for hi in range(2):
            for h in range(4 * grp, 4 * grp + 4):
                for d in range(32):
                    perm.append(64 * h + 32 * hi + d)
    perm = np.array(perm)
    wq = g("L0_ffn_wqkv")
    for l in range(2):
        w = wq[l]
        m[f"wqkv{l}"] = np.ascontiguousarray(
            np.concatenate([w[:, perm], w[:, 512 + perm], w[:, 1024:]], 1))
        m[f"wo{l}"] = np.ascontiguousarray(g("L0_ffn_wo")[l])
    m["lng"] = np.ascontiguousarray(
        g("L0_ffn_ln_g").reshape(2, FCH, 128).transpose(2, 1, 0).reshape(128, FCH * 2, order="F")) \
        if False else np.ascontiguousarray(
        np.concatenate([g("L0_ffn_ln_g")[l].reshape(FCH, 128).T for l in range(2)], 1))
    m["lnb"] = np.ascontiguousarray(
        np.concatenate([g("L0_ffn_ln_b")[l].reshape(FCH, 128).T for l in range(2)], 1))
    qs = g("L0_ffn_q_scale"); ks = g("L0_ffn_k_scale")
    m["qsl"] = np.ascontiguousarray(np.stack([np.tile(qs[l][0:32], 4) for l in range(2)], 1))
    m["qsh"] = np.ascontiguousarray(np.stack([np.tile(qs[l][32:64], 4) for l in range(2)], 1))
    m["ksl"] = np.ascontiguousarray(np.stack([np.tile(ks[l][0:32], 4) for l in range(2)], 1))
    m["ksh"] = np.ascontiguousarray(np.stack([np.tile(ks[l][32:64], 4) for l in range(2)], 1))
    w1 = g("L0_ffn_w1"); b1 = g("L0_ffn_b1"); w2 = g("L0_ffn_w2")
    HIDP = 2816
    b1tiles = []
    for l in range(2):
        w1p = np.zeros((D, 2 * HIDP), np.float32)
        w2p = np.zeros((HIDP, D), np.float32)
        bt = np.zeros((128, 44), np.float32)
        for i in range(NBLK):
            w_ = W1BLK[i]
            a0 = W1OFF[i]
            w1p[:, 256 * i: 256 * i + w_] = w1[l][:, a0:a0 + w_]
            w1p[:, 256 * i + 128: 256 * i + 128 + w_] = w1[l][:, HID + a0:HID + a0 + w_]
            w2p[128 * i: 128 * i + w_, :] = w2[l][a0:a0 + w_, :]
            bt[0:w_, 2 * i] = b1[l][a0:a0 + w_]
            bt[0:w_, 2 * i + 1] = b1[l][HID + a0:HID + a0 + w_]
        m[f"w1_{l}"] = w1p
        m[f"w2_{l}"] = w2p
        b1tiles.append(bt)
    m["b1t"] = np.ascontiguousarray(np.concatenate(b1tiles, 1))
    m["b2t"] = np.ascontiguousarray(
        np.concatenate([g("L0_ffn_b2")[l].reshape(FCH, 128).T for l in range(2)], 1))
    m.update(host_shared())
    m.update(host_percore(core))
    return m



HIDP = 2816


def build_nc(upto="all", dbg=()):
    nc = bass.Bass("TRN2", target_bir_lowering=False, debug=False)
    es = ExitStack()
    P = Prog(nc, es)
    dbg = set(dbg)
    dram = {}

    def din(name, shape, dtype=F32):
        dram[name] = nc.dram_tensor(name, shape, dtype, kind="ExternalInput").ap()
        return dram[name]

    def dout(name, shape, dtype=F32):
        dram[name] = nc.dram_tensor(name, shape, dtype, kind="ExternalOutput").ap()
        return dram[name]

    def dint(name, shape, dtype=F32):
        dram[name] = nc.dram_tensor(name, shape, dtype).ap()
        return dram[name]

    din("xT", [D, TPC])
    for w in ("Wq", "Wk", "Wv", "Wo"):
        din(w, [D, D])
    for bias in ("bq", "bk", "bv", "bo"):
        din(bias, [128, FCH])
    din("dparam", [128, 1])
    din("qexp", [128, CH]); din("kexp", [128, CH])
    din("eclamp", [128, NCH * CH]); din("d01", [128, NCH * CH])
    din("chexp", [128, NCH]); din("coefexp", [128, 4]); din("coefval", [128, 4])
    din("hsel", [128, 4]); din("validp", [128, 1])
    din("identb", [128, 128], BF16); din("onesb", [128, 1], BF16)
    din("onesf", [1, 128]); din("hm2b", [128, 128], BF16)
    din("wqkv0", [D, 3 * INNER]); din("wqkv1", [D, 3 * INNER])
    din("wo0", [INNER, D]); din("wo1", [INNER, D])
    din("lng", [128, FCH * 2]); din("lnb", [128, FCH * 2])
    din("qsl", [128, 2]); din("qsh", [128, 2]); din("ksl", [128, 2]); din("ksh", [128, 2])
    din("w1_0", [D, 2 * HIDP]); din("w1_1", [D, 2 * HIDP])
    din("b1t", [128, 44 * 2])
    din("w2_0", [HIDP, D]); din("w2_1", [HIDP, D])
    din("b2t", [128, FCH * 2])
    din("csq", [128, 2561]); din("ssq", [128, 2561])
    din("csk", [128, 2561]); din("ssk", [128, 2561])
    din("maskb", [5 * 128, 1280], BF16)
    din("maskspec", [128, 1280], BF16)
    dout("y", [D, TPC])

    dint("a_in", [D, D], BF16); dint("a_out", [4 * D, D], BF16)
    dint("h2_in", [D, HALO], BF16); dint("h2_out", [4 * D, HALO], BF16)
    dint("h3_in", [D, HALO], BF16); dint("h3_out", [4 * D, HALO], BF16)
    dint("sp_x1", [D, TPC])
    dint("sp_outT", [D, TPC], BF16)
    dint("sp_qt", [D, TPC], BF16)
    dint("sp_xin0", [D, TPC]); dint("sp_xin1", [D, TPC]); dint("sp_xs2", [D, TPC])
    dint("sp_xsm", [D, 1 + TPC])
    sp_xin = [dram["sp_xin0"], dram["sp_xin1"]]
    sp_xout = [dram["sp_xin1"], dram["sp_xs2"]]

    def drr(name, f):
        return dram[name][f * 128:(f + 1) * 128, :]

    sbl = lambda nm, shp, dt=F32: es.enter_context(nc.sbuf_tensor(nm, shp, dt))
    ident = sbl("ident", [128, 128], BF16)
    onesb = sbl("ones_b", [128, 1], BF16)
    onesf = sbl("ones_f", [1, 128], F32)
    hm2 = sbl("hm2", [128, 128], BF16)
    Lt = sbl("Lt", [128, 1], F32)
    dCt = sbl("dCt", [128, 1], F32)
    stDer = ExitStack()
    qvec = stDer.enter_context(nc.sbuf_tensor("qvec", [128, CH], F32, side="right"))
    kvec = stDer.enter_context(nc.sbuf_tensor("kvec", [128, CH], F32, side="right"))
    dmask = stDer.enter_context(nc.sbuf_tensor("dmask", [128, NCH * CH], BF16, side="right"))
    chs = sbl("chs", [128, NCH], F32)
    coefs = sbl("coefs", [128, 4], F32)
    hselt = sbl("hselt", [128, 4], F32)
    validt = sbl("validt", [128, 1], F32)
    biast = {w: sbl("bias_" + w, [128, FCH], F32) for w in ("bq", "bk", "bv", "bo")}
    lngt = sbl("lngt", [128, FCH * 2], F32)
    lnbt = sbl("lnbt", [128, FCH * 2], F32)
    qst = {k: sbl("t" + k, [128, 2], F32) for k in ("qsl", "qsh", "ksl", "ksh")}
    b2tt = sbl("b2tt", [128, FCH * 2], F32)
    b1tt = sbl("b1tt", [128, 88], F32)
    stat = sbl("stat", [1, 1536], F32)
    lnrep = [sbl(f"lnrep{i}", [128, 1024], F32) for i in range(2)]
    onec = sbl("onec", [128, 1], F32)
    epsc = sbl("epsc", [128, 1], F32)
    st2 = sbl("st2", [128, 8], F32)
    sctch = sbl("sctch", [128, 8], F32)
    xh1 = sbl("xh1", [128, FCH], F32)

    ARENA = es.enter_context(nc.psum_tensor("ARENA", [128, 4096], F32))
    G = [ARENA[:, 512 * i:512 * (i + 1)] for i in range(4)]
    BIG = ARENA[:, 2048:3072]
    H = [ARENA[:, 3072 + 512 * i:3072 + 512 * (i + 1)] for i in range(2)]
    BA = BIG[:, 0:512]
    BB = BIG[:, 512:1024]

    for nm, ap in (("identb", ident), ("onesb", onesb), ("onesf", onesf),
                   ("hm2b", hm2), ("hsel", hselt), ("validp", validt),
                   ("lng", lngt), ("lnb", lnbt), ("b2t", b2tt), ("b1t", b1tt)):
        P.dma("sp", ap[:], dram[nm], writes=(nm,))
    for w in ("bq", "bk", "bv", "bo"):
        P.dma("sp", biast[w][:], dram[w], writes=(w,))
    for k in qst:
        P.dma("sp", qst[k][:], dram[k], writes=(k,))

    P.memset("dve", onec[:], 1.0, writes=("onec",))
    P.memset("dve", epsc[:], EPS, writes=("epsc",))
    P.dma("sp", sctch[:, 0:1], dram["dparam"], writes=("sc0",))
    P.act(sctch[:, 1:2], sctch[:, 0:1], AF.Abs, reads=("sc0",), writes=("sc1",))
    P.act(sctch[:, 2:3], sctch[:, 1:2], AF.Exp, scale=-1.0,
          reads=("sc1",), writes=("sc2",))
    P.act(sctch[:, 3:4], sctch[:, 2:3], AF.Ln, bias=onec[:, 0:1],
          reads=("sc2", "onec"), writes=("sc3",))
    P.ts("dve", sctch[:, 4:5], sctch[:, 0:1], 0.0, ALU.min,
         reads=("sc0",), writes=("sc4",))
    P.tt("dve", Lt[:], sctch[:, 4:5], sctch[:, 3:4], ALU.subtract,
         reads=("sc3", "sc4"), writes=("Lt",))

    def dpow(dst, key, src, w):
        P.dma("sp", dst, src, writes=(key,))
        P.ts("dve", dst, dst, Lt[:, 0:1], ALU.mult, reads=(key, "Lt"), writes=(key,))
        P.act(dst, dst, AF.Exp, reads=(key,), writes=(key,))

    dpow(qvec[:], "qvec", dram["qexp"], CH)
    dpow(kvec[:], "kvec", dram["kexp"], CH)
    dpow(chs[:], "chs", dram["chexp"], NCH)
    P.ts("dve", dCt[:], Lt[:], float(CH), ALU.mult, reads=("Lt",), writes=("dC",))
    P.act(dCt[:], dCt[:], AF.Exp, reads=("dC",), writes=("dC",))
    stSet = ExitStack()
    d01t = stSet.enter_context(nc.sbuf_tensor("d01t", [128, NCH * CH], F32))
    ecl = stSet.enter_context(nc.sbuf_tensor("eclt", [128, NCH * CH], F32))
    dpow(ecl[:], "eclt", dram["eclamp"], NCH * CH)
    P.dma("sp", d01t[:], dram["d01"], writes=("d01t",))
    P.tt("dve", dmask[:], ecl[:], d01t[:], ALU.mult,
         reads=("eclt", "d01t"), writes=("dmask",))
    cfet = stSet.enter_context(nc.sbuf_tensor("cfet", [128, 4], F32))
    dpow(cfet[:], "cfe", dram["coefexp"], 4)
    cvt = stSet.enter_context(nc.sbuf_tensor("cvt", [128, 4], F32))
    P.dma("sp", cvt[:], dram["coefval"], writes=("cvt",))
    P.tt("dve", coefs[:], cfet[:], cvt[:], ALU.mult,
         reads=("cfe", "cvt"), writes=("coefs",))
    stSet.close()

    _lnrot = [0]

    def ln_stats_epi(bkA=None, bkB=None, kA="BIGa", kB="BIGb", n_feat=1024.0):
        bkA = BA if bkA is None else bkA
        bkB = BB if bkB is None else bkB
        j = _lnrot[0] % 2
        _lnrot[0] += 1
        ra = lnrep[j][:, 0:512]; rb = lnrep[j][:, 512:1024]
        rak, rbk = f"lnra{j}", f"lnrb{j}"
        me = stat[0:1, 0:512]; va = stat[0:1, 512:1024]
        rs = stat[0:1, 1024:1536]; mr = va
        P.act(me, bkA[0:1, 0:512], AF.Copy, scale=1.0 / n_feat,
              reads=(kA,), writes=("st_me",))
        P.act(va, me, AF.Square, reads=("st_me",), writes=("st_va",))
        P.stt("dve", va, bkB[0:1, 0:512], 1.0 / n_feat, va, ALU.mult,
              ALU.subtract, reads=(kB, "st_va"), writes=("st_va",))
        P.act(rs, va, AF.Ln, bias=epsc[0:1, 0:1], reads=("st_va", "epsc"),
              writes=("st_rs",))
        P.act(rs, rs, AF.Exp, scale=-0.5, reads=("st_rs",), writes=("st_rs",))
        P.tt("dve", mr, me, rs, ALU.mult, reads=("st_me", "st_rs"), writes=("st_va",))
        P.mmg([(bkA[:, 0:512], onesf[0:1, :], rs, None)], reads=("st_rs",), writes=(kA,))
        P.mmg([(bkB[:, 0:512], onesf[0:1, :], mr, None)], reads=("st_va",), writes=(kB,))
        P.act(ra, bkA[:, 0:512], AF.Copy, reads=(kA,), writes=(rak,))
        P.act(rb, bkB[:, 0:512], AF.Copy, reads=(kB,), writes=(rbk,))
        return ra, rb, rak, rbk

    stA = ExitStack()
    sba = lambda nm, shp, dt=F32: stA.enter_context(nc.sbuf_tensor(nm, shp, dt))
    Mst = sba("Mst", [128, FCH * D], F32)
    Mbf = sba("Mbf", [128, FCH * D], BF16)
    Wqs = sba("Wqs", [128, FCH * D], BF16)
    Wks = sba("Wks", [128, FCH * D], BF16)
    Wvs = sba("Wvs", [128, FCH * D], BF16)
    xt = [sba(f"xt{i}", [128, 512], F32) for i in range(1)]
    xcb = [sba(f"xcb{i}", [128, 512], BF16) for i in range(2)]
    sqc = [sba(f"sqc{i}", [128, 512], BF16) for i in range(2)]
    xab = sba("xab", [128, FCH * CH], BF16)
    qb = sba("q_b", [128, FCH * CH], BF16)
    kb = sba("k_b", [128, FCH * CH], BF16)
    vb = sba("v_b", [128, FCH * CH], BF16)
    ktl = sba("ktl", [128, FCH * CH], BF16)
    qtl = sba("qtl", [128, FCH * CH], BF16)
    VT = sba("VTt", [128, NCH * D], BF16)
    KT = sba("KTt", [128, NCH * D], BF16)
    Pb = sba("Pb", [128, 4 * CH], BF16)
    PT = sba("PTt", [128, 4 * CH], BF16)
    zb = sba("zb", [128, 512], F32)
    zb2 = sba("zb2", [128, 512], F32)
    oste = [sba(f"oste{i}", [128, 512], F32) for i in range(2)]

    P.dma("pool", Wqs[:], dram["Wq"].rearrange("(c p) o -> p c o", p=128),
          writes=("Wqs",))
    P.dma("pool", Wks[:], dram["Wk"].rearrange("(c p) o -> p c o", p=128),
          writes=("Wks",))
    P.dma("pool", Wvs[:], dram["Wv"].rearrange("(c p) o -> p c o", p=128),
          writes=("Wvs",))

    def wsl(wt, kch, och):
        return wt[:, kch * D + och * 128: kch * D + och * 128 + 128]

    for c in range(NCH):
        col0 = c * CH
        xbufs = [xt[0], oste[0], oste[1]]
        xkeys_ = ["xt0", "oste0", "oste1"]
        for f in range(FCH):
            bkey = xkeys_[f % 3]
            ck, sk = f"xcb{f % 2}", f"sqc{f % 2}"
            P.dma("sp", xbufs[f % 3][:], drr("xT", f)[:, col0:col0 + CH],
                  writes=(bkey,))
            P.act(xcb[f % 2][:], xbufs[f % 3][:], AF.Copy, reads=(bkey,),
                  writes=(ck,))
            P.tt("dve", sqc[f % 2][:], xbufs[f % 3][:], xbufs[f % 3][:],
                 ALU.mult, reads=(bkey,), writes=(sk,))
            P.mm1(H[0][0:1, 0:512], onesb[:, 0:1], xcb[f % 2][:],
                  start=(f == 0), stop=(f == FCH - 1),
                  reads=(ck, "onesb"), writes=("H0",))
            P.mm1(H[1][0:1, 0:512], onesb[:, 0:1], sqc[f % 2][:],
                  start=(f == 0), stop=(f == FCH - 1),
                  reads=(sk, "onesb"), writes=("H1",))
        ra, rb, rak, rbk = ln_stats_epi(H[0], H[1], "H0", "H1")
        for f in range(FCH):
            bkey = xkeys_[f % 3]
            P.dma("act", xbufs[f % 3][:], drr("xT", f)[:, col0:col0 + CH],
                  writes=(bkey,))
            P.tt("pool", zb[:], xbufs[f % 3][:], ra, ALU.mult,
                 reads=(bkey, rak), writes=("zb",))
            P.tt("pool", xab[:, f * CH:(f + 1) * CH], zb[:], rb, ALU.subtract,
                 reads=("zb", rbk), writes=(f"xab.{f}",))
        xareads = tuple(f"xab.{f}" for f in range(FCH))

        def gkey(i):
            return f"G{i}"

        for och in range(FCH):
            g = G[och % 2]
            P.mmg([(g[:, 0:CH], wsl(Wqs, kch, och),
                    xab[:, kch * CH:(kch + 1) * CH], None) for kch in range(FCH)],
                  reads=("Wqs",) + xareads, writes=(gkey(och % 2),))
            P.ts("dve", qb[:, och * CH:(och + 1) * CH], g[:, 0:CH],
                 biast["bq"][:, och:och + 1], ALU.add,
                 reads=(gkey(och % 2), "bq"), writes=(f"qb.{och}",))
        for och in range(FCH):
            g = G[och % 2]
            P.mmg([(g[:, 0:CH], wsl(Wks, kch, och),
                    xab[:, kch * CH:(kch + 1) * CH], None) for kch in range(FCH)],
                  reads=("Wks",) + xareads, writes=(gkey(och % 2),))
            P.ts("dve", zb2[:], g[:, 0:CH], biast["bk"][:, och:och + 1], ALU.add,
                 reads=(gkey(och % 2), "bk"), writes=("zb2",))
            P.ts("dve", zb[:], zb2[:], 0.0, ALU.min, reads=("zb2",), writes=("zb",))
            P.act(zb[:], zb[:], AF.Exp, reads=("zb",), writes=("zb",))
            P.stt("dve", kb[:, och * CH:(och + 1) * CH], zb2[:], 0.0, zb[:],
                  ALU.max, ALU.add, reads=("zb2", "zb"), writes=(f"kb.{och}",))
        for och in range(FCH):
            g = G[och % 2]
            P.mmg([(g[:, 0:CH], wsl(Wvs, kch, och),
                    xab[:, kch * CH:(kch + 1) * CH], None) for kch in range(FCH)],
                  reads=("Wvs",) + xareads, writes=(gkey(och % 2),))
            P.ts("dve", vb[:, och * CH:(och + 1) * CH], g[:, 0:CH],
                 biast["bv"][:, och:och + 1], ALU.add,
                 reads=(gkey(och % 2), "bv"), writes=(f"vb.{och}",))

        for f in range(FCH):
            sl = slice(f * CH, (f + 1) * CH)
            P.tt("dve", qtl[:, sl], qb[:, sl], qvec[:], ALU.mult,
                 reads=(f"qb.{f}", "qvec"), writes=(f"qtl.{f}",))
            P.tt("dve", ktl[:, sl], kb[:, sl], kvec[:], ALU.mult,
                 reads=(f"kb.{f}", "kvec"), writes=(f"ktl.{f}",))
        P.dma("act", dram["sp_qt"].rearrange("(ff p) t -> p ff t", p=128)[:, :, col0:col0 + CH],
              qtl.ap().rearrange("p (ff t) -> p ff t", t=CH),
              reads=tuple(f"qtl.{f}" for f in range(FCH)), writes=("d_qt",))

        for jt in range(4):
            for half in range(2):
                g = G[2 + half]
                gk = gkey(2 + half)
                parts = []
                rd = []
                for dq in range(4):
                    dch = half * 4 + dq
                    parts.append((g[:, dq * 128:(dq + 1) * 128],
                                  vb[:, dch * CH + jt * 128: dch * CH + jt * 128 + 128],
                                  ident[:], None))
                    rd.append(f"vb.{dch}")
                P.mmg(parts, reads=tuple(rd) + ("identb",), writes=(gk,))
                P.cp("dve", VT[:, jt * D + half * 512: jt * D + half * 512 + 512],
                     g[:, 0:512], reads=(gk,), writes=(f"VT.{jt}.{half}",))
        for jt in range(4):
            for half in range(2):
                g = G[2 + half]
                gk = gkey(2 + half)
                parts = []
                rd = []
                for dq in range(4):
                    dch = half * 4 + dq
                    parts.append((g[:, dq * 128:(dq + 1) * 128],
                                  ktl[:, dch * CH + jt * 128: dch * CH + jt * 128 + 128],
                                  ident[:], None))
                    rd.append(f"ktl.{dch}")
                P.mmg(parts, reads=tuple(rd) + ("identb",), writes=(gk,))
                P.cp("dve", KT[:, jt * D + half * 512: jt * D + half * 512 + 512],
                     g[:, 0:512], reads=(gk,), writes=(f"KT.{jt}.{half}",))

        def sec_out(c=c, col0=col0):
            for qt in range(4):
                g = G[qt % 2]
                P.mmg([(g[:, 0:CH], qb[:, kch * CH + qt * 128: kch * CH + qt * 128 + 128],
                        kb[:, kch * CH:(kch + 1) * CH], None) for kch in range(FCH)],
                      reads=tuple(f"qb.{f}" for f in range(FCH)) +
                            tuple(f"kb.{f}" for f in range(FCH)),
                      writes=(gkey(qt % 2),))
                P.tt("dve", Pb[:, qt * CH:(qt + 1) * CH], g[:, 0:CH],
                     dmask[:, qt * CH:(qt + 1) * CH], ALU.mult,
                     reads=(gkey(qt % 2), "dmask"), writes=(f"Pb.{qt}",))
            for mt in range(4):
                g = G[2 + mt % 2]
                gk = gkey(2 + mt % 2)
                parts = []
                for qt in range(4):
                    parts.append((g[:, qt * 128:(qt + 1) * 128],
                                  Pb[:, qt * CH + mt * 128: qt * CH + mt * 128 + 128],
                                  ident[:], None))
                P.mmg(parts, reads=tuple(f"Pb.{qt}" for qt in range(4)) + ("identb",),
                      writes=(gk,))
                P.cp("dve", PT[:, mt * CH:(mt + 1) * CH], g[:, 0:512],
                     reads=(gk,), writes=(f"PT.{mt}",))

            for ech in range(FCH):
                gps = G[ech % 2]
                gk = gkey(ech % 2)
                parts = []
                rd = []
                if c > 0:
                    for dch in range(FCH):
                        parts.append((gps[:, 0:CH],
                                      Mbf[:, dch * D + ech * 128: dch * D + ech * 128 + 128],
                                      qtl[:, dch * CH:(dch + 1) * CH], None))
                        rd += ["Mbf", f"qtl.{dch}"]
                for mt in range(4):
                    parts.append((gps[:, 0:CH],
                                  VT[:, mt * D + ech * 128: mt * D + ech * 128 + 128],
                                  PT[:, mt * CH:(mt + 1) * CH], None))
                    rd += [f"VT.{mt}.0", f"VT.{mt}.1", f"PT.{mt}"]
                P.mmg(parts, reads=tuple(set(rd)), writes=(gk,))
                ok_ = f"xcb{ech % 2}"
                P.act(xcb[ech % 2][:], gps[:, 0:CH], AF.Copy, reads=(gk,),
                      writes=(ok_,))
                P.dma("sp", drr("sp_outT", ech)[:, col0:col0 + CH], xcb[ech % 2][:],
                      reads=(ok_,), writes=("d_outT",))

        def sec_state(c=c):
            for dch in range(FCH):
                for half in range(2):
                    bkey = "BIGa" if half == 0 else "BIGb"
                    bap = BA if half == 0 else BB
                    parts = []
                    rd = []
                    for jt in range(4):
                        parts.append((bap,
                                      KT[:, jt * D + dch * 128: jt * D + dch * 128 + 128],
                                      VT[:, jt * D + half * 512: jt * D + half * 512 + 512],
                                      None))
                        rd += [f"KT.{jt}.{dch // 4}", f"VT.{jt}.{half}"]
                    P.mmg(parts, reads=tuple(set(rd)), writes=(bkey,))
                    msl = slice(dch * D + half * 512, dch * D + half * 512 + 512)
                    if c == 0:
                        P.cp("dve", Mst[:, msl], bap, reads=(bkey,),
                             writes=(f"Mst.{dch}.{half}",))
                    else:
                        P.stt("dve", Mst[:, msl], Mst[:, msl], dCt[:, 0:1], bap,
                              ALU.mult, ALU.add,
                              reads=(f"Mst.{dch}.{half}", "dC", bkey),
                              writes=(f"Mst.{dch}.{half}",))

        if c < NCH - 1:
            sec_out()
            sec_state()
            for dch in range(FCH):
                P.act(Mbf[:, dch * D:(dch + 1) * D], Mst[:, dch * D:(dch + 1) * D],
                      AF.Copy, reads=(f"Mst.{dch}.0", f"Mst.{dch}.1"),
                      writes=("Mbf",))
        else:
            sec_state()
            for dch in range(FCH):
                P.act(Wvs[:, dch * D:(dch + 1) * D],
                      Mst[:, dch * D:(dch + 1) * D],
                      AF.Copy, reads=(f"Mst.{dch}.0", f"Mst.{dch}.1"),
                      writes=("Wvs", "Mb2"))
            P.dma("sp", dram["a_in"].rearrange("(dd p) e -> p dd e", p=128),
                  Wvs.ap().rearrange("p (dd e) -> p dd e", e=D),
                  reads=("Mb2",), writes=("d_a_in",))
            P.coll("AllGather", ALU.bypass, [[0, 1, 2, 3], [4, 5, 6, 7]],
                   [dram["a_in"]], [dram["a_out"]], sem="cc1",
                   reads=("d_a_in",), writes=("d_a_out",))
            sec_out()


    if "A" in dbg:
        da = dout("dbg_A", [D, D])
        P.dma("sp", da.rearrange("(dd p) e -> p dd e", p=128),
              Mst.ap().rearrange("p (dd e) -> p dd e", e=D),
              reads=tuple(f"Mst.{d_}.{h_}" for d_ in range(FCH) for h_ in range(2)),
              writes=("d_dbgA",))
    if "outT" in dbg:
        do_ = dout("dbg_outT", [D, TPC], BF16)
        P.dma("sp", do_.rearrange("(ff p) t -> p ff t", p=128),
              dram["sp_outT"].rearrange("(ff p) t -> p ff t", p=128),
              reads=("d_outT",), writes=("d_dbgoT",))

    if upto == "attn1":
        P.barrier()
        P.emit()
        stA.close()
        es.close()
        return nc, dram

    dint("d_minit", [D, D], BF16)
    dint("d_xsb", [D, TW], BF16)
    P.barrier()
    stA.close()
    stDer.close()

    warena = ExitStack()
    wqkvb = warena.enter_context(nc.sbuf_tensor("wqkvb", [128, FCH * 3 * INNER], BF16))
    wob = warena.enter_context(nc.sbuf_tensor("wob", [128, 4 * D], BF16))

    stMI = ExitStack()
    minitb = stMI.enter_context(nc.sbuf_tensor("minitb", [128, FCH * D], BF16))
    st2p = ExitStack()
    slotb = st2p.enter_context(nc.sbuf_tensor("slotb", [128, 4 * D], F32))
    macc = st2p.enter_context(nc.sbuf_tensor("macc", [128, D], F32))
    for dch in range(FCH):
        for sl_ in range(4):
            P.dma("pool", slotb[:, sl_ * D:(sl_ + 1) * D],
                  dram["a_out"][sl_ * D + dch * 128: sl_ * D + dch * 128 + 128, :],
                  reads=("d_a_out",), writes=(f"slot.{sl_}",))
        P.ts("dve", macc[:], slotb[:, 0:D], coefs[:, 0:1], ALU.mult,
             reads=("slot.0", "coefs"), writes=("macc",))
        for sl_ in range(1, 3):
            P.stt("dve", macc[:], slotb[:, sl_ * D:(sl_ + 1) * D],
                  coefs[:, sl_:sl_ + 1], macc[:], ALU.mult, ALU.add,
                  reads=(f"slot.{sl_}", "coefs", "macc"), writes=("macc",))
        P.stt("dve", minitb[:, dch * D:(dch + 1) * D], slotb[:, 3 * D:4 * D],
              coefs[:, 3:4], macc[:],
              ALU.mult, ALU.add, reads=("slot.3", "coefs", "macc"),
              writes=("minitb",))
    P.dma("pool", wqkvb[:], dram["wqkv0"].rearrange("(c p) o -> p c o", p=128),
          writes=("wqkvb",))
    P.dma("pool", wob[:], dram["wo0"].rearrange("(c p) o -> p c o", p=128),
          writes=("wob",))

    if "minit" in dbg:
        dmn = dout("dbg_minit", [D, D], BF16)
        P.dma("sp", dmn.rearrange("(dd p) e -> p dd e", p=128),
              minitb.ap().rearrange("p (dd e) -> p dd e", e=D),
              reads=("minitb",), writes=("dbmn",))
    st3 = ExitStack()
    s3 = lambda nm, shp, dt=F32: st3.enter_context(nc.sbuf_tensor(nm, shp, dt))
    qhb = [s3(f"qhb{i}", [128, FCH * 512], BF16) for i in range(2)]
    oob2 = s3("oob2", [128, FCH * 512], BF16)
    oob = s3("oob", [128, FCH * 512], BF16)
    Wos = s3("Wos", [128, FCH * D], BF16)
    aof = [s3(f"aof{i}", [128, FCH * 512], BF16) for i in range(2)]
    sqb3 = [s3(f"sqb3{i}", [128, FCH * 512], BF16) for i in range(2)]
    x1f = s3("x1f", [128, FCH * 512], F32)
    xff = s3("xff", [128, FCH * 512], F32)
    xfb = s3("xfb", [128, 512], BF16)
    x1b = s3("x1b", [128, 512], BF16)
    sq1 = s3("sq1", [128, 512], BF16)
    xres = [s3(f"xres{i}", [128, 512], F32) for i in range(2)]
    zb3 = s3("zb3", [128, 512], F32)
    zb3b = s3("zb3b", [128, 512], F32)
    zcol = s3("zcol", [128, 1], BF16)
    P.memset("dve", zcol[:], 0.0, writes=("zcol",))
    P.dma("pool", Wos[:], dram["Wo"].rearrange("(c p) o -> p c o", p=128),
          writes=("Wos",))

    def stageX(tt):
        j = tt % 2
        col0 = tt * 512
        q_, qk_ = qhb[j], f"qhb{j}"
        P.dma("sp", q_.ap().rearrange("p (ff t) -> p ff t", t=512),
              dram["sp_qt"].rearrange("(ff p) t -> p ff t", p=128)[:, :, col0:col0 + 512],
              reads=("d_qt",), writes=(qk_,))
        P.ts("dve", q_[:], q_[:], chs[:, tt:tt + 1], ALU.mult,
             reads=(qk_, "chs"), writes=(qk_,))
        for ech in range(FCH):
            gk = f"G{ech % 2}"
            g = G[ech % 2]
            P.mmg([(g[:, 0:512],
                    minitb[:, dch * D + ech * 128: dch * D + ech * 128 + 128],
                    q_[:, dch * 512:(dch + 1) * 512], None) for dch in range(FCH)],
                  reads=("minitb", qk_), writes=(gk,))
            sl = slice(ech * 512, (ech + 1) * 512)
            P.dma("act", oob2[:, sl], drr("sp_outT", ech)[:, col0:col0 + 512],
                  reads=("d_outT",), writes=(f"oo2.{ech}",))
            P.tt("dve", oob[:, sl], oob2[:, sl], g[:, 0:512], ALU.add,
                 reads=(f"oo2.{ech}", gk), writes=(f"oob.{ech}",))
        for och in range(FCH):
            hk = f"H{och % 2}"
            h = H[och % 2]
            P.mmg([(h[:, 0:512], Wos[:, ech * D + och * 128: ech * D + och * 128 + 128],
                    oob[:, ech * 512:(ech + 1) * 512], None) for ech in range(FCH)],
                  reads=("Wos",) + tuple(f"oob.{e_}" for e_ in range(FCH)),
                  writes=(hk,))
            sl = slice(och * 512, (och + 1) * 512)
            P.ts("dve", aof[j][:, sl], h[:, 0:512], biast["bo"][:, och:och + 1],
                 ALU.add, reads=(hk, "bo"), writes=(f"aof{j}.{och}",))
            P.tt("dve", sqb3[j][:, sl], aof[j][:, sl], aof[j][:, sl], ALU.mult,
                 reads=(f"aof{j}.{och}",), writes=(f"sqb3{j}.{och}",))

    def stageY(tt):
        j = tt % 2
        col0 = tt * 512
        for och in range(FCH):
            sl = slice(och * 512, (och + 1) * 512)
            P.mm1(BIG[0:1, 0:512], onesb[:, 0:1], aof[j][:, sl],
                  start=(och == 0), stop=(och == FCH - 1),
                  reads=(f"aof{j}.{och}", "onesb"), writes=("BIGa",))
            P.mm1(BIG[0:1, 512:1024], onesb[:, 0:1], sqb3[j][:, sl],
                  start=(och == 0), stop=(och == FCH - 1),
                  reads=(f"sqb3{j}.{och}", "onesb"), writes=("BIGb",))
        ra, rb, rak, rbk = ln_stats_epi()
        for f in range(FCH):
            sl = slice(f * 512, (f + 1) * 512)
            xk = f"xres{f % 2}"
            P.dma("sp", xres[f % 2][:], drr("xT", f)[:, col0:col0 + 512],
                  writes=(xk,))
            zeng = "pool" if f % 3 == 0 else "dve"
            z, zk = (zb3, "zb3") if zeng == "pool" else (zb3b, "zb3b")
            P.tt(zeng, z[:], aof[j][:, sl], ra, ALU.mult,
                 reads=(f"aof{j}.{f}", rak), writes=(zk,))
            P.tt(zeng, z[:], z[:], rb, ALU.subtract,
                 reads=(zk, rbk), writes=(zk,))
            P.tt(zeng, x1f[:, sl], xres[f % 2][:], z[:], ALU.add,
                 reads=(xk, zk), writes=(f"x1f.{f}",))
            P.dma("pool", drr("sp_x1", f)[:, col0:col0 + 512], x1f[:, sl],
                  reads=(f"x1f.{f}",), writes=("d_x1",))
        for f in range(FCH):
            sl = slice(f * 512, (f + 1) * 512)
            P.act(x1b[:], x1f[:, sl], AF.Copy, reads=(f"x1f.{f}",),
                  writes=("x1b",))
            P.tt("dve", sq1[:], x1f[:, sl], x1f[:, sl], ALU.mult,
                 reads=(f"x1f.{f}",), writes=("sq1",))
            P.mm1(BIG[0:1, 0:512], onesb[:, 0:1], x1b[:],
                  start=(f == 0), stop=(f == FCH - 1),
                  reads=("x1b", "onesb"), writes=("BIGa",))
            P.mm1(BIG[0:1, 512:1024], onesb[:, 0:1], sq1[:],
                  start=(f == 0), stop=(f == FCH - 1),
                  reads=("sq1", "onesb"), writes=("BIGb",))
        ra, rb, rak, rbk = ln_stats_epi()
        for f in range(FCH):
            sl = slice(f * 512, (f + 1) * 512)
            zeng = "pool" if f % 3 == 0 else "dve"
            z, zk = (zb3, "zb3") if zeng == "pool" else (zb3b, "zb3b")
            P.tt(zeng, z[:], x1f[:, sl], ra, ALU.mult,
                 reads=(f"x1f.{f}", rak), writes=(zk,))
            P.tt(zeng, xff[:, sl], z[:], rb, ALU.subtract,
                 reads=(zk, rbk), writes=(f"xff.{f}",))
            P.dma("act", drr("sp_xin0", f)[:, col0:col0 + 512], xff[:, sl],
                  reads=(f"xff.{f}",), writes=("d_xin0",))
            P.act(xfb[:], xff[:, sl], AF.Copy, reads=(f"xff.{f}",),
                  writes=("xfb",))
            P.dma("sp", dram["d_xsb"][f * 128:(f + 1) * 128,
                                      QCOL0 + col0:QCOL0 + col0 + 512],
                  xfb[:], reads=("xfb",), writes=("d_xsb_own",))

    for tt in range(NCH + 1):
        if tt < NCH:
            stageX(tt)
        if tt >= 1:
            stageY(tt - 1)

    if "x1" in dbg:
        dx = dout("dbg_x1", [D, TPC])
        P.dma("sp", dx, dram["sp_x1"], reads=("d_x1",), writes=("d_dbg1",))
        dx2 = dout("dbg_xf", [D, TPC])
        P.dma("sp", dx2, dram["sp_xin0"], reads=("d_xin0",), writes=("d_dbg2",))

    for f in range(FCH):
        P.dma("sp", dram["d_xsb"][f * 128:(f + 1) * 128, 0:1], zcol[:],
              reads=("zcol",), writes=("d_xsb_halo",), slow=True)
    P.dma("pool", dram["h2_in"], dram["sp_xin0"][:, TPC - HALO:TPC],
          reads=("d_xin0",), writes=("d_h2in",))
    P.barrier()
    P.coll("AllGather", ALU.bypass, [[0, 1, 2, 3], [4, 5, 6, 7]],
           [dram["h2_in"]], [dram["h2_out"]], sem="cc2",
           reads=("d_h2in",), writes=("d_h2out",))
    st3.close()
    st2p.close()
    stMI.close()

    stXB = ExitStack()

    def halo_fill(hout_key, hout_ap, stHF, pfx):
        hsl = stHF.enter_context(nc.sbuf_tensor("hsl" + pfx, [128, 4 * HALO], F32))
        halof = stHF.enter_context(nc.sbuf_tensor("hf" + pfx, [128, HALO], F32))
        halob = stHF.enter_context(nc.sbuf_tensor("hb" + pfx, [128, HALO], BF16))
        for f in range(FCH):
            for sl_ in range(4):
                P.dma("pool", hsl[:, sl_ * HALO:(sl_ + 1) * HALO],
                      hout_ap[sl_ * D + f * 128: sl_ * D + f * 128 + 128, :],
                      reads=(hout_key,), writes=(f"hsl.{sl_}",))
            P.ts("dve", halof[:], hsl[:, 0:HALO], hselt[:, 0:1], ALU.mult,
                 reads=("hsl.0", "hsel"), writes=("halof",))
            for sl_ in range(1, 4):
                P.stt("dve", halof[:], hsl[:, sl_ * HALO:(sl_ + 1) * HALO],
                      hselt[:, sl_:sl_ + 1], halof[:], ALU.mult, ALU.add,
                      reads=(f"hsl.{sl_}", "hsel", "halof"), writes=("halof",))
            P.act(halob[:], halof[:], AF.Copy, reads=("halof",), writes=("halob",))
            P.dma("sp", dram["d_xsb"][f * 128:(f + 1) * 128, 1:1 + HALO],
                  halob[:], reads=("halob",), writes=("d_xsb_halo",))
            P.cp("dve", xh1[:, f:f + 1], halof[:, HALO - 1:HALO],
                 reads=("halof",), writes=("xh1",))

    def emit_layer(l, xs_in_d, xs_out_d, write_next_xsb, hkey, hout):
        LQ = 2561
        stH1 = ExitStack()
        h2b = stH1.enter_context(nc.sbuf_tensor("h2b" + str(l), [128, FCH * TPC],
                                                BF16, side="right"))
        stL1 = ExitStack()
        sL1 = lambda nm, shp, dt=BF16: stL1.enter_context(
            nc.sbuf_tensor(nm + str(l), shp, dt, side="right"))
        oTb = sL1("oTb", [128, 4 * TPC])
        xmx = sL1("xmx", [128, FCH], F32)
        stL2 = ExitStack()
        sL2 = lambda nm, shp, dt=BF16: stL2.enter_context(nc.sbuf_tensor(nm + str(l), shp, dt))
        qrb = sL2("qrb", [128, 4 * LQ])
        krb = sL2("krb", [128, 4 * LQ])
        stVR = ExitStack()
        vrb = stVR.enter_context(nc.sbuf_tensor("vrb" + str(l), [128, 4 * LQ], BF16))

        stq = ExitStack()
        sQ = lambda nm, shp, dt=F32: stq.enter_context(nc.sbuf_tensor(nm + str(l), shp, dt))
        xst = [sQ(f"xst{i}", [128, FCH * 513], BF16) for i in range(2)]
        qraw = sQ("qraw", [128, 2 * 512], F32)
        sqcs = sQ("sqcs", [128, 2 * 512], BF16)
        invs = sQ("invs", [128, 512], F32)
        tblb = [sQ(f"tblb{i}", [128, 2 * 512], F32) for i in range(2)]
        rt0 = sQ("rt0", [128, 512], F32)
        rt1 = sQ("rt1", [128, 512], F32)

        TILES = [(1027, 512), (1539, 512), (2051, 512),
                 (515, 512), (3, 512), (2, 1)]
        sides = [("q", 0, "csq", "ssq", "qsl", "qsh"),
                 ("k", 4, "csk", "ssk", "ksl", "ksh")]
        for ti, (c0, w) in enumerate(TILES):
            if ti == 3:
                halo_fill(hkey, hout, stq, f"l{l}")
            if c0 >= 1027:
                xrd = ("d_xsb_own",)
            elif c0 == 515:
                xrd = ("d_xsb_own", "d_xsb_halo")
            else:
                xrd = ("d_xsb_halo",)
            nb = ti % len(xst)
            xk = f"xst{nb}"
            P.dma("sp", xst[nb].ap().rearrange("p (ff t) -> p ff t", t=513)[:, :, 0:w + 1],
                  dram["d_xsb"].rearrange("(ff p) t -> p ff t", p=128)[:, :, c0 - 1:c0 + w],
                  reads=xrd, writes=(xk,))

            def rhs(kch, sh, nb=nb):
                base = kch * 513
                return xst[nb][:, base + 1 - sh: base + 1 - sh + w]

            for sname, obase, csn, ssn, sln, shn in sides:
                for grp in range(2):
                    for hi in range(2):
                        och = obase + 2 * grp + hi
                        g = G[hi]
                        gk = f"G{hi}"
                        P.mmg([(g[:, 0:w],
                                wqkvb[:, kch * 3 * INNER + och * 128: kch * 3 * INNER + och * 128 + 128],
                                rhs(kch, 1 if kch >= 4 else 0), None)
                               for kch in range(FCH)],
                              reads=("wqkvb", xk), writes=(gk,))
                        P.act(qraw[:, hi * 512:hi * 512 + w], g[:, 0:w], AF.Copy,
                              reads=(gk,), writes=(f"qraw.{hi}",))
                        P.tt("dve", sqcs[:, hi * 512:hi * 512 + w],
                             qraw[:, hi * 512:hi * 512 + w],
                             qraw[:, hi * 512:hi * 512 + w], ALU.mult,
                             reads=(f"qraw.{hi}",), writes=(f"sqcs.{hi}",))
                    P.mm1(H[0][:, 0:w], hm2[:, :], sqcs[:, 0:w], start=True,
                          stop=False, reads=("sqcs.0", "hm2b"), writes=("H0",))
                    P.mm1(H[0][:, 0:w], hm2[:, :], sqcs[:, 512:512 + w],
                          start=False, stop=True, reads=("sqcs.1", "hm2b"),
                          writes=("H0",))
                    P.ts("dve", invs[:, 0:w], H[0][:, 0:w], 1e-24, ALU.max,
                         reads=("H0",), writes=("invs",))
                    P.act(invs[:, 0:w], invs[:, 0:w], AF.Ln, reads=("invs",),
                          writes=("invs",))
                    P.act(invs[:, 0:w], invs[:, 0:w], AF.Exp, scale=-0.5,
                          reads=("invs",), writes=("invs",))
                    P.ts("dve", qraw[:, 0:w], qraw[:, 0:w],
                         qst[sln][:, l:l + 1], ALU.mult,
                         reads=("qraw.0", sln), writes=("qraw.0",))
                    P.ts("dve", qraw[:, 512:512 + w], qraw[:, 512:512 + w],
                         qst[shn][:, l:l + 1], ALU.mult,
                         reads=("qraw.1", shn), writes=("qraw.1",))
                    P.tt("dve", qraw[:, 0:w], qraw[:, 0:w], invs[:, 0:w],
                         ALU.mult, reads=("qraw.0", "invs"), writes=("qraw.0",))
                    P.tt("dve", qraw[:, 512:512 + w], qraw[:, 512:512 + w],
                         invs[:, 0:w], ALU.mult, reads=("qraw.1", "invs"),
                         writes=("qraw.1",))
                    tk = f"tblb{(2 * ti + grp) % 2}"
                    tb = tblb[(2 * ti + grp) % 2]
                    P.dma("sp", tb[:, 0:w], dram[csn][:, c0 - 2:c0 - 2 + w],
                          writes=(tk,), slow=(w == 1))
                    P.dma("sp", tb[:, 512:512 + w], dram[ssn][:, c0 - 2:c0 - 2 + w],
                          reads=(tk,), writes=(tk,), slow=(w == 1))
                    dst = qrb if sname == "q" else krb
                    dk = ("qrb" if sname == "q" else "krb")
                    qc1 = (2 * grp) * LQ + (c0 - 2)
                    qc2 = (2 * grp + 1) * LQ + (c0 - 2)
                    P.tt("dve", rt0[:, 0:w], qraw[:, 0:w], tb[:, 0:w], ALU.mult,
                         reads=("qraw.0", tk), writes=("rt0",))
                    P.tt("dve", rt1[:, 0:w], qraw[:, 512:512 + w],
                         tb[:, 512:512 + w], ALU.mult,
                         reads=("qraw.1", tk), writes=("rt1",))
                    P.tt("dve", dst[:, qc1:qc1 + w], rt0[:, 0:w], rt1[:, 0:w],
                         ALU.subtract, reads=("rt0", "rt1"),
                         writes=(f"{dk}.{grp}.lo.{ti}",))
                    P.tt("dve", rt0[:, 0:w], qraw[:, 512:512 + w], tb[:, 0:w],
                         ALU.mult, reads=("qraw.1", tk), writes=("rt0",))
                    P.tt("dve", rt1[:, 0:w], qraw[:, 0:w], tb[:, 512:512 + w],
                         ALU.mult, reads=("qraw.0", tk), writes=("rt1",))
                    P.tt("dve", dst[:, qc2:qc2 + w], rt0[:, 0:w], rt1[:, 0:w],
                         ALU.add, reads=("rt0", "rt1"),
                         writes=(f"{dk}.{grp}.hi.{ti}",))
            for vi in range(4):
                och = 8 + vi
                g = G[2 + vi % 2]
                gk = f"G{2 + vi % 2}"
                P.mmg([(g[:, 0:w],
                        wqkvb[:, kch * 3 * INNER + och * 128: kch * 3 * INNER + och * 128 + 128],
                        rhs(kch, 1 if kch >= 4 else 0), None)
                       for kch in range(FCH)],
                      reads=("wqkvb", xk), writes=(gk,))
                P.act(vrb[:, vi * LQ + (c0 - 2): vi * LQ + (c0 - 2) + w],
                      g[:, 0:w], AF.Copy, reads=(gk,), writes=(f"vrb.{vi}.{ti}",))
        P.barrier()
        stq.close()
        stVT = ExitStack()
        VTb = stVT.enter_context(nc.sbuf_tensor("VTb" + str(l), [128, 20 * 512],
                                                BF16, side="right"))
        SVTb = stVT.enter_context(nc.sbuf_tensor("SVTb" + str(l), [128, 5 * 512],
                                                 BF16, side="right"))
        if l == 0:
            P.dma("pool", wqkvb[:], dram["wqkv1"].rearrange("(c p) o -> p c o", p=128),
                  writes=("wqkvb",))

        vkeys = tuple(f"vrb.{vi}.{ti}" for vi in range(4) for ti in range(6))
        if l == 0 and "vrb" in dbg:
            dv0 = dout("dbg_vrb0", [128, 4 * LQ], BF16)
            P.dma("sp", dv0, vrb[:], reads=vkeys, writes=("dbv0",))
        qkeys = lambda dk: tuple(f"{dk}.{g_}.{h_}.{ti}" for g_ in range(2)
                                 for h_ in ("lo", "hi") for ti in range(6))
        for k in range(20):
            g = G[k % 2]
            gk = f"G{k % 2}"
            P.mmg([(g[:, vi * 128:(vi + 1) * 128],
                    vrb[:, vi * LQ + 1 + 128 * k: vi * LQ + 1 + 128 * k + 128],
                    ident[:], None) for vi in range(4)],
                  reads=vkeys + ("identb",), writes=(gk,))
            P.cp("dve", VTb[:, k * 512:(k + 1) * 512], g[:, 0:512],
                 reads=(gk,), writes=(f"VTb.{k}",))
        for mtx in range(5):
            g = G[2 + mtx % 2]
            gk = f"G{2 + mtx % 2}"
            P.mmg([(g[:, vi * 128:(vi + 1) * 128],
                    vrb[:, vi * LQ + 128 * mtx: vi * LQ + 128 * mtx + 128],
                    ident[:], None) for vi in range(4)],
                  reads=vkeys + ("identb",), writes=(gk,))
            P.cp("dve", SVTb[:, mtx * 512:(mtx + 1) * 512], g[:, 0:512],
                 reads=(gk,), writes=(f"SVTb.{mtx}",))
        P.barrier()
        stVR.close()

        P.barrier()
        stAt = ExitStack()
        sA = lambda nm, shp, dt=BF16: stAt.enter_context(
            nc.sbuf_tensor(nm + str(l), shp, dt, side="right"))
        SsbP = [sA(f"SsbP{i}", [128, 1280], F32) for i in range(2)]
        maskv = sA("maskv", [128, 4 * 1280])
        maskc = sA("maskc", [128, 1280])
        masksp = sA("masksp", [128, 1280])
        P.dma("sp", maskc[:], dram["maskb"][4 * 128:5 * 128, :], writes=("maskc",))
        P.dma("sp", maskv.ap().rearrange("p (s m) -> p s m", m=1280),
              dram["maskb"].rearrange("(s p) m -> p s m", p=128)[:, 0:4, :],
              writes=("maskv",))
        P.dma("sp", masksp[:], dram["maskspec"], writes=("masksp",))
        PsbP = [sA(f"PsbP{i}", [128, 1280]) for i in range(4)]
        PTsbP = [sA(f"PTsbP{i}", [128, 1280]) for i in range(2)]
        dgbP = [sA(f"dgbP{i}", [128, 256]) for i in range(4)]
        sml = sA("sml", [128, 16], F32)
        oxb = sA("oxb", [128, 512])

        tasks = []
        for qt in [-1] + list(range(16)):
            for h in range(8):
                tasks.append((qt, h))
        NT = len(tasks)
        NG = NT // 2
        qk_r = qkeys("qrb") + qkeys("krb")
        scores2 = ARENA.ap().rearrange("p (s c) -> p s c", c=1024)[:, 0:2, 0:640]
        SCKEYS = ("SCA", "BKA", "SCB", "BKB")

        def tinfo(i):
            qt, h = tasks[i]
            sl_ = i % 4
            p = i % 2
            spec = (qt < 0)
            kwc = 0 if spec else 1 + 128 * qt
            qc = 385 if spec else 513 + 128 * qt
            if spec:
                mask, mkk = masksp[:], "masksp"
            elif qt < 4:
                mask, mkk = maskv[:, qt * 1280:(qt + 1) * 1280], "maskv"
            else:
                mask, mkk = maskc[:], "maskc"
            grp = h // 4
            hb = 32 * (h % 4)
            return dict(sl=sl_, p=p, spec=spec, h=h, qt=qt, qtidx=i // 8,
                        hb=hb, mask=mask, mkk=mkk,
                        qlo=qrb[hb:hb + 32, (2 * grp) * LQ + qc: (2 * grp) * LQ + qc + 128],
                        qhi=qrb[hb:hb + 32, (2 * grp + 1) * LQ + qc: (2 * grp + 1) * LQ + qc + 128],
                        klo=krb[hb:hb + 32, (2 * grp) * LQ + kwc: (2 * grp) * LQ + kwc + 640],
                        khi=krb[hb:hb + 32, (2 * grp + 1) * LQ + kwc: (2 * grp + 1) * LQ + kwc + 640])

        def sc_g(p):
            return ARENA[:, 1024 * p:1024 * p + 512], ("SCA", "SCB")[p]

        def sc_t(p):
            return ARENA[:, 1024 * p + 512:1024 * p + 640], ("BKA", "BKB")[p]

        def pt_g(p):
            return ARENA[:, 2048 + 512 * p:2560 + 512 * p], ("PTA", "PTB")[p]

        def pt_t(p):
            return ARENA[:, 1024 * p + 640:1024 * p + 768], ("BKA", "BKB")[p]

        def stage_A(g):
            ts_ = [tinfo(2 * g), tinfo(2 * g + 1)]
            parts = []
            for t in ts_:
                parts.append((sc_g(t["p"])[0], t["qlo"],
                              t["klo"][:, 0:512], True, False, (t["hb"], 0)))
            for t in ts_:
                parts.append((sc_g(t["p"])[0], t["qhi"],
                              t["khi"][:, 0:512], False, True, (t["hb"], 0)))
            for t in ts_:
                parts.append((sc_t(t["p"])[0], t["qlo"],
                              t["klo"][:, 512:640], True, False, (t["hb"], 0)))
            for t in ts_:
                parts.append((sc_t(t["p"])[0], t["qhi"],
                              t["khi"][:, 512:640], False, True, (t["hb"], 0)))
            P.mseq(parts, reads=qk_r, writes=SCKEYS)

        def stage_BC(g):
            j = g % 2
            ts_ = [tinfo(2 * g), tinfo(2 * g + 1)]
            assert ts_[0]["mkk"] == ts_[1]["mkk"]
            sk = f"SsbP{j}"
            nmk = f"nmP{j}"
            nmP = sml[:, 2 * j:2 * j + 2]
            sv = SsbP[j].ap().rearrange("p (s c) -> p s c", c=640)
            P.tt("dve", sv, scores2,
                 ts_[0]["mask"].rearrange("p (s c) -> p s c", c=640),
                 ALU.add, reads=SCKEYS + (ts_[0]["mkk"],), writes=(sk,))
            P.red(nmP, sv, ALU.max, reads=(sk,), writes=(nmk,))
            P.ts("dve", nmP, nmP, -1.0, ALU.mult, reads=(nmk,), writes=(nmk,))
            jj = g % 4
            for p, t in enumerate(ts_):
                sl_ = t["sl"]
                sm1 = sml[:, 4 + sl_:5 + sl_]
                P.act(PsbP[jj][:, 640 * p:640 * p + 640],
                      SsbP[j][:, 640 * p:640 * p + 640], AF.Exp,
                      bias=sml[:, 2 * j + p:2 * j + p + 1], accum_out=sm1,
                      reads=(sk, nmk), writes=(f"PsbP{jj}", f"sm.{sl_}"))
            j2 = (2 * g) % 4
            P.recip(sml[:, 8 + j2:10 + j2], sml[:, 4 + j2:6 + j2],
                    reads=(f"sm.{ts_[0]['sl']}", f"sm.{ts_[1]['sl']}"),
                    writes=(f"rc.{ts_[0]['sl']}", f"rc.{ts_[1]['sl']}"))
            for p, t in enumerate(ts_):
                sl_ = t["sl"]
                rc = sml[:, 8 + sl_:9 + sl_]
                P.ts("dve", dgbP[jj][:, 128 * p:128 * p + 128], ident[:], rc,
                     ALU.mult, reads=(f"rc.{sl_}", "identb"),
                     writes=(f"dgbP{jj}",))

        def stage_D(g):
            ts_ = [tinfo(2 * g), tinfo(2 * g + 1)]
            j = g % 2
            jj = g % 4
            parts = []
            wr_ = []
            for p, t in enumerate(ts_):
                wr_ += [pt_g(p)[1], pt_t(p)[1]]
                for mtp in range(4):
                    parts.append((pt_g(p)[0][:, mtp * 128:(mtp + 1) * 128],
                                  PsbP[jj][:, 640 * p + mtp * 128: 640 * p + mtp * 128 + 128],
                                  dgbP[jj][:, 128 * p:128 * p + 128],
                                  True, True, None))
                parts.append((pt_t(p)[0], PsbP[jj][:, 640 * p + 512:640 * p + 640],
                              dgbP[jj][:, 128 * p:128 * p + 128],
                              True, True, None))
            P.mseq(parts, reads=(f"PsbP{jj}", f"dgbP{jj}"), writes=tuple(wr_))
            ptk = f"PTP{j}"
            P.act(PTsbP[j].ap().rearrange("p (s c) -> p s c", c=640)[:, :, 0:512],
                  ARENA[:, 2048:3072].rearrange("p (s c) -> p s c", c=512),
                  AF.Copy, reads=("PTA", "PTB"), writes=(ptk,))
            P.act(PTsbP[j].ap().rearrange("p (s c) -> p s c", c=640)[:, :, 512:640],
                  ARENA.ap().rearrange("p (s c) -> p s c", c=1024)[:, 0:2, 640:768],
                  AF.Copy, reads=("BKA", "BKB"), writes=(ptk,))

        def stage_E(g):
            ts_ = [tinfo(2 * g), tinfo(2 * g + 1)]
            j = g % 2
            parts = []
            rd_ = [f"PTP{j}"]
            wr_ = []
            for mtp in range(5):
                for t in ts_:
                    h = t["h"]
                    p = t["p"]
                    rb_ = 64 * p
                    cb = 128 * (h // 2)
                    if t["spec"]:
                        vt = SVTb[:, mtp * 512 + 64 * h: mtp * 512 + 64 * h + 64]
                        rd_.append(f"SVTb.{mtp}")
                    else:
                        vt = VTb[:, (t["qt"] + mtp) * 512 + 64 * h:
                                 (t["qt"] + mtp) * 512 + 64 * h + 64]
                        rd_.append(f"VTb.{t['qt'] + mtp}")
                    wr_.append(f"H{p}")
                    parts.append((H[p][rb_:rb_ + 64, cb:cb + 128], vt,
                                  PTsbP[j][:, 640 * p + mtp * 128: 640 * p + mtp * 128 + 128],
                                  mtp == 0, mtp == 4, None))
            P.mseq(parts, reads=tuple(set(rd_)), writes=tuple(set(wr_)))
            if g % 4 == 3:
                if ts_[0]["spec"]:
                    P.cp("dve", oxb[0:64, :], H[0][0:64, 0:512],
                         reads=("H0",), writes=("oxb",))
                    P.cp("dve", oxb[64:128, :], H[1][64:128, 0:512],
                         reads=("H1",), writes=("oxb",))
                else:
                    qt = ts_[0]["qt"]
                    dst = oTb.ap().rearrange("p (c t) -> p c t", t=TPC)
                    P.cp("dve", dst[0:64, :, 128 * qt:128 * qt + 128],
                         H[0].rearrange("p (c t) -> p c t", t=128)[0:64],
                         reads=("H0",), writes=("oTb",))
                    P.cp("dve", dst[64:128, :, 128 * qt:128 * qt + 128],
                         H[1].rearrange("p (c t) -> p c t", t=128)[64:128],
                         reads=("H1",), writes=("oTb",))

        for g in range(NG + 4):
            if 3 <= g < NG + 3:
                stage_D(g - 3)
            if 4 <= g:
                stage_E(g - 4)
            if g < NG:
                stage_A(g)
                stage_BC(g)

        for och in range(FCH):
            P.mmg([(H[0][:, och:och + 1],
                    wob[:, ch * D + och * 128: ch * D + och * 128 + 128],
                    oxb[:, ch * 128 + 127: ch * 128 + 128], None)
                   for ch in range(4)],
                  reads=("wob", "oxb"), writes=("H0",))
        P.tt("dve", xmx[:], H[0][:, 0:FCH], xh1[:], ALU.add,
             reads=("H0", "xh1"), writes=("xmx",))
        P.ts("dve", xmx[:], xmx[:], validt[:, 0:1], ALU.mult,
             reads=("xmx", "validp"), writes=("xmx",))
        if l == 0 and "lyr" in dbg:
            dq = dout("dbg_qrb", [128, 4 * LQ], BF16)
            dk2 = dout("dbg_krb", [128, 4 * LQ], BF16)
            dv2 = dout("dbg_vrb", [128, 4 * LQ], BF16)
            do2 = dout("dbg_oTb", [128, 4 * TPC], BF16)
            P.dma("sp", dq, qrb[:], reads=qkeys("qrb"), writes=("dbq",))
            P.dma("sp", dk2, krb[:], reads=qkeys("krb"), writes=("dbk",))
            P.dma("sp", dv2, vrb[:], reads=vkeys, writes=("dbv",))
            P.dma("sp", do2, oTb[:], reads=("oTb",), writes=("dbo",))
        P.barrier()
        stAt.close()
        stVT.close()
        stL2.close()

        stH2 = ExitStack()
        sH = lambda nm, shp, dt=F32: stH2.enter_context(nc.sbuf_tensor(nm + str(l), shp, dt))
        xsm_sb = sH("xsm_sb", [128, FCH * 513])
        xsit = [sH(f"xsit{i}", [128, 512]) for i in range(4)]
        h2c = sH("h2c", [128, 512], BF16)
        sqh = sH("sqh", [128, 512], BF16)
        zb4 = sH("zb4", [128, 512])
        zb4b = sH("zb4b", [128, 512])
        for f in range(FCH):
            P.cp("dve", xsm_sb[:, f * 513:f * 513 + 1], xmx[:, f:f + 1],
                 reads=("xmx",), writes=(f"xsm.{f}",))

        def hslice(f):
            base = f * 513
            return xsm_sb[:, base + (1 if f < 4 else 0): base + (1 if f < 4 else 0) + 512]

        for tt in range(4):
            col0 = tt * 512
            for f in range(FCH):
                hk = f"H{f % 2}"
                h = H[f % 2]
                P.mmg([(h[:, 0:512],
                        wob[:, ch * D + f * 128: ch * D + f * 128 + 128],
                        oTb[:, ch * TPC + col0: ch * TPC + col0 + 512], None)
                       for ch in range(4)],
                      reads=("wob", "oTb"), writes=(hk,))
                xk = f"xsit{f % 4}"
                P.dma("sp", xsit[f % 4][:],
                      xs_in_d[f * 128:(f + 1) * 128, col0:col0 + 512],
                      writes=(xk,))
                mk = f"xsm.{f}"
                P.tt("dve", xsm_sb[:, f * 513 + 1: f * 513 + 513],
                     xsit[f % 4][:], h[:, 0:512], ALU.add,
                     reads=(xk, hk, mk), writes=(mk,))
                P.dma("act", dram["sp_xsm"][f * 128:(f + 1) * 128, 1 + col0:1 + col0 + 512],
                      xsm_sb[:, f * 513 + 1: f * 513 + 513],
                      reads=(mk,), writes=("d_xsm",))
                P.act(h2c[:], hslice(f), AF.Copy, reads=(mk,), writes=("h2c",))
                P.tt("dve", sqh[:], hslice(f), hslice(f), ALU.mult,
                     reads=(mk,), writes=("sqh",))
                P.mm1(G[2][0:1, 0:512], onesb[:, 0:1], h2c[:],
                      start=(f == 0), stop=(f == FCH - 1),
                      reads=("h2c", "onesb"), writes=("G2",))
                P.mm1(G[3][0:1, 0:512], onesb[:, 0:1], sqh[:],
                      start=(f == 0), stop=(f == FCH - 1),
                      reads=("sqh", "onesb"), writes=("G3",))
            ra, rb, rak, rbk = ln_stats_epi(G[2], G[3], "G2", "G3")
            for f in range(FCH):
                mk = f"xsm.{f}"
                zeng = "pool" if f % 3 == 0 else "dve"
                z, zk = (zb4, "zb4") if zeng == "pool" else (zb4b, "zb4b")
                P.tt(zeng, z[:], hslice(f), ra, ALU.mult,
                     reads=(mk, rak), writes=(zk,))
                P.tt(zeng, z[:], z[:], rb, ALU.subtract,
                     reads=(zk, rbk), writes=(zk,))
                P.ts(zeng, h2b[:, f * TPC + tt * 512: f * TPC + tt * 512 + 512],
                     z[:], lngt[:, l * FCH + f:l * FCH + f + 1], ALU.mult,
                     reads=(zk, "lng", "lnb"), writes=(f"h2b.{f}.{tt}",),
                     s2=lnbt[:, l * FCH + f:l * FCH + f + 1], op1=ALU.add)
                if tt < 3:
                    P.cp("dve", xsm_sb[:, f * 513:f * 513 + 1],
                         xsm_sb[:, f * 513 + 512:f * 513 + 513],
                         reads=(mk,), writes=(mk,))
        if l == 0 and "lyr" in dbg:
            dxm = dout("dbg_xsm", [D, 1 + TPC])
            P.dma("sp", dram["sp_xsm"].rearrange("(ff p) t -> p ff t", p=128)[:, :, 0:1],
                  xmx.ap().rearrange("p (ff o) -> p ff o", o=1),
                  reads=("xmx",), writes=("d_xsm",), slow=True)
            P.dma("sp", dxm, dram["sp_xsm"], reads=("d_xsm",), writes=("dbxm",))
        P.barrier()
        stH2.close()
        stL1.close()
        if l == 0:
            P.dma("pool", wob[:], dram["wo1"].rearrange("(c p) o -> p c o", p=128),
                  writes=("wob",))

        stW = ExitStack()
        sW = lambda nm, shp, dt=BF16: sW_st.enter_context(nc.sbuf_tensor(nm + str(l), shp, dt))
        sW_st = stW
        hgelu = stW.enter_context(nc.sbuf_tensor("hgelu" + str(l), [128, NBLK * TPC],
                                                 BF16, side="right"))
        hh = [sW(f"hh{i}", [128, 512], F32) for i in range(3)]
        glt = sW("glt", [128, 512])
        abt = sW("abt", [128, 512])
        w1b = [sW(f"w1b{i}", [128, FCH * 256]) for i in range(2)]
        w2b = [sW(f"w2b{i}", [128, NBLK * 128]) for i in range(2)]
        tf = sW("tf", [128, 512], F32)
        xsot = sW("xsot", [128, 512], F32)
        xsob = sW("xsob", [128, 512])

        h2keys = tuple(f"h2b.{f}.{tt}" for f in range(FCH) for tt in range(4))
        for i in range(NBLK):
            wk = f"w1b{i % 2}"
            P.dma("pool", w1b[i % 2].ap().rearrange("p (c o) -> p c o", o=256),
                  dram[f"w1_{l}"].rearrange("(c p) o -> p c o", p=128)[:, :, 256 * i:256 * i + 256],
                  writes=(wk,))
            for tt in range(4):
                ga = G[(2 * tt) % 4]
                gg = G[(2 * tt + 1) % 4]
                gak = f"G{(2 * tt) % 4}"
                ggk = f"G{(2 * tt + 1) % 4}"
                P.mmg([(ga[:, 0:512], w1b[i % 2][:, kch * 256: kch * 256 + 128],
                        h2b[:, kch * TPC + tt * 512: kch * TPC + tt * 512 + 512],
                        None) for kch in range(FCH)],
                      reads=(wk,) + h2keys, writes=(gak,))
                P.mmg([(gg[:, 0:512], w1b[i % 2][:, kch * 256 + 128: kch * 256 + 256],
                        h2b[:, kch * TPC + tt * 512: kch * TPC + tt * 512 + 512],
                        None) for kch in range(FCH)],
                      reads=(wk,) + h2keys, writes=(ggk,))
                P.act(glt[:], gg[:, 0:512], AF.Gelu,
                      bias=b1tt[:, l * 44 + 2 * i + 1:l * 44 + 2 * i + 2],
                      reads=(ggk, "b1t"), writes=("glt",))
                P.ts("dve", abt[:], ga[:, 0:512],
                     b1tt[:, l * 44 + 2 * i:l * 44 + 2 * i + 1], ALU.add,
                     reads=(gak, "b1t"), writes=("abt",))
                P.tt("dve", hgelu[:, i * TPC + tt * 512: i * TPC + tt * 512 + 512],
                     abt[:], glt[:], ALU.mult, reads=("abt", "glt"),
                     writes=(f"hg.{i}.{tt}",))
        hgkeys = tuple(f"hg.{i}.{tt}" for i in range(NBLK) for tt in range(4))
        for och in range(FCH):
            wk = f"w2b{och % 2}"
            P.dma("pool", w2b[och % 2].ap().rearrange("p (c o) -> p c o", o=128),
                  dram[f"w2_{l}"].rearrange("(c p) o -> p c o", p=128)[:, :, och * 128:(och + 1) * 128],
                  writes=(wk,))
            for tt in range(4):
                col0 = tt * 512
                hk = f"H{tt % 2}"
                h = H[tt % 2]
                P.mmg([(h[:, 0:512], w2b[och % 2][:, ki * 128:(ki + 1) * 128],
                        hgelu[:, ki * TPC + col0: ki * TPC + col0 + 512], None)
                       for ki in range(NBLK)],
                      reads=(wk,) + hgkeys, writes=(hk,))
                P.ts("dve", tf[:], h[:, 0:512],
                     b2tt[:, l * FCH + och:l * FCH + och + 1], ALU.add,
                     reads=(hk, "b2t"), writes=("tf",))
                xk = f"hh{(och * 4 + tt) % 3}"
                P.dma("sp", hh[(och * 4 + tt) % 3][:],
                      dram["sp_xsm"][och * 128:(och + 1) * 128, 1 + col0:1 + col0 + 512],
                      reads=("d_xsm",), writes=(xk,))
                P.tt("dve", xsot[:], tf[:], hh[(och * 4 + tt) % 3][:], ALU.add,
                     reads=("tf", xk), writes=("xsot",))
                P.dma("sp", xs_out_d[och * 128:(och + 1) * 128, col0:col0 + 512],
                      xsot[:], reads=("xsot",), writes=("d_xsout",))
                if write_next_xsb:
                    P.act(xsob[:], xsot[:], AF.Copy, reads=("xsot",),
                          writes=("xsob",))
                    P.dma("sp", dram["d_xsb"][och * 128:(och + 1) * 128,
                                              QCOL0 + col0:QCOL0 + col0 + 512],
                          xsob[:], reads=("xsob",), writes=("d_xsb_own",))
        P.barrier()
        stW.close()
        stH1.close()

    emit_layer(0, dram["sp_xin0"], dram["sp_xin1"], True,
               "d_h2out", dram["h2_out"])
    if upto == "layer1":
        if "xs1" in dbg:
            dx = dout("dbg_xs1", [D, TPC])
            P.dma("sp", dx, dram["sp_xin1"], reads=("d_xsout",), writes=("dbg3",))
        P.barrier()
        P.emit()
        stXB.close()
        warena.close()
        es.close()
        return nc, dram

    P.dma("pool", dram["h3_in"], dram["sp_xin1"][:, TPC - HALO:TPC],
          reads=("d_xsout",), writes=("d_h3in",))
    P.coll("AllGather", ALU.bypass, [[0, 1, 2, 3], [4, 5, 6, 7]],
           [dram["h3_in"]], [dram["h3_out"]], sem="cc3",
           reads=("d_h3in",), writes=("d_h3out",))

    emit_layer(1, dram["sp_xin1"], dram["sp_xs2"], False,
               "d_h3out", dram["h3_out"])

    stF = ExitStack()
    sF = lambda nm, shp, dt=F32: stF.enter_context(nc.sbuf_tensor(nm, shp, dt))
    xsf2 = [sF(f"xsf{i}", [128, FCH * 512]) for i in range(2)]
    x1t = [sF(f"x1t{i}", [128, 512]) for i in range(4)]
    xcb2 = [sF(f"xcb2{i}", [128, 512], BF16) for i in range(2)]
    sqf2 = [sF(f"sqf2{i}", [128, 512], BF16) for i in range(2)]
    zb5 = sF("zb5", [128, 512])
    zb5b = sF("zb5b", [128, 512])
    ytb = sF("ytb", [128, 512])
    yt = sF("yt", [128, 512])

    def load_xs(tt):
        j = tt % 2
        col0 = tt * 512
        for f in range(FCH):
            P.dma("act", xsf2[j][:, f * 512:(f + 1) * 512],
                  dram["sp_xs2"][f * 128:(f + 1) * 128, col0:col0 + 512],
                  reads=("d_xsout",), writes=(f"xsf{j}.{f}",))

    load_xs(0)
    for tt in range(4):
        col0 = tt * 512
        j = tt % 2
        xsf = xsf2[j]
        for f in range(FCH):
            sl = slice(f * 512, (f + 1) * 512)
            fk = f"xsf{j}.{f}"
            ck, sk = f"xcb2{f % 2}", f"sqf2{f % 2}"
            P.act(xcb2[f % 2][:], xsf[:, sl], AF.Copy, reads=(fk,),
                  writes=(ck,))
            P.tt("dve", sqf2[f % 2][:], xsf[:, sl], xsf[:, sl], ALU.mult,
                 reads=(fk,), writes=(sk,))
            P.mm1(BIG[0:1, 0:512], onesb[:, 0:1], xcb2[f % 2][:],
                  start=(f == 0), stop=(f == FCH - 1),
                  reads=(ck, "onesb"), writes=("BIGa",))
            P.mm1(BIG[0:1, 512:1024], onesb[:, 0:1], sqf2[f % 2][:],
                  start=(f == 0), stop=(f == FCH - 1),
                  reads=(sk, "onesb"), writes=("BIGb",))
        ra, rb, rak, rbk = ln_stats_epi()
        if tt < 3:
            load_xs(tt + 1)
        for f in range(FCH):
            sl = slice(f * 512, (f + 1) * 512)
            xk = f"x1t{f % 4}"
            P.dma("pool", x1t[f % 4][:], dram["sp_x1"][f * 128:(f + 1) * 128, col0:col0 + 512],
                  reads=("d_x1",), writes=(xk,))
            zeng = "pool" if f % 3 == 0 else "dve"
            z, zk = (zb5, "zb5") if zeng == "pool" else (zb5b, "zb5b")
            y_, yk = (yt, "yt") if zeng == "pool" else (ytb, "ytb")
            P.tt(zeng, z[:], xsf[:, sl], ra, ALU.mult,
                 reads=(f"xsf{j}.{f}", rak), writes=(zk,))
            P.tt(zeng, z[:], z[:], rb, ALU.subtract,
                 reads=(zk, rbk), writes=(zk,))
            P.tt(zeng, y_[:], x1t[f % 4][:], z[:], ALU.add,
                 reads=(xk, zk), writes=(yk,))
            P.dma("sp", dram["y"][f * 128:(f + 1) * 128, col0:col0 + 512],
                  y_[:], reads=(yk,), writes=("d_y",))
    stF.close()

    P.barrier()
    P.emit()
    stXB.close()
    warena.close()
    es.close()
    return nc, dram


_NC_CACHE = {}


def kernel(**inputs):
    in_maps = [host_inputs(inputs, c) for c in range(8)]
    if "all" not in _NC_CACHE:
        _NC_CACHE["all"] = build_nc()
    nc, dram = _NC_CACHE["all"]
    res = run_bass_kernel_spmd(nc, in_maps, core_ids=list(range(8)))
    out = np.empty((B, N, D), np.float32)
    for c in range(8):
        b, s = c // 4, c % 4
        out[b, s * TPC:(s + 1) * TPC, :] = res.results[c]["y"].T
    return out


# version 38
# speedup vs baseline: 1.0051x; 1.0051x over previous
import numpy as np
import ml_dtypes
from contextlib import ExitStack
import concourse.bass as bass
import concourse.mybir as mybir
from concourse.bass_utils import run_bass_kernel_spmd

F32 = mybir.dt.float32
BF16 = mybir.dt.bfloat16
AF = mybir.ActivationFunctionType
ALU = mybir.AluOpType
AX = mybir.AxisListType.X
BF = ml_dtypes.bfloat16

B, N, D = 2, 8192, 1024
FCH = 8
TPC = 2048
CH = 512
NCH = TPC // CH
HEADS, DH, WIN, INNER, HID = 8, 64, 512, 512, 2730
QKS = 8.0
HALO = 514
TW = 1 + HALO + TPC
QCOL0 = 515
NEG = -1e9
EPS = 1e-5
ATTN_PIPE = True
ATTN_STAGES = "ABCDE"
W1BLK = [128] * 21 + [42]
W1OFF = np.cumsum([0] + W1BLK).tolist()
NBLK = len(W1BLK)

ENGS = ("pe", "act", "dve", "pool", "sp")


class Prog:

    QR = 8

    def __init__(self, nc, es):
        names = ["pe", "act", "dve", "pool", "sp", "cc1", "cc2", "cc3", "ccf1"]
        for q in ("sp", "pool", "act"):
            names += [f"q_{q}{i}" for i in range(self.QR)]
        self.nc = nc
        self.qrot = {"sp": 0, "pool": 0, "act": 0}
        self.sems = {s: es.enter_context(nc.semaphore(s)) for s in names}
        self.cnt = {s: 0 for s in names}
        self.plan = {e: [] for e in ENGS}
        self.lastw = {e: {} for e in ENGS}
        self.wr = {}
        self.rd = {}

    def _filter(self, eng, deps):
        out = []
        for d in deps:
            if d is None:
                continue
            s, v = d
            if self.lastw[eng].get(s, 0) < v:
                out.append((s, v))
                self.lastw[eng][s] = v
        return out

    def _rw_deps(self, reads, writes):
        deps = []
        for k in reads:
            deps.extend(self.wr.get(k, ()))
        for k in writes:
            deps.extend(self.wr.get(k, ()))
            deps.extend(self.rd.get(k, ()))
        return deps

    def _rw_post(self, tok, reads, writes):
        for k in reads:
            self.rd.setdefault(k, []).append(tok)
        for k in writes:
            self.wr.setdefault(k, []).append(tok)

    def step(self, eng, fn, reads=(), writes=(), extra=(), sem="auto", amt=1):
        deps = self._rw_deps(reads, writes) + list(extra)
        waits = self._filter(eng, deps)
        s = eng if sem == "auto" else sem
        self.cnt[s] += amt
        tok = (s, self.cnt[s])
        self.plan[eng].append((waits, fn, [(s, amt)]))
        self._rw_post(tok, reads, writes)
        return tok

    def dma(self, q, out, in_, reads=(), writes=(), extra=(), slow=False):
        i = self.qrot[q]
        self.qrot[q] = (i + 1) % self.QR
        s = f"q_{q}{i}"
        extra = list(extra) + [(s, self.cnt[s])]
        return self.step(
            q, lambda e: e.dma_start(out=out, in_=in_,
                                     allow_slow_non_contiguous=slow),
            reads, writes, extra, sem=s, amt=16)

    def mmg(self, parts, reads=(), writes=(), extra=()):
        deps = self._rw_deps(reads, writes) + list(extra)
        waits = self._filter("pe", deps)
        n = len(parts)
        for i, (o, l, r, tp) in enumerate(parts):
            st, sp = (i == 0), (i == n - 1)
            def f(e, o=o, l=l, r=r, st=st, sp=sp, tp=tp):
                return e.matmul(o, l, r, start=st, stop=sp, tile_position=tp)
            if sp:
                self.cnt["pe"] += 1
                tok = ("pe", self.cnt["pe"])
                self.plan["pe"].append((waits if i == 0 else [], f, [("pe", 1)]))
            else:
                self.plan["pe"].append((waits if i == 0 else [], f, []))
        self._rw_post(tok, reads, writes)
        return tok

    def mm1(self, out, lhsT, rhs, start, stop, reads=(), writes=(), extra=(),
            tp=None):
        def f(e):
            return e.matmul(out, lhsT, rhs, start=start, stop=stop,
                            tile_position=tp)
        return self.step("pe", f, reads, writes, extra)

    def mseq(self, parts, reads=(), writes=(), extra=()):
        deps = self._rw_deps(reads, writes) + list(extra)
        waits = self._filter("pe", deps)
        self.cnt["pe"] += 1
        tok = ("pe", self.cnt["pe"])
        n = len(parts)
        for i, (o, l, r, st, sp, tp) in enumerate(parts):
            def f(e, o=o, l=l, r=r, st=st, sp=sp, tp=tp):
                return e.matmul(o, l, r, start=st, stop=sp, tile_position=tp)
            self.plan["pe"].append((waits if i == 0 else [], f,
                                    [("pe", 1)] if i == n - 1 else []))
        self._rw_post(tok, reads, writes)
        return tok

    def cp(self, eng, out, in_, reads=(), writes=(), extra=()):
        return self.step(eng, lambda e: e.tensor_copy(out, in_),
                         reads, writes, extra)

    def act(self, out, in_, func, reads=(), writes=(), extra=(), bias=0.0,
            scale=1.0, accum_out=None):
        def f(e):
            return e.activation(out, in_, func, bias=bias, scale=scale,
                                accum_out=accum_out)
        return self.step("act", f, reads, writes, extra)

    def tt(self, eng, out, in0, in1, op, reads=(), writes=(), extra=()):
        return self.step(eng, lambda e: e.tensor_tensor(out, in0, in1, op),
                         reads, writes, extra)

    def ts(self, eng, out, in0, s1, op0, reads=(), writes=(), extra=(),
           s2=None, op1=ALU.bypass):
        def f(e):
            return e.tensor_scalar(out, in0, s1, s2, op0, op1)
        return self.step(eng, f, reads, writes, extra)

    def stt(self, eng, out, in0, scalar, in1, op0, op1, reads=(), writes=(),
            extra=()):
        def f(e):
            return e.scalar_tensor_tensor(out, in0, scalar, in1, op0, op1)
        return self.step(eng, f, reads, writes, extra)

    def red(self, out, in_, op, reads=(), writes=(), extra=(), negate=False):
        def f(e):
            return e.tensor_reduce(out, in_, AX, op, negate=negate)
        return self.step("dve", f, reads, writes, extra)

    def ttr(self, out, in0, in1, scalar, accum_out, op0, op1, reads=(),
            writes=(), extra=(), scale=1.0):
        def f(e):
            return e.tensor_tensor_reduce(out=out, in0=in0, in1=in1,
                                          scale=scale, scalar=scalar,
                                          op0=op0, op1=op1,
                                          accum_out=accum_out)
        return self.step("dve", f, reads, writes, extra)

    def recip(self, out, in_, reads=(), writes=(), extra=()):
        return self.step("dve", lambda e: e.reciprocal(out, in_),
                         reads, writes, extra)

    def memset(self, eng, ap, val, reads=(), writes=(), extra=()):
        return self.step(eng, lambda e: e.memset(ap, val), reads, writes, extra)

    def coll(self, kind, op, groups, ins, outs, sem, reads=(), writes=(),
             fence_sem="pool"):
        def f(e):
            return e.collective_compute(kind, op, replica_groups=groups,
                                        ins=ins, outs=outs)
        self.step("pool", f, reads, writes, sem=sem)
        return self.step("pool", lambda e: e.engine_nop(), (), writes,
                         sem=fence_sem)

    def barrier(self):
        toks = [(s, self.cnt[s]) for s in self.cnt if not s.startswith("cc")]
        for e in ENGS:
            w = self._filter(e, toks)
            if w:
                self.plan[e].append((w, None, []))
        self.wr = {}
        self.rd = {}

    def emit(self):
        nc, sems = self.nc, self.sems

        def replay(name):
            def f(eng):
                for waits, fn, incs in self.plan[name]:
                    for s, v in waits:
                        eng.wait_ge(sems[s], v)
                    if fn is not None:
                        inst = fn(eng)
                        for s, a in incs:
                            inst.then_inc(sems[s], a)
            return f

        with nc.Block() as block:
            block.tensor(replay("pe"))
            block.scalar(replay("act"))
            block.vector(replay("dve"))
            block.gpsimd(replay("pool"))
            block.sync(replay("sp"))



def host_shared():
    t = {}
    t["identb"] = np.eye(128, dtype=BF)
    t["onesb"] = np.ones((128, 1), BF)
    t["onesf"] = np.ones((1, 128), np.float32)
    hm2 = np.zeros((128, 128), np.float32)
    for g in range(4):
        hm2[g * 32:(g + 1) * 32, g * 32:(g + 1) * 32] = 1.0
    t["hm2b"] = hm2.astype(BF)
    j = np.arange(CH, dtype=np.float32)
    t["qexp"] = np.broadcast_to(j, (128, CH)).copy()
    t["kexp"] = np.broadcast_to(CH - 1 - j, (128, CH)).copy()
    p = np.arange(128, dtype=np.float32)[:, None]
    ecl, d01 = [], []
    for qt in range(NCH):
        E = (128 * qt + p) - 1.0 - j[None, :]
        ecl.append(np.maximum(E, 0.0))
        d01.append((E >= 0).astype(np.float32))
    t["eclamp"] = np.concatenate(ecl, 1)
    t["d01"] = np.concatenate(d01, 1)
    t["chexp"] = np.broadcast_to(np.arange(NCH, dtype=np.float32) * CH,
                                 (128, NCH)).copy()
    return t


def host_percore(core):
    b, s = core // 4, core % 4
    t0 = s * TPC
    out = {}
    ce = np.zeros((128, 4), np.float32)
    cv = np.zeros((128, 4), np.float32)
    for cp in range(s):
        ce[:, cp] = TPC * (s - 1 - cp)
        cv[:, cp] = 1.0
    out["coefexp"], out["coefval"] = ce, cv
    hs = np.zeros((128, 4), np.float32)
    if s > 0:
        hs[:, s - 1] = 1.0
    out["hsel"] = hs
    out["validp"] = np.full((128, 1), 1.0 if s > 0 else 0.0, np.float32)
    toks = t0 - 513 + np.arange(2561)
    half = np.arange(0, DH, 2, dtype=np.float64)
    inv_freq = 1.0 / 10000.0 ** (half / DH)
    base = (half + 0.4 * DH) / (1.4 * DH)
    tt = toks.astype(np.float64)
    freqs = tt[:, None] * inv_freq[None, :]
    power = (tt - N // 2) / float(WIN // 2)
    scale = base[None, :] ** power[:, None]
    cos, sin = np.cos(freqs), np.sin(freqs)
    neg = toks < 0
    csq = cos * scale * QKS; ssq = sin * scale * QKS
    csk = cos / scale; ssk = sin / scale
    for a in (csq, ssq, csk, ssk):
        a[neg] = 0.0

    def tile(a):
        return np.tile(a.T.astype(np.float32), (4, 1))
    out["csq"], out["ssq"] = tile(csq), tile(ssq)
    out["csk"], out["ssk"] = tile(csk), tile(ssk)
    pp = np.arange(128)[:, None]
    mm = np.arange(640)[None, :]
    band = (mm >= pp) & (mm <= pp + 512)
    slots = []
    for qt in range(4):
        if s == 0:
            keyabs = t0 + 128 * qt - 512 + mm
            msk = band & (keyabs >= 0)
        else:
            msk = band
        slots.append(np.where(msk, 0.0, NEG))
    slots.append(np.where(band, 0.0, NEG))
    mb = np.concatenate(slots, 0)
    out["maskb"] = np.tile(mb, (1, 2)).astype(BF)
    sp = np.full((128, 640), NEG, np.float32)
    sp[127, 0:513] = 0.0
    out["maskspec"] = np.tile(sp, (1, 2)).astype(BF)
    return out


def host_inputs(inputs, core):
    b, s = core // 4, core % 4
    t0 = s * TPC
    g = lambda k: np.asarray(inputs[k])
    m = {}
    m["xT"] = np.ascontiguousarray(g("L0_attn_x")[b, t0:t0 + TPC, :].T)
    for w in ("Wq", "Wk", "Wv", "Wo"):
        m[w] = np.ascontiguousarray(g("L0_attn_" + w))
    for bias, nm in (("bq", "L0_attn_bq"), ("bk", "L0_attn_bk"),
                     ("bv", "L0_attn_bv"), ("bo", "L0_attn_bo")):
        m[bias] = np.ascontiguousarray(g(nm).reshape(FCH, 128).T)
    m["dparam"] = np.broadcast_to(
        g("L0_attn_decay_param").reshape(1, 1), (128, 1)).copy()
    perm = []
    for grp in range(2):
        for hi in range(2):
            for h in range(4 * grp, 4 * grp + 4):
                for d in range(32):
                    perm.append(64 * h + 32 * hi + d)
    perm = np.array(perm)
    wq = g("L0_ffn_wqkv")
    for l in range(2):
        w = wq[l]
        m[f"wqkv{l}"] = np.ascontiguousarray(
            np.concatenate([w[:, perm], w[:, 512 + perm], w[:, 1024:]], 1))
        m[f"wo{l}"] = np.ascontiguousarray(g("L0_ffn_wo")[l])
    m["lng"] = np.ascontiguousarray(
        g("L0_ffn_ln_g").reshape(2, FCH, 128).transpose(2, 1, 0).reshape(128, FCH * 2, order="F")) \
        if False else np.ascontiguousarray(
        np.concatenate([g("L0_ffn_ln_g")[l].reshape(FCH, 128).T for l in range(2)], 1))
    m["lnb"] = np.ascontiguousarray(
        np.concatenate([g("L0_ffn_ln_b")[l].reshape(FCH, 128).T for l in range(2)], 1))
    qs = g("L0_ffn_q_scale"); ks = g("L0_ffn_k_scale")
    m["qsl"] = np.ascontiguousarray(np.stack([np.tile(qs[l][0:32], 4) for l in range(2)], 1))
    m["qsh"] = np.ascontiguousarray(np.stack([np.tile(qs[l][32:64], 4) for l in range(2)], 1))
    m["ksl"] = np.ascontiguousarray(np.stack([np.tile(ks[l][0:32], 4) for l in range(2)], 1))
    m["ksh"] = np.ascontiguousarray(np.stack([np.tile(ks[l][32:64], 4) for l in range(2)], 1))
    w1 = g("L0_ffn_w1"); b1 = g("L0_ffn_b1"); w2 = g("L0_ffn_w2")
    HIDP = 2816
    b1tiles = []
    for l in range(2):
        w1p = np.zeros((D, 2 * HIDP), np.float32)
        w2p = np.zeros((HIDP, D), np.float32)
        bt = np.zeros((128, 44), np.float32)
        for i in range(NBLK):
            w_ = W1BLK[i]
            a0 = W1OFF[i]
            w1p[:, 256 * i: 256 * i + w_] = w1[l][:, a0:a0 + w_]
            w1p[:, 256 * i + 128: 256 * i + 128 + w_] = w1[l][:, HID + a0:HID + a0 + w_]
            w2p[128 * i: 128 * i + w_, :] = w2[l][a0:a0 + w_, :]
            bt[0:w_, 2 * i] = b1[l][a0:a0 + w_]
            bt[0:w_, 2 * i + 1] = b1[l][HID + a0:HID + a0 + w_]
        m[f"w1_{l}"] = w1p
        m[f"w2_{l}"] = w2p
        b1tiles.append(bt)
    m["b1t"] = np.ascontiguousarray(np.concatenate(b1tiles, 1))
    m["b2t"] = np.ascontiguousarray(
        np.concatenate([g("L0_ffn_b2")[l].reshape(FCH, 128).T for l in range(2)], 1))
    m.update(host_shared())
    m.update(host_percore(core))
    return m



HIDP = 2816


def build_nc(upto="all", dbg=()):
    nc = bass.Bass("TRN2", target_bir_lowering=False, debug=False)
    es = ExitStack()
    P = Prog(nc, es)
    dbg = set(dbg)
    dram = {}

    def din(name, shape, dtype=F32):
        dram[name] = nc.dram_tensor(name, shape, dtype, kind="ExternalInput").ap()
        return dram[name]

    def dout(name, shape, dtype=F32):
        dram[name] = nc.dram_tensor(name, shape, dtype, kind="ExternalOutput").ap()
        return dram[name]

    def dint(name, shape, dtype=F32):
        dram[name] = nc.dram_tensor(name, shape, dtype).ap()
        return dram[name]

    din("xT", [D, TPC])
    for w in ("Wq", "Wk", "Wv", "Wo"):
        din(w, [D, D])
    for bias in ("bq", "bk", "bv", "bo"):
        din(bias, [128, FCH])
    din("dparam", [128, 1])
    din("qexp", [128, CH]); din("kexp", [128, CH])
    din("eclamp", [128, NCH * CH]); din("d01", [128, NCH * CH])
    din("chexp", [128, NCH]); din("coefexp", [128, 4]); din("coefval", [128, 4])
    din("hsel", [128, 4]); din("validp", [128, 1])
    din("identb", [128, 128], BF16); din("onesb", [128, 1], BF16)
    din("onesf", [1, 128]); din("hm2b", [128, 128], BF16)
    din("wqkv0", [D, 3 * INNER]); din("wqkv1", [D, 3 * INNER])
    din("wo0", [INNER, D]); din("wo1", [INNER, D])
    din("lng", [128, FCH * 2]); din("lnb", [128, FCH * 2])
    din("qsl", [128, 2]); din("qsh", [128, 2]); din("ksl", [128, 2]); din("ksh", [128, 2])
    din("w1_0", [D, 2 * HIDP]); din("w1_1", [D, 2 * HIDP])
    din("b1t", [128, 44 * 2])
    din("w2_0", [HIDP, D]); din("w2_1", [HIDP, D])
    din("b2t", [128, FCH * 2])
    din("csq", [128, 2561]); din("ssq", [128, 2561])
    din("csk", [128, 2561]); din("ssk", [128, 2561])
    din("maskb", [5 * 128, 1280], BF16)
    din("maskspec", [128, 1280], BF16)
    dout("y", [D, TPC])

    dint("a_in", [D, D], BF16); dint("a_out", [4 * D, D], BF16)
    dint("h2_in", [D, HALO], BF16); dint("h2_out", [4 * D, HALO], BF16)
    dint("h3_in", [D, HALO], BF16); dint("h3_out", [4 * D, HALO], BF16)
    dint("sp_x1", [D, TPC])
    dint("sp_outT", [D, TPC], BF16)
    dint("sp_qt", [D, TPC], BF16)
    dint("sp_xin0", [D, TPC]); dint("sp_xin1", [D, TPC]); dint("sp_xs2", [D, TPC])
    dint("sp_xsm", [D, 1 + TPC])
    sp_xin = [dram["sp_xin0"], dram["sp_xin1"]]
    sp_xout = [dram["sp_xin1"], dram["sp_xs2"]]

    def drr(name, f):
        return dram[name][f * 128:(f + 1) * 128, :]

    sbl = lambda nm, shp, dt=F32: es.enter_context(nc.sbuf_tensor(nm, shp, dt))
    ident = sbl("ident", [128, 128], BF16)
    onesb = sbl("ones_b", [128, 1], BF16)
    onesf = sbl("ones_f", [1, 128], F32)
    hm2 = sbl("hm2", [128, 128], BF16)
    Lt = sbl("Lt", [128, 1], F32)
    dCt = sbl("dCt", [128, 1], F32)
    stDer = ExitStack()
    qvec = stDer.enter_context(nc.sbuf_tensor("qvec", [128, CH], F32, side="right"))
    kvec = stDer.enter_context(nc.sbuf_tensor("kvec", [128, CH], F32, side="right"))
    dmask = stDer.enter_context(nc.sbuf_tensor("dmask", [128, NCH * CH], BF16, side="right"))
    chs = sbl("chs", [128, NCH], F32)
    coefs = sbl("coefs", [128, 4], F32)
    hselt = sbl("hselt", [128, 4], F32)
    validt = sbl("validt", [128, 1], F32)
    biast = {w: sbl("bias_" + w, [128, FCH], F32) for w in ("bq", "bk", "bv", "bo")}
    lngt = sbl("lngt", [128, FCH * 2], F32)
    lnbt = sbl("lnbt", [128, FCH * 2], F32)
    qst = {k: sbl("t" + k, [128, 2], F32) for k in ("qsl", "qsh", "ksl", "ksh")}
    b2tt = sbl("b2tt", [128, FCH * 2], F32)
    b1tt = sbl("b1tt", [128, 88], F32)
    stat = sbl("stat", [1, 1536], F32)
    lnrep = [sbl(f"lnrep{i}", [128, 1024], F32) for i in range(2)]
    onec = sbl("onec", [128, 1], F32)
    epsc = sbl("epsc", [128, 1], F32)
    st2 = sbl("st2", [128, 8], F32)
    sctch = sbl("sctch", [128, 8], F32)
    xh1 = sbl("xh1", [128, FCH], F32)

    ARENA = es.enter_context(nc.psum_tensor("ARENA", [128, 4096], F32))
    G = [ARENA[:, 512 * i:512 * (i + 1)] for i in range(4)]
    BIG = ARENA[:, 2048:3072]
    H = [ARENA[:, 3072 + 512 * i:3072 + 512 * (i + 1)] for i in range(2)]
    BA = BIG[:, 0:512]
    BB = BIG[:, 512:1024]

    for nm, ap in (("identb", ident), ("onesb", onesb), ("onesf", onesf),
                   ("hm2b", hm2), ("hsel", hselt), ("validp", validt),
                   ("lng", lngt), ("lnb", lnbt), ("b2t", b2tt), ("b1t", b1tt)):
        P.dma("sp", ap[:], dram[nm], writes=(nm,))
    for w in ("bq", "bk", "bv", "bo"):
        P.dma("sp", biast[w][:], dram[w], writes=(w,))
    for k in qst:
        P.dma("sp", qst[k][:], dram[k], writes=(k,))

    P.memset("dve", onec[:], 1.0, writes=("onec",))
    P.memset("dve", epsc[:], EPS, writes=("epsc",))
    P.dma("sp", sctch[:, 0:1], dram["dparam"], writes=("sc0",))
    P.act(sctch[:, 1:2], sctch[:, 0:1], AF.Abs, reads=("sc0",), writes=("sc1",))
    P.act(sctch[:, 2:3], sctch[:, 1:2], AF.Exp, scale=-1.0,
          reads=("sc1",), writes=("sc2",))
    P.act(sctch[:, 3:4], sctch[:, 2:3], AF.Ln, bias=onec[:, 0:1],
          reads=("sc2", "onec"), writes=("sc3",))
    P.ts("dve", sctch[:, 4:5], sctch[:, 0:1], 0.0, ALU.min,
         reads=("sc0",), writes=("sc4",))
    P.tt("dve", Lt[:], sctch[:, 4:5], sctch[:, 3:4], ALU.subtract,
         reads=("sc3", "sc4"), writes=("Lt",))

    def dpow(dst, key, src, w):
        P.dma("sp", dst, src, writes=(key,))
        P.ts("dve", dst, dst, Lt[:, 0:1], ALU.mult, reads=(key, "Lt"), writes=(key,))
        P.act(dst, dst, AF.Exp, reads=(key,), writes=(key,))

    dpow(qvec[:], "qvec", dram["qexp"], CH)
    dpow(kvec[:], "kvec", dram["kexp"], CH)
    dpow(chs[:], "chs", dram["chexp"], NCH)
    P.ts("dve", dCt[:], Lt[:], float(CH), ALU.mult, reads=("Lt",), writes=("dC",))
    P.act(dCt[:], dCt[:], AF.Exp, reads=("dC",), writes=("dC",))
    stSet = ExitStack()
    d01t = stSet.enter_context(nc.sbuf_tensor("d01t", [128, NCH * CH], F32))
    ecl = stSet.enter_context(nc.sbuf_tensor("eclt", [128, NCH * CH], F32))
    dpow(ecl[:], "eclt", dram["eclamp"], NCH * CH)
    P.dma("sp", d01t[:], dram["d01"], writes=("d01t",))
    P.tt("dve", dmask[:], ecl[:], d01t[:], ALU.mult,
         reads=("eclt", "d01t"), writes=("dmask",))
    cfet = stSet.enter_context(nc.sbuf_tensor("cfet", [128, 4], F32))
    dpow(cfet[:], "cfe", dram["coefexp"], 4)
    cvt = stSet.enter_context(nc.sbuf_tensor("cvt", [128, 4], F32))
    P.dma("sp", cvt[:], dram["coefval"], writes=("cvt",))
    P.tt("dve", coefs[:], cfet[:], cvt[:], ALU.mult,
         reads=("cfe", "cvt"), writes=("coefs",))
    stSet.close()

    _lnrot = [0]

    def ln_stats_epi(bkA=None, bkB=None, kA="BIGa", kB="BIGb", n_feat=1024.0):
        bkA = BA if bkA is None else bkA
        bkB = BB if bkB is None else bkB
        j = _lnrot[0] % 2
        _lnrot[0] += 1
        ra = lnrep[j][:, 0:512]; rb = lnrep[j][:, 512:1024]
        rak, rbk = f"lnra{j}", f"lnrb{j}"
        me = stat[0:1, 0:512]; va = stat[0:1, 512:1024]
        rs = stat[0:1, 1024:1536]; mr = va
        P.act(me, bkA[0:1, 0:512], AF.Copy, scale=1.0 / n_feat,
              reads=(kA,), writes=("st_me",))
        P.act(va, me, AF.Square, reads=("st_me",), writes=("st_va",))
        P.stt("dve", va, bkB[0:1, 0:512], 1.0 / n_feat, va, ALU.mult,
              ALU.subtract, reads=(kB, "st_va"), writes=("st_va",))
        P.act(rs, va, AF.Ln, bias=epsc[0:1, 0:1], reads=("st_va", "epsc"),
              writes=("st_rs",))
        P.act(rs, rs, AF.Exp, scale=-0.5, reads=("st_rs",), writes=("st_rs",))
        P.tt("dve", mr, me, rs, ALU.mult, reads=("st_me", "st_rs"), writes=("st_va",))
        P.mmg([(bkA[:, 0:512], onesf[0:1, :], rs, None)], reads=("st_rs",), writes=(kA,))
        P.mmg([(bkB[:, 0:512], onesf[0:1, :], mr, None)], reads=("st_va",), writes=(kB,))
        P.act(ra, bkA[:, 0:512], AF.Copy, reads=(kA,), writes=(rak,))
        P.act(rb, bkB[:, 0:512], AF.Copy, reads=(kB,), writes=(rbk,))
        return ra, rb, rak, rbk

    stA = ExitStack()
    sba = lambda nm, shp, dt=F32: stA.enter_context(nc.sbuf_tensor(nm, shp, dt))
    Mst = sba("Mst", [128, FCH * D], F32)
    Mbf = sba("Mbf", [128, FCH * D], BF16)
    Wqs = sba("Wqs", [128, FCH * D], BF16)
    Wks = sba("Wks", [128, FCH * D], BF16)
    Wvs = sba("Wvs", [128, FCH * D], BF16)
    xt = [sba(f"xt{i}", [128, 512], F32) for i in range(1)]
    xcb = [sba(f"xcb{i}", [128, 512], BF16) for i in range(2)]
    sqc = [sba(f"sqc{i}", [128, 512], BF16) for i in range(2)]
    xab = sba("xab", [128, FCH * CH], BF16)
    qb = sba("q_b", [128, FCH * CH], BF16)
    kb = sba("k_b", [128, FCH * CH], BF16)
    vb = sba("v_b", [128, FCH * CH], BF16)
    ktl = sba("ktl", [128, FCH * CH], BF16)
    qtl = sba("qtl", [128, FCH * CH], BF16)
    VT = sba("VTt", [128, NCH * D], BF16)
    KT = sba("KTt", [128, NCH * D], BF16)
    Pb = sba("Pb", [128, 4 * CH], BF16)
    PT = sba("PTt", [128, 4 * CH], BF16)
    zb = sba("zb", [128, 512], F32)
    zb2 = sba("zb2", [128, 512], F32)
    oste = [sba(f"oste{i}", [128, 512], F32) for i in range(2)]

    P.dma("pool", Wqs[:], dram["Wq"].rearrange("(c p) o -> p c o", p=128),
          writes=("Wqs",))
    P.dma("pool", Wks[:], dram["Wk"].rearrange("(c p) o -> p c o", p=128),
          writes=("Wks",))
    P.dma("pool", Wvs[:], dram["Wv"].rearrange("(c p) o -> p c o", p=128),
          writes=("Wvs",))

    def wsl(wt, kch, och):
        return wt[:, kch * D + och * 128: kch * D + och * 128 + 128]

    for c in range(NCH):
        col0 = c * CH
        xbufs = [xt[0], oste[0], oste[1]]
        xkeys_ = ["xt0", "oste0", "oste1"]
        for f in range(FCH):
            bkey = xkeys_[f % 3]
            ck, sk = f"xcb{f % 2}", f"sqc{f % 2}"
            P.dma("sp", xbufs[f % 3][:], drr("xT", f)[:, col0:col0 + CH],
                  writes=(bkey,))
            P.act(xcb[f % 2][:], xbufs[f % 3][:], AF.Copy, reads=(bkey,),
                  writes=(ck,))
            P.tt("dve", sqc[f % 2][:], xbufs[f % 3][:], xbufs[f % 3][:],
                 ALU.mult, reads=(bkey,), writes=(sk,))
            P.mm1(H[0][0:1, 0:512], onesb[:, 0:1], xcb[f % 2][:],
                  start=(f == 0), stop=(f == FCH - 1),
                  reads=(ck, "onesb"), writes=("H0",))
            P.mm1(H[1][0:1, 0:512], onesb[:, 0:1], sqc[f % 2][:],
                  start=(f == 0), stop=(f == FCH - 1),
                  reads=(sk, "onesb"), writes=("H1",))
        ra, rb, rak, rbk = ln_stats_epi(H[0], H[1], "H0", "H1")
        for f in range(FCH):
            bkey = xkeys_[f % 3]
            P.dma("act", xbufs[f % 3][:], drr("xT", f)[:, col0:col0 + CH],
                  writes=(bkey,))
            P.tt("pool", zb[:], xbufs[f % 3][:], ra, ALU.mult,
                 reads=(bkey, rak), writes=("zb",))
            P.tt("pool", xab[:, f * CH:(f + 1) * CH], zb[:], rb, ALU.subtract,
                 reads=("zb", rbk), writes=(f"xab.{f}",))
        xareads = tuple(f"xab.{f}" for f in range(FCH))

        def gkey(i):
            return f"G{i}"

        for och in range(FCH):
            g = G[och % 2]
            P.mmg([(g[:, 0:CH], wsl(Wqs, kch, och),
                    xab[:, kch * CH:(kch + 1) * CH], None) for kch in range(FCH)],
                  reads=("Wqs",) + xareads, writes=(gkey(och % 2),))
            P.ts("dve", qb[:, och * CH:(och + 1) * CH], g[:, 0:CH],
                 biast["bq"][:, och:och + 1], ALU.add,
                 reads=(gkey(och % 2), "bq"), writes=(f"qb.{och}",))
        for och in range(FCH):
            g = G[och % 2]
            P.mmg([(g[:, 0:CH], wsl(Wks, kch, och),
                    xab[:, kch * CH:(kch + 1) * CH], None) for kch in range(FCH)],
                  reads=("Wks",) + xareads, writes=(gkey(och % 2),))
            P.ts("dve", zb2[:], g[:, 0:CH], biast["bk"][:, och:och + 1], ALU.add,
                 reads=(gkey(och % 2), "bk"), writes=("zb2",))
            P.ts("dve", zb[:], zb2[:], 0.0, ALU.min, reads=("zb2",), writes=("zb",))
            P.act(zb[:], zb[:], AF.Exp, reads=("zb",), writes=("zb",))
            P.stt("dve", kb[:, och * CH:(och + 1) * CH], zb2[:], 0.0, zb[:],
                  ALU.max, ALU.add, reads=("zb2", "zb"), writes=(f"kb.{och}",))
        for och in range(FCH):
            g = G[och % 2]
            P.mmg([(g[:, 0:CH], wsl(Wvs, kch, och),
                    xab[:, kch * CH:(kch + 1) * CH], None) for kch in range(FCH)],
                  reads=("Wvs",) + xareads, writes=(gkey(och % 2),))
            P.ts("dve", vb[:, och * CH:(och + 1) * CH], g[:, 0:CH],
                 biast["bv"][:, och:och + 1], ALU.add,
                 reads=(gkey(och % 2), "bv"), writes=(f"vb.{och}",))

        for f in range(FCH):
            sl = slice(f * CH, (f + 1) * CH)
            P.tt("dve", qtl[:, sl], qb[:, sl], qvec[:], ALU.mult,
                 reads=(f"qb.{f}", "qvec"), writes=(f"qtl.{f}",))
            P.tt("dve", ktl[:, sl], kb[:, sl], kvec[:], ALU.mult,
                 reads=(f"kb.{f}", "kvec"), writes=(f"ktl.{f}",))
        P.dma("act", dram["sp_qt"].rearrange("(ff p) t -> p ff t", p=128)[:, :, col0:col0 + CH],
              qtl.ap().rearrange("p (ff t) -> p ff t", t=CH),
              reads=tuple(f"qtl.{f}" for f in range(FCH)), writes=("d_qt",))

        for jt in range(4):
            for half in range(2):
                g = G[2 + half]
                gk = gkey(2 + half)
                parts = []
                rd = []
                for dq in range(4):
                    dch = half * 4 + dq
                    parts.append((g[:, dq * 128:(dq + 1) * 128],
                                  vb[:, dch * CH + jt * 128: dch * CH + jt * 128 + 128],
                                  ident[:], None))
                    rd.append(f"vb.{dch}")
                P.mmg(parts, reads=tuple(rd) + ("identb",), writes=(gk,))
                P.cp("dve", VT[:, jt * D + half * 512: jt * D + half * 512 + 512],
                     g[:, 0:512], reads=(gk,), writes=(f"VT.{jt}.{half}",))
        for jt in range(4):
            for half in range(2):
                g = G[2 + half]
                gk = gkey(2 + half)
                parts = []
                rd = []
                for dq in range(4):
                    dch = half * 4 + dq
                    parts.append((g[:, dq * 128:(dq + 1) * 128],
                                  ktl[:, dch * CH + jt * 128: dch * CH + jt * 128 + 128],
                                  ident[:], None))
                    rd.append(f"ktl.{dch}")
                P.mmg(parts, reads=tuple(rd) + ("identb",), writes=(gk,))
                P.cp("dve", KT[:, jt * D + half * 512: jt * D + half * 512 + 512],
                     g[:, 0:512], reads=(gk,), writes=(f"KT.{jt}.{half}",))

        def sec_out(c=c, col0=col0):
            for qt in range(4):
                g = G[qt % 2]
                P.mmg([(g[:, 0:CH], qb[:, kch * CH + qt * 128: kch * CH + qt * 128 + 128],
                        kb[:, kch * CH:(kch + 1) * CH], None) for kch in range(FCH)],
                      reads=tuple(f"qb.{f}" for f in range(FCH)) +
                            tuple(f"kb.{f}" for f in range(FCH)),
                      writes=(gkey(qt % 2),))
                P.tt("dve", Pb[:, qt * CH:(qt + 1) * CH], g[:, 0:CH],
                     dmask[:, qt * CH:(qt + 1) * CH], ALU.mult,
                     reads=(gkey(qt % 2), "dmask"), writes=(f"Pb.{qt}",))
            for mt in range(4):
                g = G[2 + mt % 2]
                gk = gkey(2 + mt % 2)
                parts = []
                for qt in range(4):
                    parts.append((g[:, qt * 128:(qt + 1) * 128],
                                  Pb[:, qt * CH + mt * 128: qt * CH + mt * 128 + 128],
                                  ident[:], None))
                P.mmg(parts, reads=tuple(f"Pb.{qt}" for qt in range(4)) + ("identb",),
                      writes=(gk,))
                P.cp("dve", PT[:, mt * CH:(mt + 1) * CH], g[:, 0:512],
                     reads=(gk,), writes=(f"PT.{mt}",))

            for ech in range(FCH):
                gps = G[ech % 2]
                gk = gkey(ech % 2)
                parts = []
                rd = []
                if c > 0:
                    for dch in range(FCH):
                        parts.append((gps[:, 0:CH],
                                      Mbf[:, dch * D + ech * 128: dch * D + ech * 128 + 128],
                                      qtl[:, dch * CH:(dch + 1) * CH], None))
                        rd += ["Mbf", f"qtl.{dch}"]
                for mt in range(4):
                    parts.append((gps[:, 0:CH],
                                  VT[:, mt * D + ech * 128: mt * D + ech * 128 + 128],
                                  PT[:, mt * CH:(mt + 1) * CH], None))
                    rd += [f"VT.{mt}.0", f"VT.{mt}.1", f"PT.{mt}"]
                P.mmg(parts, reads=tuple(set(rd)), writes=(gk,))
                ok_ = f"xcb{ech % 2}"
                P.act(xcb[ech % 2][:], gps[:, 0:CH], AF.Copy, reads=(gk,),
                      writes=(ok_,))
                P.dma("sp", drr("sp_outT", ech)[:, col0:col0 + CH], xcb[ech % 2][:],
                      reads=(ok_,), writes=("d_outT",))

        def sec_state(c=c):
            for dch in range(FCH):
                for half in range(2):
                    bkey = "BIGa" if half == 0 else "BIGb"
                    bap = BA if half == 0 else BB
                    parts = []
                    rd = []
                    for jt in range(4):
                        parts.append((bap,
                                      KT[:, jt * D + dch * 128: jt * D + dch * 128 + 128],
                                      VT[:, jt * D + half * 512: jt * D + half * 512 + 512],
                                      None))
                        rd += [f"KT.{jt}.{dch // 4}", f"VT.{jt}.{half}"]
                    P.mmg(parts, reads=tuple(set(rd)), writes=(bkey,))
                    msl = slice(dch * D + half * 512, dch * D + half * 512 + 512)
                    if c == 0:
                        P.cp("dve", Mst[:, msl], bap, reads=(bkey,),
                             writes=(f"Mst.{dch}.{half}",))
                    else:
                        P.stt("dve", Mst[:, msl], Mst[:, msl], dCt[:, 0:1], bap,
                              ALU.mult, ALU.add,
                              reads=(f"Mst.{dch}.{half}", "dC", bkey),
                              writes=(f"Mst.{dch}.{half}",))

        if c < NCH - 1:
            sec_out()
            sec_state()
            for dch in range(FCH):
                P.act(Mbf[:, dch * D:(dch + 1) * D], Mst[:, dch * D:(dch + 1) * D],
                      AF.Copy, reads=(f"Mst.{dch}.0", f"Mst.{dch}.1"),
                      writes=("Mbf",))
        else:
            sec_state()
            for dch in range(FCH):
                P.act(Wvs[:, dch * D:(dch + 1) * D],
                      Mst[:, dch * D:(dch + 1) * D],
                      AF.Copy, reads=(f"Mst.{dch}.0", f"Mst.{dch}.1"),
                      writes=("Wvs", "Mb2"))
            P.dma("sp", dram["a_in"].rearrange("(dd p) e -> p dd e", p=128),
                  Wvs.ap().rearrange("p (dd e) -> p dd e", e=D),
                  reads=("Mb2",), writes=("d_a_in",))
            P.coll("AllGather", ALU.bypass, [[0, 1, 2, 3], [4, 5, 6, 7]],
                   [dram["a_in"]], [dram["a_out"]], sem="cc1",
                   reads=("d_a_in",), writes=("d_a_out",), fence_sem="ccf1")
            sec_out()


    if "A" in dbg:
        da = dout("dbg_A", [D, D])
        P.dma("sp", da.rearrange("(dd p) e -> p dd e", p=128),
              Mst.ap().rearrange("p (dd e) -> p dd e", e=D),
              reads=tuple(f"Mst.{d_}.{h_}" for d_ in range(FCH) for h_ in range(2)),
              writes=("d_dbgA",))
    if "outT" in dbg:
        do_ = dout("dbg_outT", [D, TPC], BF16)
        P.dma("sp", do_.rearrange("(ff p) t -> p ff t", p=128),
              dram["sp_outT"].rearrange("(ff p) t -> p ff t", p=128),
              reads=("d_outT",), writes=("d_dbgoT",))

    if upto == "attn1":
        P.barrier()
        P.emit()
        stA.close()
        es.close()
        return nc, dram

    dint("d_minit", [D, D], BF16)
    dint("d_xsb", [D, TW], BF16)
    P.barrier()
    stA.close()
    stDer.close()

    warena = ExitStack()
    wqkvb = warena.enter_context(nc.sbuf_tensor("wqkvb", [128, FCH * 3 * INNER], BF16))
    wob = warena.enter_context(nc.sbuf_tensor("wob", [128, 4 * D], BF16))

    stMI = ExitStack()
    minitb = stMI.enter_context(nc.sbuf_tensor("minitb", [128, FCH * D], BF16))
    st2p = ExitStack()
    slotb = st2p.enter_context(nc.sbuf_tensor("slotb", [128, 4 * D], F32))
    macc = st2p.enter_context(nc.sbuf_tensor("macc", [128, D], F32))
    for dch in range(FCH):
        for sl_ in range(4):
            P.dma("pool", slotb[:, sl_ * D:(sl_ + 1) * D],
                  dram["a_out"][sl_ * D + dch * 128: sl_ * D + dch * 128 + 128, :],
                  reads=("d_a_out",), writes=(f"slot.{sl_}",))
        P.ts("dve", macc[:], slotb[:, 0:D], coefs[:, 0:1], ALU.mult,
             reads=("slot.0", "coefs"), writes=("macc",))
        for sl_ in range(1, 3):
            P.stt("dve", macc[:], slotb[:, sl_ * D:(sl_ + 1) * D],
                  coefs[:, sl_:sl_ + 1], macc[:], ALU.mult, ALU.add,
                  reads=(f"slot.{sl_}", "coefs", "macc"), writes=("macc",))
        P.stt("dve", minitb[:, dch * D:(dch + 1) * D], slotb[:, 3 * D:4 * D],
              coefs[:, 3:4], macc[:],
              ALU.mult, ALU.add, reads=("slot.3", "coefs", "macc"),
              writes=("minitb",))
    P.dma("pool", wqkvb[:], dram["wqkv0"].rearrange("(c p) o -> p c o", p=128),
          writes=("wqkvb",))
    P.dma("pool", wob[:], dram["wo0"].rearrange("(c p) o -> p c o", p=128),
          writes=("wob",))

    if "minit" in dbg:
        dmn = dout("dbg_minit", [D, D], BF16)
        P.dma("sp", dmn.rearrange("(dd p) e -> p dd e", p=128),
              minitb.ap().rearrange("p (dd e) -> p dd e", e=D),
              reads=("minitb",), writes=("dbmn",))
    st3 = ExitStack()
    s3 = lambda nm, shp, dt=F32: st3.enter_context(nc.sbuf_tensor(nm, shp, dt))
    qhb = [s3(f"qhb{i}", [128, FCH * 512], BF16) for i in range(2)]
    oob2 = s3("oob2", [128, FCH * 512], BF16)
    oob = s3("oob", [128, FCH * 512], BF16)
    Wos = s3("Wos", [128, FCH * D], BF16)
    aof = [s3(f"aof{i}", [128, FCH * 512], BF16) for i in range(2)]
    sqb3 = [s3(f"sqb3{i}", [128, FCH * 512], BF16) for i in range(2)]
    x1f = s3("x1f", [128, FCH * 512], F32)
    xff = s3("xff", [128, FCH * 512], F32)
    xfb = s3("xfb", [128, 512], BF16)
    x1b = s3("x1b", [128, 512], BF16)
    sq1 = s3("sq1", [128, 512], BF16)
    xres = [s3(f"xres{i}", [128, 512], F32) for i in range(2)]
    zb3 = s3("zb3", [128, 512], F32)
    zb3b = s3("zb3b", [128, 512], F32)
    zcol = s3("zcol", [128, 1], BF16)
    P.memset("dve", zcol[:], 0.0, writes=("zcol",))
    P.dma("pool", Wos[:], dram["Wo"].rearrange("(c p) o -> p c o", p=128),
          writes=("Wos",))

    def stageX(tt):
        j = tt % 2
        col0 = tt * 512
        q_, qk_ = qhb[j], f"qhb{j}"
        P.dma("sp", q_.ap().rearrange("p (ff t) -> p ff t", t=512),
              dram["sp_qt"].rearrange("(ff p) t -> p ff t", p=128)[:, :, col0:col0 + 512],
              reads=("d_qt",), writes=(qk_,))
        P.ts("dve", q_[:], q_[:], chs[:, tt:tt + 1], ALU.mult,
             reads=(qk_, "chs"), writes=(qk_,))
        for ech in range(FCH):
            gk = f"G{ech % 2}"
            g = G[ech % 2]
            P.mmg([(g[:, 0:512],
                    minitb[:, dch * D + ech * 128: dch * D + ech * 128 + 128],
                    q_[:, dch * 512:(dch + 1) * 512], None) for dch in range(FCH)],
                  reads=("minitb", qk_), writes=(gk,))
            sl = slice(ech * 512, (ech + 1) * 512)
            P.dma("act", oob2[:, sl], drr("sp_outT", ech)[:, col0:col0 + 512],
                  reads=("d_outT",), writes=(f"oo2.{ech}",))
            P.tt("dve", oob[:, sl], oob2[:, sl], g[:, 0:512], ALU.add,
                 reads=(f"oo2.{ech}", gk), writes=(f"oob.{ech}",))
        for och in range(FCH):
            hk = f"H{och % 2}"
            h = H[och % 2]
            P.mmg([(h[:, 0:512], Wos[:, ech * D + och * 128: ech * D + och * 128 + 128],
                    oob[:, ech * 512:(ech + 1) * 512], None) for ech in range(FCH)],
                  reads=("Wos",) + tuple(f"oob.{e_}" for e_ in range(FCH)),
                  writes=(hk,))
            sl = slice(och * 512, (och + 1) * 512)
            P.ts("dve", aof[j][:, sl], h[:, 0:512], biast["bo"][:, och:och + 1],
                 ALU.add, reads=(hk, "bo"), writes=(f"aof{j}.{och}",))
            P.tt("dve", sqb3[j][:, sl], aof[j][:, sl], aof[j][:, sl], ALU.mult,
                 reads=(f"aof{j}.{och}",), writes=(f"sqb3{j}.{och}",))

    def stageY(tt):
        j = tt % 2
        col0 = tt * 512
        for och in range(FCH):
            sl = slice(och * 512, (och + 1) * 512)
            P.mm1(BIG[0:1, 0:512], onesb[:, 0:1], aof[j][:, sl],
                  start=(och == 0), stop=(och == FCH - 1),
                  reads=(f"aof{j}.{och}", "onesb"), writes=("BIGa",))
            P.mm1(BIG[0:1, 512:1024], onesb[:, 0:1], sqb3[j][:, sl],
                  start=(och == 0), stop=(och == FCH - 1),
                  reads=(f"sqb3{j}.{och}", "onesb"), writes=("BIGb",))
        ra, rb, rak, rbk = ln_stats_epi()
        for f in range(FCH):
            sl = slice(f * 512, (f + 1) * 512)
            xk = f"xres{f % 2}"
            P.dma("sp", xres[f % 2][:], drr("xT", f)[:, col0:col0 + 512],
                  writes=(xk,))
            zeng = "pool" if f % 3 == 0 else "dve"
            z, zk = (zb3, "zb3") if zeng == "pool" else (zb3b, "zb3b")
            P.tt(zeng, z[:], aof[j][:, sl], ra, ALU.mult,
                 reads=(f"aof{j}.{f}", rak), writes=(zk,))
            P.tt(zeng, z[:], z[:], rb, ALU.subtract,
                 reads=(zk, rbk), writes=(zk,))
            P.tt(zeng, x1f[:, sl], xres[f % 2][:], z[:], ALU.add,
                 reads=(xk, zk), writes=(f"x1f.{f}",))
            P.dma("pool", drr("sp_x1", f)[:, col0:col0 + 512], x1f[:, sl],
                  reads=(f"x1f.{f}",), writes=("d_x1",))
        for f in range(FCH):
            sl = slice(f * 512, (f + 1) * 512)
            P.act(x1b[:], x1f[:, sl], AF.Copy, reads=(f"x1f.{f}",),
                  writes=("x1b",))
            P.tt("dve", sq1[:], x1f[:, sl], x1f[:, sl], ALU.mult,
                 reads=(f"x1f.{f}",), writes=("sq1",))
            P.mm1(BIG[0:1, 0:512], onesb[:, 0:1], x1b[:],
                  start=(f == 0), stop=(f == FCH - 1),
                  reads=("x1b", "onesb"), writes=("BIGa",))
            P.mm1(BIG[0:1, 512:1024], onesb[:, 0:1], sq1[:],
                  start=(f == 0), stop=(f == FCH - 1),
                  reads=("sq1", "onesb"), writes=("BIGb",))
        ra, rb, rak, rbk = ln_stats_epi()
        for f in range(FCH):
            sl = slice(f * 512, (f + 1) * 512)
            zeng = "pool" if f % 3 == 0 else "dve"
            z, zk = (zb3, "zb3") if zeng == "pool" else (zb3b, "zb3b")
            P.tt(zeng, z[:], x1f[:, sl], ra, ALU.mult,
                 reads=(f"x1f.{f}", rak), writes=(zk,))
            P.tt(zeng, xff[:, sl], z[:], rb, ALU.subtract,
                 reads=(zk, rbk), writes=(f"xff.{f}",))
            P.dma("act", drr("sp_xin0", f)[:, col0:col0 + 512], xff[:, sl],
                  reads=(f"xff.{f}",), writes=("d_xin0",))
            P.act(xfb[:], xff[:, sl], AF.Copy, reads=(f"xff.{f}",),
                  writes=("xfb",))
            P.dma("sp", dram["d_xsb"][f * 128:(f + 1) * 128,
                                      QCOL0 + col0:QCOL0 + col0 + 512],
                  xfb[:], reads=("xfb",), writes=("d_xsb_own",))

    for tt in range(NCH + 1):
        if tt < NCH:
            stageX(tt)
        if tt >= 1:
            stageY(tt - 1)

    if "x1" in dbg:
        dx = dout("dbg_x1", [D, TPC])
        P.dma("sp", dx, dram["sp_x1"], reads=("d_x1",), writes=("d_dbg1",))
        dx2 = dout("dbg_xf", [D, TPC])
        P.dma("sp", dx2, dram["sp_xin0"], reads=("d_xin0",), writes=("d_dbg2",))

    for f in range(FCH):
        P.dma("sp", dram["d_xsb"][f * 128:(f + 1) * 128, 0:1], zcol[:],
              reads=("zcol",), writes=("d_xsb_halo",), slow=True)
    P.dma("pool", dram["h2_in"], dram["sp_xin0"][:, TPC - HALO:TPC],
          reads=("d_xin0",), writes=("d_h2in",))
    P.barrier()
    P.coll("AllGather", ALU.bypass, [[0, 1, 2, 3], [4, 5, 6, 7]],
           [dram["h2_in"]], [dram["h2_out"]], sem="cc2",
           reads=("d_h2in",), writes=("d_h2out",))
    st3.close()
    st2p.close()
    stMI.close()

    stXB = ExitStack()

    def halo_fill(hout_key, hout_ap, stHF, pfx):
        hsl = stHF.enter_context(nc.sbuf_tensor("hsl" + pfx, [128, 4 * HALO], F32))
        halof = stHF.enter_context(nc.sbuf_tensor("hf" + pfx, [128, HALO], F32))
        halob = stHF.enter_context(nc.sbuf_tensor("hb" + pfx, [128, HALO], BF16))
        for f in range(FCH):
            for sl_ in range(4):
                P.dma("pool", hsl[:, sl_ * HALO:(sl_ + 1) * HALO],
                      hout_ap[sl_ * D + f * 128: sl_ * D + f * 128 + 128, :],
                      reads=(hout_key,), writes=(f"hsl.{sl_}",))
            P.ts("dve", halof[:], hsl[:, 0:HALO], hselt[:, 0:1], ALU.mult,
                 reads=("hsl.0", "hsel"), writes=("halof",))
            for sl_ in range(1, 4):
                P.stt("dve", halof[:], hsl[:, sl_ * HALO:(sl_ + 1) * HALO],
                      hselt[:, sl_:sl_ + 1], halof[:], ALU.mult, ALU.add,
                      reads=(f"hsl.{sl_}", "hsel", "halof"), writes=("halof",))
            P.act(halob[:], halof[:], AF.Copy, reads=("halof",), writes=("halob",))
            P.dma("sp", dram["d_xsb"][f * 128:(f + 1) * 128, 1:1 + HALO],
                  halob[:], reads=("halob",), writes=("d_xsb_halo",))
            P.cp("dve", xh1[:, f:f + 1], halof[:, HALO - 1:HALO],
                 reads=("halof",), writes=("xh1",))

    def emit_layer(l, xs_in_d, xs_out_d, write_next_xsb, hkey, hout):
        LQ = 2561
        stH1 = ExitStack()
        h2b = stH1.enter_context(nc.sbuf_tensor("h2b" + str(l), [128, FCH * TPC],
                                                BF16, side="right"))
        stL1 = ExitStack()
        sL1 = lambda nm, shp, dt=BF16: stL1.enter_context(
            nc.sbuf_tensor(nm + str(l), shp, dt, side="right"))
        oTb = sL1("oTb", [128, 4 * TPC])
        xmx = sL1("xmx", [128, FCH], F32)
        stL2 = ExitStack()
        sL2 = lambda nm, shp, dt=BF16: stL2.enter_context(nc.sbuf_tensor(nm + str(l), shp, dt))
        qrb = sL2("qrb", [128, 4 * LQ])
        krb = sL2("krb", [128, 4 * LQ])
        stVR = ExitStack()
        vrb = stVR.enter_context(nc.sbuf_tensor("vrb" + str(l), [128, 4 * LQ], BF16))

        stq = ExitStack()
        sQ = lambda nm, shp, dt=F32: stq.enter_context(nc.sbuf_tensor(nm + str(l), shp, dt))
        xst = [sQ(f"xst{i}", [128, FCH * 513], BF16) for i in range(2)]
        qraw = sQ("qraw", [128, 2 * 512], F32)
        sqcs = sQ("sqcs", [128, 2 * 512], BF16)
        invs = sQ("invs", [128, 512], F32)
        tblb = [sQ(f"tblb{i}", [128, 2 * 512], F32) for i in range(2)]
        rt0 = sQ("rt0", [128, 512], F32)
        rt1 = sQ("rt1", [128, 512], F32)

        TILES = [(1027, 512), (1539, 512), (2051, 512),
                 (515, 512), (3, 512), (2, 1)]
        sides = [("q", 0, "csq", "ssq", "qsl", "qsh"),
                 ("k", 4, "csk", "ssk", "ksl", "ksh")]
        for ti, (c0, w) in enumerate(TILES):
            if ti == 3:
                halo_fill(hkey, hout, stq, f"l{l}")
            if c0 >= 1027:
                xrd = ("d_xsb_own",)
            elif c0 == 515:
                xrd = ("d_xsb_own", "d_xsb_halo")
            else:
                xrd = ("d_xsb_halo",)
            nb = ti % len(xst)
            xk = f"xst{nb}"
            P.dma("sp", xst[nb].ap().rearrange("p (ff t) -> p ff t", t=513)[:, :, 0:w + 1],
                  dram["d_xsb"].rearrange("(ff p) t -> p ff t", p=128)[:, :, c0 - 1:c0 + w],
                  reads=xrd, writes=(xk,))

            def rhs(kch, sh, nb=nb):
                base = kch * 513
                return xst[nb][:, base + 1 - sh: base + 1 - sh + w]

            for sname, obase, csn, ssn, sln, shn in sides:
                for grp in range(2):
                    for hi in range(2):
                        och = obase + 2 * grp + hi
                        g = G[hi]
                        gk = f"G{hi}"
                        P.mmg([(g[:, 0:w],
                                wqkvb[:, kch * 3 * INNER + och * 128: kch * 3 * INNER + och * 128 + 128],
                                rhs(kch, 1 if kch >= 4 else 0), None)
                               for kch in range(FCH)],
                              reads=("wqkvb", xk), writes=(gk,))
                        P.act(qraw[:, hi * 512:hi * 512 + w], g[:, 0:w], AF.Copy,
                              reads=(gk,), writes=(f"qraw.{hi}",))
                        P.tt("dve", sqcs[:, hi * 512:hi * 512 + w],
                             qraw[:, hi * 512:hi * 512 + w],
                             qraw[:, hi * 512:hi * 512 + w], ALU.mult,
                             reads=(f"qraw.{hi}",), writes=(f"sqcs.{hi}",))
                    P.mm1(H[0][:, 0:w], hm2[:, :], sqcs[:, 0:w], start=True,
                          stop=False, reads=("sqcs.0", "hm2b"), writes=("H0",))
                    P.mm1(H[0][:, 0:w], hm2[:, :], sqcs[:, 512:512 + w],
                          start=False, stop=True, reads=("sqcs.1", "hm2b"),
                          writes=("H0",))
                    P.ts("dve", invs[:, 0:w], H[0][:, 0:w], 1e-24, ALU.max,
                         reads=("H0",), writes=("invs",))
                    P.act(invs[:, 0:w], invs[:, 0:w], AF.Ln, reads=("invs",),
                          writes=("invs",))
                    P.act(invs[:, 0:w], invs[:, 0:w], AF.Exp, scale=-0.5,
                          reads=("invs",), writes=("invs",))
                    P.ts("dve", qraw[:, 0:w], qraw[:, 0:w],
                         qst[sln][:, l:l + 1], ALU.mult,
                         reads=("qraw.0", sln), writes=("qraw.0",))
                    P.ts("dve", qraw[:, 512:512 + w], qraw[:, 512:512 + w],
                         qst[shn][:, l:l + 1], ALU.mult,
                         reads=("qraw.1", shn), writes=("qraw.1",))
                    P.tt("dve", qraw[:, 0:w], qraw[:, 0:w], invs[:, 0:w],
                         ALU.mult, reads=("qraw.0", "invs"), writes=("qraw.0",))
                    P.tt("dve", qraw[:, 512:512 + w], qraw[:, 512:512 + w],
                         invs[:, 0:w], ALU.mult, reads=("qraw.1", "invs"),
                         writes=("qraw.1",))
                    tk = f"tblb{(2 * ti + grp) % 2}"
                    tb = tblb[(2 * ti + grp) % 2]
                    P.dma("sp", tb[:, 0:w], dram[csn][:, c0 - 2:c0 - 2 + w],
                          writes=(tk,), slow=(w == 1))
                    P.dma("sp", tb[:, 512:512 + w], dram[ssn][:, c0 - 2:c0 - 2 + w],
                          reads=(tk,), writes=(tk,), slow=(w == 1))
                    dst = qrb if sname == "q" else krb
                    dk = ("qrb" if sname == "q" else "krb")
                    qc1 = (2 * grp) * LQ + (c0 - 2)
                    qc2 = (2 * grp + 1) * LQ + (c0 - 2)
                    P.tt("dve", rt0[:, 0:w], qraw[:, 0:w], tb[:, 0:w], ALU.mult,
                         reads=("qraw.0", tk), writes=("rt0",))
                    P.tt("dve", rt1[:, 0:w], qraw[:, 512:512 + w],
                         tb[:, 512:512 + w], ALU.mult,
                         reads=("qraw.1", tk), writes=("rt1",))
                    P.tt("dve", dst[:, qc1:qc1 + w], rt0[:, 0:w], rt1[:, 0:w],
                         ALU.subtract, reads=("rt0", "rt1"),
                         writes=(f"{dk}.{grp}.lo.{ti}",))
                    P.tt("dve", rt0[:, 0:w], qraw[:, 512:512 + w], tb[:, 0:w],
                         ALU.mult, reads=("qraw.1", tk), writes=("rt0",))
                    P.tt("dve", rt1[:, 0:w], qraw[:, 0:w], tb[:, 512:512 + w],
                         ALU.mult, reads=("qraw.0", tk), writes=("rt1",))
                    P.tt("dve", dst[:, qc2:qc2 + w], rt0[:, 0:w], rt1[:, 0:w],
                         ALU.add, reads=("rt0", "rt1"),
                         writes=(f"{dk}.{grp}.hi.{ti}",))
            for vi in range(4):
                och = 8 + vi
                g = G[2 + vi % 2]
                gk = f"G{2 + vi % 2}"
                P.mmg([(g[:, 0:w],
                        wqkvb[:, kch * 3 * INNER + och * 128: kch * 3 * INNER + och * 128 + 128],
                        rhs(kch, 1 if kch >= 4 else 0), None)
                       for kch in range(FCH)],
                      reads=("wqkvb", xk), writes=(gk,))
                P.act(vrb[:, vi * LQ + (c0 - 2): vi * LQ + (c0 - 2) + w],
                      g[:, 0:w], AF.Copy, reads=(gk,), writes=(f"vrb.{vi}.{ti}",))
        P.barrier()
        stq.close()
        stVT = ExitStack()
        VTb = stVT.enter_context(nc.sbuf_tensor("VTb" + str(l), [128, 20 * 512],
                                                BF16, side="right"))
        SVTb = stVT.enter_context(nc.sbuf_tensor("SVTb" + str(l), [128, 5 * 512],
                                                 BF16, side="right"))
        if l == 0:
            P.dma("pool", wqkvb[:], dram["wqkv1"].rearrange("(c p) o -> p c o", p=128),
                  writes=("wqkvb",))

        vkeys = tuple(f"vrb.{vi}.{ti}" for vi in range(4) for ti in range(6))
        if l == 0 and "vrb" in dbg:
            dv0 = dout("dbg_vrb0", [128, 4 * LQ], BF16)
            P.dma("sp", dv0, vrb[:], reads=vkeys, writes=("dbv0",))
        qkeys = lambda dk: tuple(f"{dk}.{g_}.{h_}.{ti}" for g_ in range(2)
                                 for h_ in ("lo", "hi") for ti in range(6))
        for k in range(20):
            g = G[k % 2]
            gk = f"G{k % 2}"
            P.mmg([(g[:, vi * 128:(vi + 1) * 128],
                    vrb[:, vi * LQ + 1 + 128 * k: vi * LQ + 1 + 128 * k + 128],
                    ident[:], None) for vi in range(4)],
                  reads=vkeys + ("identb",), writes=(gk,))
            P.cp("dve", VTb[:, k * 512:(k + 1) * 512], g[:, 0:512],
                 reads=(gk,), writes=(f"VTb.{k}",))
        for mtx in range(5):
            g = G[2 + mtx % 2]
            gk = f"G{2 + mtx % 2}"
            P.mmg([(g[:, vi * 128:(vi + 1) * 128],
                    vrb[:, vi * LQ + 128 * mtx: vi * LQ + 128 * mtx + 128],
                    ident[:], None) for vi in range(4)],
                  reads=vkeys + ("identb",), writes=(gk,))
            P.cp("dve", SVTb[:, mtx * 512:(mtx + 1) * 512], g[:, 0:512],
                 reads=(gk,), writes=(f"SVTb.{mtx}",))
        P.barrier()
        stVR.close()

        P.barrier()
        stAt = ExitStack()
        sA = lambda nm, shp, dt=BF16: stAt.enter_context(
            nc.sbuf_tensor(nm + str(l), shp, dt, side="right"))
        SsbP = [sA(f"SsbP{i}", [128, 1280], F32) for i in range(2)]
        maskv = sA("maskv", [128, 4 * 1280])
        maskc = sA("maskc", [128, 1280])
        masksp = sA("masksp", [128, 1280])
        P.dma("sp", maskc[:], dram["maskb"][4 * 128:5 * 128, :], writes=("maskc",))
        P.dma("sp", maskv.ap().rearrange("p (s m) -> p s m", m=1280),
              dram["maskb"].rearrange("(s p) m -> p s m", p=128)[:, 0:4, :],
              writes=("maskv",))
        P.dma("sp", masksp[:], dram["maskspec"], writes=("masksp",))
        PsbP = [sA(f"PsbP{i}", [128, 1280]) for i in range(4)]
        PTsbP = [sA(f"PTsbP{i}", [128, 1280]) for i in range(2)]
        dgbP = [sA(f"dgbP{i}", [128, 256]) for i in range(4)]
        sml = sA("sml", [128, 16], F32)
        oxb = sA("oxb", [128, 512])

        tasks = []
        for qt in [-1] + list(range(16)):
            for h in range(8):
                tasks.append((qt, h))
        NT = len(tasks)
        NG = NT // 2
        qk_r = qkeys("qrb") + qkeys("krb")
        scores2 = ARENA.ap().rearrange("p (s c) -> p s c", c=1024)[:, 0:2, 0:640]
        SCKEYS = ("SCA", "BKA", "SCB", "BKB")

        def tinfo(i):
            qt, h = tasks[i]
            sl_ = i % 4
            p = i % 2
            spec = (qt < 0)
            kwc = 0 if spec else 1 + 128 * qt
            qc = 385 if spec else 513 + 128 * qt
            if spec:
                mask, mkk = masksp[:], "masksp"
            elif qt < 4:
                mask, mkk = maskv[:, qt * 1280:(qt + 1) * 1280], "maskv"
            else:
                mask, mkk = maskc[:], "maskc"
            grp = h // 4
            hb = 32 * (h % 4)
            return dict(sl=sl_, p=p, spec=spec, h=h, qt=qt, qtidx=i // 8,
                        hb=hb, mask=mask, mkk=mkk,
                        qlo=qrb[hb:hb + 32, (2 * grp) * LQ + qc: (2 * grp) * LQ + qc + 128],
                        qhi=qrb[hb:hb + 32, (2 * grp + 1) * LQ + qc: (2 * grp + 1) * LQ + qc + 128],
                        klo=krb[hb:hb + 32, (2 * grp) * LQ + kwc: (2 * grp) * LQ + kwc + 640],
                        khi=krb[hb:hb + 32, (2 * grp + 1) * LQ + kwc: (2 * grp + 1) * LQ + kwc + 640])

        def sc_g(p):
            return ARENA[:, 1024 * p:1024 * p + 512], ("SCA", "SCB")[p]

        def sc_t(p):
            return ARENA[:, 1024 * p + 512:1024 * p + 640], ("BKA", "BKB")[p]

        def pt_g(p):
            return ARENA[:, 2048 + 512 * p:2560 + 512 * p], ("PTA", "PTB")[p]

        def pt_t(p):
            return ARENA[:, 1024 * p + 640:1024 * p + 768], ("BKA", "BKB")[p]

        def stage_A(g):
            ts_ = [tinfo(2 * g), tinfo(2 * g + 1)]
            parts = []
            for t in ts_:
                parts.append((sc_g(t["p"])[0], t["qlo"],
                              t["klo"][:, 0:512], True, False, (t["hb"], 0)))
            for t in ts_:
                parts.append((sc_g(t["p"])[0], t["qhi"],
                              t["khi"][:, 0:512], False, True, (t["hb"], 0)))
            for t in ts_:
                parts.append((sc_t(t["p"])[0], t["qlo"],
                              t["klo"][:, 512:640], True, False, (t["hb"], 0)))
            for t in ts_:
                parts.append((sc_t(t["p"])[0], t["qhi"],
                              t["khi"][:, 512:640], False, True, (t["hb"], 0)))
            P.mseq(parts, reads=qk_r, writes=SCKEYS)

        def stage_BC(g):
            j = g % 2
            ts_ = [tinfo(2 * g), tinfo(2 * g + 1)]
            assert ts_[0]["mkk"] == ts_[1]["mkk"]
            sk = f"SsbP{j}"
            nmk = f"nmP{j}"
            nmP = sml[:, 2 * j:2 * j + 2]
            sv = SsbP[j].ap().rearrange("p (s c) -> p s c", c=640)
            P.tt("dve", sv, scores2,
                 ts_[0]["mask"].rearrange("p (s c) -> p s c", c=640),
                 ALU.add, reads=SCKEYS + (ts_[0]["mkk"],), writes=(sk,))
            P.red(nmP, sv, ALU.max, reads=(sk,), writes=(nmk,))
            P.ts("dve", nmP, nmP, -1.0, ALU.mult, reads=(nmk,), writes=(nmk,))
            jj = g % 4
            for p, t in enumerate(ts_):
                sl_ = t["sl"]
                sm1 = sml[:, 4 + sl_:5 + sl_]
                P.act(PsbP[jj][:, 640 * p:640 * p + 640],
                      SsbP[j][:, 640 * p:640 * p + 640], AF.Exp,
                      bias=sml[:, 2 * j + p:2 * j + p + 1], accum_out=sm1,
                      reads=(sk, nmk), writes=(f"PsbP{jj}", f"sm.{sl_}"))
            j2 = (2 * g) % 4
            P.recip(sml[:, 8 + j2:10 + j2], sml[:, 4 + j2:6 + j2],
                    reads=(f"sm.{ts_[0]['sl']}", f"sm.{ts_[1]['sl']}"),
                    writes=(f"rc.{ts_[0]['sl']}", f"rc.{ts_[1]['sl']}"))
            for p, t in enumerate(ts_):
                sl_ = t["sl"]
                rc = sml[:, 8 + sl_:9 + sl_]
                P.ts("dve", dgbP[jj][:, 128 * p:128 * p + 128], ident[:], rc,
                     ALU.mult, reads=(f"rc.{sl_}", "identb"),
                     writes=(f"dgbP{jj}",))

        def stage_D(g):
            ts_ = [tinfo(2 * g), tinfo(2 * g + 1)]
            j = g % 2
            jj = g % 4
            parts = []
            wr_ = []
            for p, t in enumerate(ts_):
                wr_ += [pt_g(p)[1], pt_t(p)[1]]
                for mtp in range(4):
                    parts.append((pt_g(p)[0][:, mtp * 128:(mtp + 1) * 128],
                                  PsbP[jj][:, 640 * p + mtp * 128: 640 * p + mtp * 128 + 128],
                                  dgbP[jj][:, 128 * p:128 * p + 128],
                                  True, True, None))
                parts.append((pt_t(p)[0], PsbP[jj][:, 640 * p + 512:640 * p + 640],
                              dgbP[jj][:, 128 * p:128 * p + 128],
                              True, True, None))
            P.mseq(parts, reads=(f"PsbP{jj}", f"dgbP{jj}"), writes=tuple(wr_))
            ptk = f"PTP{j}"
            P.act(PTsbP[j].ap().rearrange("p (s c) -> p s c", c=640)[:, :, 0:512],
                  ARENA[:, 2048:3072].rearrange("p (s c) -> p s c", c=512),
                  AF.Copy, reads=("PTA", "PTB"), writes=(ptk,))
            P.act(PTsbP[j].ap().rearrange("p (s c) -> p s c", c=640)[:, :, 512:640],
                  ARENA.ap().rearrange("p (s c) -> p s c", c=1024)[:, 0:2, 640:768],
                  AF.Copy, reads=("BKA", "BKB"), writes=(ptk,))

        def stage_E(g):
            ts_ = [tinfo(2 * g), tinfo(2 * g + 1)]
            j = g % 2
            parts = []
            rd_ = [f"PTP{j}"]
            wr_ = []
            for mtp in range(5):
                for t in ts_:
                    h = t["h"]
                    p = t["p"]
                    rb_ = 64 * p
                    cb = 128 * (h // 2)
                    if t["spec"]:
                        vt = SVTb[:, mtp * 512 + 64 * h: mtp * 512 + 64 * h + 64]
                        rd_.append(f"SVTb.{mtp}")
                    else:
                        vt = VTb[:, (t["qt"] + mtp) * 512 + 64 * h:
                                 (t["qt"] + mtp) * 512 + 64 * h + 64]
                        rd_.append(f"VTb.{t['qt'] + mtp}")
                    wr_.append(f"H{p}")
                    parts.append((H[p][rb_:rb_ + 64, cb:cb + 128], vt,
                                  PTsbP[j][:, 640 * p + mtp * 128: 640 * p + mtp * 128 + 128],
                                  mtp == 0, mtp == 4, None))
            P.mseq(parts, reads=tuple(set(rd_)), writes=tuple(set(wr_)))
            if g % 4 == 3:
                if ts_[0]["spec"]:
                    P.cp("dve", oxb[0:64, :], H[0][0:64, 0:512],
                         reads=("H0",), writes=("oxb",))
                    P.cp("dve", oxb[64:128, :], H[1][64:128, 0:512],
                         reads=("H1",), writes=("oxb",))
                else:
                    qt = ts_[0]["qt"]
                    dst = oTb.ap().rearrange("p (c t) -> p c t", t=TPC)
                    P.cp("dve", dst[0:64, :, 128 * qt:128 * qt + 128],
                         H[0].rearrange("p (c t) -> p c t", t=128)[0:64],
                         reads=("H0",), writes=("oTb",))
                    P.cp("dve", dst[64:128, :, 128 * qt:128 * qt + 128],
                         H[1].rearrange("p (c t) -> p c t", t=128)[64:128],
                         reads=("H1",), writes=("oTb",))

        for g in range(NG + 4):
            if 3 <= g < NG + 3:
                stage_D(g - 3)
            if 4 <= g:
                stage_E(g - 4)
            if g < NG:
                stage_A(g)
                stage_BC(g)

        for och in range(FCH):
            P.mmg([(H[0][:, och:och + 1],
                    wob[:, ch * D + och * 128: ch * D + och * 128 + 128],
                    oxb[:, ch * 128 + 127: ch * 128 + 128], None)
                   for ch in range(4)],
                  reads=("wob", "oxb"), writes=("H0",))
        P.tt("dve", xmx[:], H[0][:, 0:FCH], xh1[:], ALU.add,
             reads=("H0", "xh1"), writes=("xmx",))
        P.ts("dve", xmx[:], xmx[:], validt[:, 0:1], ALU.mult,
             reads=("xmx", "validp"), writes=("xmx",))
        if l == 0 and "lyr" in dbg:
            dq = dout("dbg_qrb", [128, 4 * LQ], BF16)
            dk2 = dout("dbg_krb", [128, 4 * LQ], BF16)
            dv2 = dout("dbg_vrb", [128, 4 * LQ], BF16)
            do2 = dout("dbg_oTb", [128, 4 * TPC], BF16)
            P.dma("sp", dq, qrb[:], reads=qkeys("qrb"), writes=("dbq",))
            P.dma("sp", dk2, krb[:], reads=qkeys("krb"), writes=("dbk",))
            P.dma("sp", dv2, vrb[:], reads=vkeys, writes=("dbv",))
            P.dma("sp", do2, oTb[:], reads=("oTb",), writes=("dbo",))
        P.barrier()
        stAt.close()
        stVT.close()
        stL2.close()

        stH2 = ExitStack()
        sH = lambda nm, shp, dt=F32: stH2.enter_context(nc.sbuf_tensor(nm + str(l), shp, dt))
        xsm_sb = sH("xsm_sb", [128, FCH * 513])
        xsit = [sH(f"xsit{i}", [128, 512]) for i in range(4)]
        h2c = sH("h2c", [128, 512], BF16)
        sqh = sH("sqh", [128, 512], BF16)
        zb4 = sH("zb4", [128, 512])
        zb4b = sH("zb4b", [128, 512])
        for f in range(FCH):
            P.cp("dve", xsm_sb[:, f * 513:f * 513 + 1], xmx[:, f:f + 1],
                 reads=("xmx",), writes=(f"xsm.{f}",))

        def hslice(f):
            base = f * 513
            return xsm_sb[:, base + (1 if f < 4 else 0): base + (1 if f < 4 else 0) + 512]

        for tt in range(4):
            col0 = tt * 512
            for f in range(FCH):
                hk = f"H{f % 2}"
                h = H[f % 2]
                P.mmg([(h[:, 0:512],
                        wob[:, ch * D + f * 128: ch * D + f * 128 + 128],
                        oTb[:, ch * TPC + col0: ch * TPC + col0 + 512], None)
                       for ch in range(4)],
                      reads=("wob", "oTb"), writes=(hk,))
                xk = f"xsit{f % 4}"
                P.dma("sp", xsit[f % 4][:],
                      xs_in_d[f * 128:(f + 1) * 128, col0:col0 + 512],
                      writes=(xk,))
                mk = f"xsm.{f}"
                P.tt("dve", xsm_sb[:, f * 513 + 1: f * 513 + 513],
                     xsit[f % 4][:], h[:, 0:512], ALU.add,
                     reads=(xk, hk, mk), writes=(mk,))
                P.dma("act", dram["sp_xsm"][f * 128:(f + 1) * 128, 1 + col0:1 + col0 + 512],
                      xsm_sb[:, f * 513 + 1: f * 513 + 513],
                      reads=(mk,), writes=("d_xsm",))
                P.act(h2c[:], hslice(f), AF.Copy, reads=(mk,), writes=("h2c",))
                P.tt("dve", sqh[:], hslice(f), hslice(f), ALU.mult,
                     reads=(mk,), writes=("sqh",))
                P.mm1(G[2][0:1, 0:512], onesb[:, 0:1], h2c[:],
                      start=(f == 0), stop=(f == FCH - 1),
                      reads=("h2c", "onesb"), writes=("G2",))
                P.mm1(G[3][0:1, 0:512], onesb[:, 0:1], sqh[:],
                      start=(f == 0), stop=(f == FCH - 1),
                      reads=("sqh", "onesb"), writes=("G3",))
            ra, rb, rak, rbk = ln_stats_epi(G[2], G[3], "G2", "G3")
            for f in range(FCH):
                mk = f"xsm.{f}"
                zeng = "pool" if f % 3 == 0 else "dve"
                z, zk = (zb4, "zb4") if zeng == "pool" else (zb4b, "zb4b")
                P.tt(zeng, z[:], hslice(f), ra, ALU.mult,
                     reads=(mk, rak), writes=(zk,))
                P.tt(zeng, z[:], z[:], rb, ALU.subtract,
                     reads=(zk, rbk), writes=(zk,))
                P.ts(zeng, h2b[:, f * TPC + tt * 512: f * TPC + tt * 512 + 512],
                     z[:], lngt[:, l * FCH + f:l * FCH + f + 1], ALU.mult,
                     reads=(zk, "lng", "lnb"), writes=(f"h2b.{f}.{tt}",),
                     s2=lnbt[:, l * FCH + f:l * FCH + f + 1], op1=ALU.add)
                if tt < 3:
                    P.cp("dve", xsm_sb[:, f * 513:f * 513 + 1],
                         xsm_sb[:, f * 513 + 512:f * 513 + 513],
                         reads=(mk,), writes=(mk,))
        if l == 0 and "lyr" in dbg:
            dxm = dout("dbg_xsm", [D, 1 + TPC])
            P.dma("sp", dram["sp_xsm"].rearrange("(ff p) t -> p ff t", p=128)[:, :, 0:1],
                  xmx.ap().rearrange("p (ff o) -> p ff o", o=1),
                  reads=("xmx",), writes=("d_xsm",), slow=True)
            P.dma("sp", dxm, dram["sp_xsm"], reads=("d_xsm",), writes=("dbxm",))
        P.barrier()
        stH2.close()
        stL1.close()
        if l == 0:
            P.dma("pool", wob[:], dram["wo1"].rearrange("(c p) o -> p c o", p=128),
                  writes=("wob",))

        stW = ExitStack()
        sW = lambda nm, shp, dt=BF16: sW_st.enter_context(nc.sbuf_tensor(nm + str(l), shp, dt))
        sW_st = stW
        hgelu = stW.enter_context(nc.sbuf_tensor("hgelu" + str(l), [128, NBLK * TPC],
                                                 BF16, side="right"))
        hh = [sW(f"hh{i}", [128, 512], F32) for i in range(3)]
        glt = sW("glt", [128, 512])
        abt = sW("abt", [128, 512])
        w1b = [sW(f"w1b{i}", [128, FCH * 256]) for i in range(2)]
        w2b = [sW(f"w2b{i}", [128, NBLK * 128]) for i in range(2)]
        tf = sW("tf", [128, 512], F32)
        xsot = sW("xsot", [128, 512], F32)
        xsob = sW("xsob", [128, 512])

        h2keys = tuple(f"h2b.{f}.{tt}" for f in range(FCH) for tt in range(4))
        for i in range(NBLK):
            wk = f"w1b{i % 2}"
            P.dma("pool", w1b[i % 2].ap().rearrange("p (c o) -> p c o", o=256),
                  dram[f"w1_{l}"].rearrange("(c p) o -> p c o", p=128)[:, :, 256 * i:256 * i + 256],
                  writes=(wk,))
            for tt in range(4):
                ga = G[(2 * tt) % 4]
                gg = G[(2 * tt + 1) % 4]
                gak = f"G{(2 * tt) % 4}"
                ggk = f"G{(2 * tt + 1) % 4}"
                P.mmg([(ga[:, 0:512], w1b[i % 2][:, kch * 256: kch * 256 + 128],
                        h2b[:, kch * TPC + tt * 512: kch * TPC + tt * 512 + 512],
                        None) for kch in range(FCH)],
                      reads=(wk,) + h2keys, writes=(gak,))
                P.mmg([(gg[:, 0:512], w1b[i % 2][:, kch * 256 + 128: kch * 256 + 256],
                        h2b[:, kch * TPC + tt * 512: kch * TPC + tt * 512 + 512],
                        None) for kch in range(FCH)],
                      reads=(wk,) + h2keys, writes=(ggk,))
                P.act(glt[:], gg[:, 0:512], AF.Gelu,
                      bias=b1tt[:, l * 44 + 2 * i + 1:l * 44 + 2 * i + 2],
                      reads=(ggk, "b1t"), writes=("glt",))
                P.ts("dve", abt[:], ga[:, 0:512],
                     b1tt[:, l * 44 + 2 * i:l * 44 + 2 * i + 1], ALU.add,
                     reads=(gak, "b1t"), writes=("abt",))
                P.tt("dve", hgelu[:, i * TPC + tt * 512: i * TPC + tt * 512 + 512],
                     abt[:], glt[:], ALU.mult, reads=("abt", "glt"),
                     writes=(f"hg.{i}.{tt}",))
        hgkeys = tuple(f"hg.{i}.{tt}" for i in range(NBLK) for tt in range(4))
        for och in range(FCH):
            wk = f"w2b{och % 2}"
            P.dma("pool", w2b[och % 2].ap().rearrange("p (c o) -> p c o", o=128),
                  dram[f"w2_{l}"].rearrange("(c p) o -> p c o", p=128)[:, :, och * 128:(och + 1) * 128],
                  writes=(wk,))
            for tt in range(4):
                col0 = tt * 512
                hk = f"H{tt % 2}"
                h = H[tt % 2]
                P.mmg([(h[:, 0:512], w2b[och % 2][:, ki * 128:(ki + 1) * 128],
                        hgelu[:, ki * TPC + col0: ki * TPC + col0 + 512], None)
                       for ki in range(NBLK)],
                      reads=(wk,) + hgkeys, writes=(hk,))
                P.ts("dve", tf[:], h[:, 0:512],
                     b2tt[:, l * FCH + och:l * FCH + och + 1], ALU.add,
                     reads=(hk, "b2t"), writes=("tf",))
                xk = f"hh{(och * 4 + tt) % 3}"
                P.dma("sp", hh[(och * 4 + tt) % 3][:],
                      dram["sp_xsm"][och * 128:(och + 1) * 128, 1 + col0:1 + col0 + 512],
                      reads=("d_xsm",), writes=(xk,))
                P.tt("dve", xsot[:], tf[:], hh[(och * 4 + tt) % 3][:], ALU.add,
                     reads=("tf", xk), writes=("xsot",))
                P.dma("sp", xs_out_d[och * 128:(och + 1) * 128, col0:col0 + 512],
                      xsot[:], reads=("xsot",), writes=("d_xsout",))
                if write_next_xsb:
                    P.act(xsob[:], xsot[:], AF.Copy, reads=("xsot",),
                          writes=("xsob",))
                    P.dma("sp", dram["d_xsb"][och * 128:(och + 1) * 128,
                                              QCOL0 + col0:QCOL0 + col0 + 512],
                          xsob[:], reads=("xsob",), writes=("d_xsb_own",))
        P.barrier()
        stW.close()
        stH1.close()

    emit_layer(0, dram["sp_xin0"], dram["sp_xin1"], True,
               "d_h2out", dram["h2_out"])
    if upto == "layer1":
        if "xs1" in dbg:
            dx = dout("dbg_xs1", [D, TPC])
            P.dma("sp", dx, dram["sp_xin1"], reads=("d_xsout",), writes=("dbg3",))
        P.barrier()
        P.emit()
        stXB.close()
        warena.close()
        es.close()
        return nc, dram

    P.dma("pool", dram["h3_in"], dram["sp_xin1"][:, TPC - HALO:TPC],
          reads=("d_xsout",), writes=("d_h3in",))
    P.coll("AllGather", ALU.bypass, [[0, 1, 2, 3], [4, 5, 6, 7]],
           [dram["h3_in"]], [dram["h3_out"]], sem="cc3",
           reads=("d_h3in",), writes=("d_h3out",))

    emit_layer(1, dram["sp_xin1"], dram["sp_xs2"], False,
               "d_h3out", dram["h3_out"])

    stF = ExitStack()
    sF = lambda nm, shp, dt=F32: stF.enter_context(nc.sbuf_tensor(nm, shp, dt))
    xsf2 = [sF(f"xsf{i}", [128, FCH * 512]) for i in range(2)]
    x1t = [sF(f"x1t{i}", [128, 512]) for i in range(4)]
    xcb2 = [sF(f"xcb2{i}", [128, 512], BF16) for i in range(2)]
    sqf2 = [sF(f"sqf2{i}", [128, 512], BF16) for i in range(2)]
    zb5 = sF("zb5", [128, 512])
    zb5b = sF("zb5b", [128, 512])
    ytb = sF("ytb", [128, 512])
    yt = sF("yt", [128, 512])

    def load_xs(tt):
        j = tt % 2
        col0 = tt * 512
        for f in range(FCH):
            P.dma("act", xsf2[j][:, f * 512:(f + 1) * 512],
                  dram["sp_xs2"][f * 128:(f + 1) * 128, col0:col0 + 512],
                  reads=("d_xsout",), writes=(f"xsf{j}.{f}",))

    load_xs(0)
    for tt in range(4):
        col0 = tt * 512
        j = tt % 2
        xsf = xsf2[j]
        for f in range(FCH):
            sl = slice(f * 512, (f + 1) * 512)
            fk = f"xsf{j}.{f}"
            ck, sk = f"xcb2{f % 2}", f"sqf2{f % 2}"
            P.act(xcb2[f % 2][:], xsf[:, sl], AF.Copy, reads=(fk,),
                  writes=(ck,))
            P.tt("dve", sqf2[f % 2][:], xsf[:, sl], xsf[:, sl], ALU.mult,
                 reads=(fk,), writes=(sk,))
            P.mm1(BIG[0:1, 0:512], onesb[:, 0:1], xcb2[f % 2][:],
                  start=(f == 0), stop=(f == FCH - 1),
                  reads=(ck, "onesb"), writes=("BIGa",))
            P.mm1(BIG[0:1, 512:1024], onesb[:, 0:1], sqf2[f % 2][:],
                  start=(f == 0), stop=(f == FCH - 1),
                  reads=(sk, "onesb"), writes=("BIGb",))
        ra, rb, rak, rbk = ln_stats_epi()
        if tt < 3:
            load_xs(tt + 1)
        for f in range(FCH):
            sl = slice(f * 512, (f + 1) * 512)
            xk = f"x1t{f % 4}"
            P.dma("pool", x1t[f % 4][:], dram["sp_x1"][f * 128:(f + 1) * 128, col0:col0 + 512],
                  reads=("d_x1",), writes=(xk,))
            zeng = "pool" if f % 3 == 0 else "dve"
            z, zk = (zb5, "zb5") if zeng == "pool" else (zb5b, "zb5b")
            y_, yk = (yt, "yt") if zeng == "pool" else (ytb, "ytb")
            P.tt(zeng, z[:], xsf[:, sl], ra, ALU.mult,
                 reads=(f"xsf{j}.{f}", rak), writes=(zk,))
            P.tt(zeng, z[:], z[:], rb, ALU.subtract,
                 reads=(zk, rbk), writes=(zk,))
            P.tt(zeng, y_[:], x1t[f % 4][:], z[:], ALU.add,
                 reads=(xk, zk), writes=(yk,))
            P.dma("sp", dram["y"][f * 128:(f + 1) * 128, col0:col0 + 512],
                  y_[:], reads=(yk,), writes=("d_y",))
    stF.close()

    P.barrier()
    P.emit()
    stXB.close()
    warena.close()
    es.close()
    return nc, dram


_NC_CACHE = {}


def kernel(**inputs):
    in_maps = [host_inputs(inputs, c) for c in range(8)]
    if "all" not in _NC_CACHE:
        _NC_CACHE["all"] = build_nc()
    nc, dram = _NC_CACHE["all"]
    res = run_bass_kernel_spmd(nc, in_maps, core_ids=list(range(8)))
    out = np.empty((B, N, D), np.float32)
    for c in range(8):
        b, s = c // 4, c % 4
        out[b, s * TPC:(s + 1) * TPC, :] = res.results[c]["y"].T
    return out


# version 39
# speedup vs baseline: 1.0141x; 1.0089x over previous
import numpy as np
import ml_dtypes
from contextlib import ExitStack
import concourse.bass as bass
import concourse.mybir as mybir
from concourse.bass_utils import run_bass_kernel_spmd

F32 = mybir.dt.float32
BF16 = mybir.dt.bfloat16
AF = mybir.ActivationFunctionType
ALU = mybir.AluOpType
AX = mybir.AxisListType.X
BF = ml_dtypes.bfloat16

B, N, D = 2, 8192, 1024
FCH = 8
TPC = 2048
CH = 512
NCH = TPC // CH
HEADS, DH, WIN, INNER, HID = 8, 64, 512, 512, 2730
QKS = 8.0
HALO = 514
TW = 1 + HALO + TPC
QCOL0 = 515
NEG = -1e9
EPS = 1e-5
ATTN_PIPE = True
ATTN_STAGES = "ABCDE"
W1BLK = [128] * 21 + [42]
W1OFF = np.cumsum([0] + W1BLK).tolist()
NBLK = len(W1BLK)

ENGS = ("pe", "act", "dve", "pool", "sp")


class Prog:

    QR = 8

    def __init__(self, nc, es):
        names = ["pe", "act", "dve", "pool", "sp", "cc1", "cc2", "cc3"]
        for q in ("sp", "pool", "act"):
            names += [f"q_{q}{i}" for i in range(self.QR)]
        self.nc = nc
        self.qrot = {"sp": 0, "pool": 0, "act": 0}
        self.sems = {s: es.enter_context(nc.semaphore(s)) for s in names}
        self.cnt = {s: 0 for s in names}
        self.plan = {e: [] for e in ENGS}
        self.lastw = {e: {} for e in ENGS}
        self.wr = {}
        self.rd = {}

    def _filter(self, eng, deps):
        out = []
        for d in deps:
            if d is None:
                continue
            s, v = d
            if self.lastw[eng].get(s, 0) < v:
                out.append((s, v))
                self.lastw[eng][s] = v
        return out

    def _rw_deps(self, reads, writes):
        deps = []
        for k in reads:
            deps.extend(self.wr.get(k, ()))
        for k in writes:
            deps.extend(self.wr.get(k, ()))
            deps.extend(self.rd.get(k, ()))
        return deps

    def _rw_post(self, tok, reads, writes):
        for k in reads:
            self.rd.setdefault(k, []).append(tok)
        for k in writes:
            self.wr.setdefault(k, []).append(tok)

    def step(self, eng, fn, reads=(), writes=(), extra=(), sem="auto", amt=1):
        deps = self._rw_deps(reads, writes) + list(extra)
        waits = self._filter(eng, deps)
        s = eng if sem == "auto" else sem
        self.cnt[s] += amt
        tok = (s, self.cnt[s])
        self.plan[eng].append((waits, fn, [(s, amt)]))
        self._rw_post(tok, reads, writes)
        return tok

    def dma(self, q, out, in_, reads=(), writes=(), extra=(), slow=False):
        i = self.qrot[q]
        self.qrot[q] = (i + 1) % self.QR
        s = f"q_{q}{i}"
        extra = list(extra) + [(s, self.cnt[s])]
        return self.step(
            q, lambda e: e.dma_start(out=out, in_=in_,
                                     allow_slow_non_contiguous=slow),
            reads, writes, extra, sem=s, amt=16)

    def mmg(self, parts, reads=(), writes=(), extra=()):
        deps = self._rw_deps(reads, writes) + list(extra)
        waits = self._filter("pe", deps)
        n = len(parts)
        for i, (o, l, r, tp) in enumerate(parts):
            st, sp = (i == 0), (i == n - 1)
            def f(e, o=o, l=l, r=r, st=st, sp=sp, tp=tp):
                return e.matmul(o, l, r, start=st, stop=sp, tile_position=tp)
            if sp:
                self.cnt["pe"] += 1
                tok = ("pe", self.cnt["pe"])
                self.plan["pe"].append((waits if i == 0 else [], f, [("pe", 1)]))
            else:
                self.plan["pe"].append((waits if i == 0 else [], f, []))
        self._rw_post(tok, reads, writes)
        return tok

    def mm1(self, out, lhsT, rhs, start, stop, reads=(), writes=(), extra=(),
            tp=None):
        def f(e):
            return e.matmul(out, lhsT, rhs, start=start, stop=stop,
                            tile_position=tp)
        return self.step("pe", f, reads, writes, extra)

    def mseq(self, parts, reads=(), writes=(), extra=()):
        deps = self._rw_deps(reads, writes) + list(extra)
        waits = self._filter("pe", deps)
        self.cnt["pe"] += 1
        tok = ("pe", self.cnt["pe"])
        n = len(parts)
        for i, (o, l, r, st, sp, tp) in enumerate(parts):
            def f(e, o=o, l=l, r=r, st=st, sp=sp, tp=tp):
                return e.matmul(o, l, r, start=st, stop=sp, tile_position=tp)
            self.plan["pe"].append((waits if i == 0 else [], f,
                                    [("pe", 1)] if i == n - 1 else []))
        self._rw_post(tok, reads, writes)
        return tok

    def cp(self, eng, out, in_, reads=(), writes=(), extra=()):
        return self.step(eng, lambda e: e.tensor_copy(out, in_),
                         reads, writes, extra)

    def act(self, out, in_, func, reads=(), writes=(), extra=(), bias=0.0,
            scale=1.0, accum_out=None):
        def f(e):
            return e.activation(out, in_, func, bias=bias, scale=scale,
                                accum_out=accum_out)
        return self.step("act", f, reads, writes, extra)

    def tt(self, eng, out, in0, in1, op, reads=(), writes=(), extra=()):
        return self.step(eng, lambda e: e.tensor_tensor(out, in0, in1, op),
                         reads, writes, extra)

    def ts(self, eng, out, in0, s1, op0, reads=(), writes=(), extra=(),
           s2=None, op1=ALU.bypass):
        def f(e):
            return e.tensor_scalar(out, in0, s1, s2, op0, op1)
        return self.step(eng, f, reads, writes, extra)

    def stt(self, eng, out, in0, scalar, in1, op0, op1, reads=(), writes=(),
            extra=()):
        def f(e):
            return e.scalar_tensor_tensor(out, in0, scalar, in1, op0, op1)
        return self.step(eng, f, reads, writes, extra)

    def red(self, out, in_, op, reads=(), writes=(), extra=(), negate=False):
        def f(e):
            return e.tensor_reduce(out, in_, AX, op, negate=negate)
        return self.step("dve", f, reads, writes, extra)

    def ttr(self, out, in0, in1, scalar, accum_out, op0, op1, reads=(),
            writes=(), extra=(), scale=1.0):
        def f(e):
            return e.tensor_tensor_reduce(out=out, in0=in0, in1=in1,
                                          scale=scale, scalar=scalar,
                                          op0=op0, op1=op1,
                                          accum_out=accum_out)
        return self.step("dve", f, reads, writes, extra)

    def recip(self, out, in_, reads=(), writes=(), extra=()):
        return self.step("dve", lambda e: e.reciprocal(out, in_),
                         reads, writes, extra)

    def memset(self, eng, ap, val, reads=(), writes=(), extra=()):
        return self.step(eng, lambda e: e.memset(ap, val), reads, writes, extra)

    def coll(self, kind, op, groups, ins, outs, sem, reads=(), writes=()):
        def f(e):
            return e.collective_compute(kind, op, replica_groups=groups,
                                        ins=ins, outs=outs)
        self.step("pool", f, reads, writes, sem=sem)
        return self.step("pool", lambda e: e.engine_nop(), (), writes)

    def barrier(self):
        toks = [(s, self.cnt[s]) for s in self.cnt if not s.startswith("cc")]
        for e in ENGS:
            w = self._filter(e, toks)
            if w:
                self.plan[e].append((w, None, []))
        self.wr = {}
        self.rd = {}

    def emit(self):
        nc, sems = self.nc, self.sems

        def replay(name):
            def f(eng):
                for waits, fn, incs in self.plan[name]:
                    for s, v in waits:
                        eng.wait_ge(sems[s], v)
                    if fn is not None:
                        inst = fn(eng)
                        for s, a in incs:
                            inst.then_inc(sems[s], a)
            return f

        with nc.Block() as block:
            block.tensor(replay("pe"))
            block.scalar(replay("act"))
            block.vector(replay("dve"))
            block.gpsimd(replay("pool"))
            block.sync(replay("sp"))



def host_shared():
    t = {}
    t["identb"] = np.eye(128, dtype=BF)
    t["onesb"] = np.ones((128, 1), BF)
    t["onesf"] = np.ones((1, 128), np.float32)
    hm2 = np.zeros((128, 128), np.float32)
    for g in range(4):
        hm2[g * 32:(g + 1) * 32, g * 32:(g + 1) * 32] = 1.0
    t["hm2b"] = hm2.astype(BF)
    j = np.arange(CH, dtype=np.float32)
    t["qexp"] = np.broadcast_to(j, (128, CH)).copy()
    t["kexp"] = np.broadcast_to(CH - 1 - j, (128, CH)).copy()
    p = np.arange(128, dtype=np.float32)[:, None]
    ecl, d01 = [], []
    for qt in range(NCH):
        E = (128 * qt + p) - 1.0 - j[None, :]
        ecl.append(np.maximum(E, 0.0))
        d01.append((E >= 0).astype(np.float32))
    t["eclamp"] = np.concatenate(ecl, 1)
    t["d01"] = np.concatenate(d01, 1)
    t["chexp"] = np.broadcast_to(np.arange(NCH, dtype=np.float32) * CH,
                                 (128, NCH)).copy()
    return t


def host_percore(core):
    b, s = core // 4, core % 4
    t0 = s * TPC
    out = {}
    ce = np.zeros((128, 4), np.float32)
    cv = np.zeros((128, 4), np.float32)
    for cp in range(s):
        ce[:, cp] = TPC * (s - 1 - cp)
        cv[:, cp] = 1.0
    out["coefexp"], out["coefval"] = ce, cv
    hs = np.zeros((128, 4), np.float32)
    if s > 0:
        hs[:, s - 1] = 1.0
    out["hsel"] = hs
    out["validp"] = np.full((128, 1), 1.0 if s > 0 else 0.0, np.float32)
    toks = t0 - 513 + np.arange(2561)
    half = np.arange(0, DH, 2, dtype=np.float64)
    inv_freq = 1.0 / 10000.0 ** (half / DH)
    base = (half + 0.4 * DH) / (1.4 * DH)
    tt = toks.astype(np.float64)
    freqs = tt[:, None] * inv_freq[None, :]
    power = (tt - N // 2) / float(WIN // 2)
    scale = base[None, :] ** power[:, None]
    cos, sin = np.cos(freqs), np.sin(freqs)
    neg = toks < 0
    csq = cos * scale * QKS; ssq = sin * scale * QKS
    csk = cos / scale; ssk = sin / scale
    for a in (csq, ssq, csk, ssk):
        a[neg] = 0.0

    def tile(a):
        return np.tile(a.T.astype(np.float32), (4, 1))
    out["csq"], out["ssq"] = tile(csq), tile(ssq)
    out["csk"], out["ssk"] = tile(csk), tile(ssk)
    pp = np.arange(128)[:, None]
    mm = np.arange(640)[None, :]
    band = (mm >= pp) & (mm <= pp + 512)
    slots = []
    for qt in range(4):
        if s == 0:
            keyabs = t0 + 128 * qt - 512 + mm
            msk = band & (keyabs >= 0)
        else:
            msk = band
        slots.append(np.where(msk, 0.0, NEG))
    slots.append(np.where(band, 0.0, NEG))
    mb = np.concatenate(slots, 0)
    out["maskb"] = np.tile(mb, (1, 2)).astype(BF)
    sp = np.full((128, 640), NEG, np.float32)
    sp[127, 0:513] = 0.0
    out["maskspec"] = np.tile(sp, (1, 2)).astype(BF)
    return out


def host_inputs(inputs, core):
    b, s = core // 4, core % 4
    t0 = s * TPC
    g = lambda k: np.asarray(inputs[k])
    m = {}
    m["xT"] = np.ascontiguousarray(g("L0_attn_x")[b, t0:t0 + TPC, :].T)
    for w in ("Wq", "Wk", "Wv", "Wo"):
        m[w] = np.ascontiguousarray(g("L0_attn_" + w))
    for bias, nm in (("bq", "L0_attn_bq"), ("bk", "L0_attn_bk"),
                     ("bv", "L0_attn_bv"), ("bo", "L0_attn_bo")):
        m[bias] = np.ascontiguousarray(g(nm).reshape(FCH, 128).T)
    m["dparam"] = np.broadcast_to(
        g("L0_attn_decay_param").reshape(1, 1), (128, 1)).copy()
    perm = []
    for grp in range(2):
        for hi in range(2):
            for h in range(4 * grp, 4 * grp + 4):
                for d in range(32):
                    perm.append(64 * h + 32 * hi + d)
    perm = np.array(perm)
    wq = g("L0_ffn_wqkv")
    for l in range(2):
        w = wq[l]
        m[f"wqkv{l}"] = np.ascontiguousarray(
            np.concatenate([w[:, perm], w[:, 512 + perm], w[:, 1024:]], 1))
        m[f"wo{l}"] = np.ascontiguousarray(g("L0_ffn_wo")[l])
    m["lng"] = np.ascontiguousarray(
        g("L0_ffn_ln_g").reshape(2, FCH, 128).transpose(2, 1, 0).reshape(128, FCH * 2, order="F")) \
        if False else np.ascontiguousarray(
        np.concatenate([g("L0_ffn_ln_g")[l].reshape(FCH, 128).T for l in range(2)], 1))
    m["lnb"] = np.ascontiguousarray(
        np.concatenate([g("L0_ffn_ln_b")[l].reshape(FCH, 128).T for l in range(2)], 1))
    qs = g("L0_ffn_q_scale"); ks = g("L0_ffn_k_scale")
    m["qsl"] = np.ascontiguousarray(np.stack([np.tile(qs[l][0:32], 4) for l in range(2)], 1))
    m["qsh"] = np.ascontiguousarray(np.stack([np.tile(qs[l][32:64], 4) for l in range(2)], 1))
    m["ksl"] = np.ascontiguousarray(np.stack([np.tile(ks[l][0:32], 4) for l in range(2)], 1))
    m["ksh"] = np.ascontiguousarray(np.stack([np.tile(ks[l][32:64], 4) for l in range(2)], 1))
    w1 = g("L0_ffn_w1"); b1 = g("L0_ffn_b1"); w2 = g("L0_ffn_w2")
    HIDP = 2816
    b1tiles = []
    for l in range(2):
        w1p = np.zeros((D, 2 * HIDP), np.float32)
        w2p = np.zeros((HIDP, D), np.float32)
        bt = np.zeros((128, 44), np.float32)
        for i in range(NBLK):
            w_ = W1BLK[i]
            a0 = W1OFF[i]
            w1p[:, 256 * i: 256 * i + w_] = w1[l][:, a0:a0 + w_]
            w1p[:, 256 * i + 128: 256 * i + 128 + w_] = w1[l][:, HID + a0:HID + a0 + w_]
            w2p[128 * i: 128 * i + w_, :] = w2[l][a0:a0 + w_, :]
            bt[0:w_, 2 * i] = b1[l][a0:a0 + w_]
            bt[0:w_, 2 * i + 1] = b1[l][HID + a0:HID + a0 + w_]
        m[f"w1_{l}"] = w1p
        m[f"w2_{l}"] = w2p
        b1tiles.append(bt)
    m["b1t"] = np.ascontiguousarray(np.concatenate(b1tiles, 1))
    m["b2t"] = np.ascontiguousarray(
        np.concatenate([g("L0_ffn_b2")[l].reshape(FCH, 128).T for l in range(2)], 1))
    m.update(host_shared())
    m.update(host_percore(core))
    return m



HIDP = 2816


def build_nc(upto="all", dbg=()):
    nc = bass.Bass("TRN2", target_bir_lowering=False, debug=False)
    es = ExitStack()
    P = Prog(nc, es)
    dbg = set(dbg)
    dram = {}

    def din(name, shape, dtype=F32):
        dram[name] = nc.dram_tensor(name, shape, dtype, kind="ExternalInput").ap()
        return dram[name]

    def dout(name, shape, dtype=F32):
        dram[name] = nc.dram_tensor(name, shape, dtype, kind="ExternalOutput").ap()
        return dram[name]

    def dint(name, shape, dtype=F32):
        dram[name] = nc.dram_tensor(name, shape, dtype).ap()
        return dram[name]

    din("xT", [D, TPC])
    for w in ("Wq", "Wk", "Wv", "Wo"):
        din(w, [D, D])
    for bias in ("bq", "bk", "bv", "bo"):
        din(bias, [128, FCH])
    din("dparam", [128, 1])
    din("qexp", [128, CH]); din("kexp", [128, CH])
    din("eclamp", [128, NCH * CH]); din("d01", [128, NCH * CH])
    din("chexp", [128, NCH]); din("coefexp", [128, 4]); din("coefval", [128, 4])
    din("hsel", [128, 4]); din("validp", [128, 1])
    din("identb", [128, 128], BF16); din("onesb", [128, 1], BF16)
    din("onesf", [1, 128]); din("hm2b", [128, 128], BF16)
    din("wqkv0", [D, 3 * INNER]); din("wqkv1", [D, 3 * INNER])
    din("wo0", [INNER, D]); din("wo1", [INNER, D])
    din("lng", [128, FCH * 2]); din("lnb", [128, FCH * 2])
    din("qsl", [128, 2]); din("qsh", [128, 2]); din("ksl", [128, 2]); din("ksh", [128, 2])
    din("w1_0", [D, 2 * HIDP]); din("w1_1", [D, 2 * HIDP])
    din("b1t", [128, 44 * 2])
    din("w2_0", [HIDP, D]); din("w2_1", [HIDP, D])
    din("b2t", [128, FCH * 2])
    din("csq", [128, 2561]); din("ssq", [128, 2561])
    din("csk", [128, 2561]); din("ssk", [128, 2561])
    din("maskb", [5 * 128, 1280], BF16)
    din("maskspec", [128, 1280], BF16)
    dout("y", [D, TPC])

    dint("a_in", [D, D], BF16); dint("a_out", [4 * D, D], BF16)
    dint("h2_in", [D, HALO], BF16); dint("h2_out", [4 * D, HALO], BF16)
    dint("h3_in", [D, HALO], BF16); dint("h3_out", [4 * D, HALO], BF16)
    dint("sp_x1", [D, TPC])
    dint("sp_outT", [D, TPC], BF16)
    dint("sp_qt", [D, TPC], BF16)
    dint("sp_xin0", [D, TPC]); dint("sp_xin1", [D, TPC]); dint("sp_xs2", [D, TPC])
    dint("sp_xsm", [D, 1 + TPC])
    sp_xin = [dram["sp_xin0"], dram["sp_xin1"]]
    sp_xout = [dram["sp_xin1"], dram["sp_xs2"]]

    def drr(name, f):
        return dram[name][f * 128:(f + 1) * 128, :]

    sbl = lambda nm, shp, dt=F32: es.enter_context(nc.sbuf_tensor(nm, shp, dt))
    ident = sbl("ident", [128, 128], BF16)
    onesb = sbl("ones_b", [128, 1], BF16)
    onesf = sbl("ones_f", [1, 128], F32)
    hm2 = sbl("hm2", [128, 128], BF16)
    Lt = sbl("Lt", [128, 1], F32)
    dCt = sbl("dCt", [128, 1], F32)
    stDer = ExitStack()
    qvec = stDer.enter_context(nc.sbuf_tensor("qvec", [128, CH], F32, side="right"))
    kvec = stDer.enter_context(nc.sbuf_tensor("kvec", [128, CH], F32, side="right"))
    dmask = stDer.enter_context(nc.sbuf_tensor("dmask", [128, NCH * CH], BF16, side="right"))
    chs = sbl("chs", [128, NCH], F32)
    coefs = sbl("coefs", [128, 4], F32)
    hselt = sbl("hselt", [128, 4], F32)
    validt = sbl("validt", [128, 1], F32)
    biast = {w: sbl("bias_" + w, [128, FCH], F32) for w in ("bq", "bk", "bv", "bo")}
    lngt = sbl("lngt", [128, FCH * 2], F32)
    lnbt = sbl("lnbt", [128, FCH * 2], F32)
    qst = {k: sbl("t" + k, [128, 2], F32) for k in ("qsl", "qsh", "ksl", "ksh")}
    b2tt = sbl("b2tt", [128, FCH * 2], F32)
    b1tt = sbl("b1tt", [128, 88], F32)
    stat = sbl("stat", [1, 1536], F32)
    lnrep = [sbl(f"lnrep{i}", [128, 1024], F32) for i in range(2)]
    onec = sbl("onec", [128, 1], F32)
    epsc = sbl("epsc", [128, 1], F32)
    st2 = sbl("st2", [128, 8], F32)
    sctch = sbl("sctch", [128, 8], F32)
    xh1 = sbl("xh1", [128, FCH], F32)

    ARENA = es.enter_context(nc.psum_tensor("ARENA", [128, 4096], F32))
    G = [ARENA[:, 512 * i:512 * (i + 1)] for i in range(4)]
    BIG = ARENA[:, 2048:3072]
    H = [ARENA[:, 3072 + 512 * i:3072 + 512 * (i + 1)] for i in range(2)]
    BA = BIG[:, 0:512]
    BB = BIG[:, 512:1024]

    for nm, ap in (("identb", ident), ("onesb", onesb), ("onesf", onesf),
                   ("hm2b", hm2), ("hsel", hselt), ("validp", validt),
                   ("lng", lngt), ("lnb", lnbt), ("b2t", b2tt), ("b1t", b1tt)):
        P.dma("sp", ap[:], dram[nm], writes=(nm,))
    for w in ("bq", "bk", "bv", "bo"):
        P.dma("sp", biast[w][:], dram[w], writes=(w,))
    for k in qst:
        P.dma("sp", qst[k][:], dram[k], writes=(k,))

    P.memset("dve", onec[:], 1.0, writes=("onec",))
    P.memset("dve", epsc[:], EPS, writes=("epsc",))
    P.dma("sp", sctch[:, 0:1], dram["dparam"], writes=("sc0",))
    P.act(sctch[:, 1:2], sctch[:, 0:1], AF.Abs, reads=("sc0",), writes=("sc1",))
    P.act(sctch[:, 2:3], sctch[:, 1:2], AF.Exp, scale=-1.0,
          reads=("sc1",), writes=("sc2",))
    P.act(sctch[:, 3:4], sctch[:, 2:3], AF.Ln, bias=onec[:, 0:1],
          reads=("sc2", "onec"), writes=("sc3",))
    P.ts("dve", sctch[:, 4:5], sctch[:, 0:1], 0.0, ALU.min,
         reads=("sc0",), writes=("sc4",))
    P.tt("dve", Lt[:], sctch[:, 4:5], sctch[:, 3:4], ALU.subtract,
         reads=("sc3", "sc4"), writes=("Lt",))

    def dpow(dst, key, src, w):
        P.dma("sp", dst, src, writes=(key,))
        P.ts("dve", dst, dst, Lt[:, 0:1], ALU.mult, reads=(key, "Lt"), writes=(key,))
        P.act(dst, dst, AF.Exp, reads=(key,), writes=(key,))

    dpow(qvec[:], "qvec", dram["qexp"], CH)
    dpow(kvec[:], "kvec", dram["kexp"], CH)
    dpow(chs[:], "chs", dram["chexp"], NCH)
    P.ts("dve", dCt[:], Lt[:], float(CH), ALU.mult, reads=("Lt",), writes=("dC",))
    P.act(dCt[:], dCt[:], AF.Exp, reads=("dC",), writes=("dC",))
    stSet = ExitStack()
    d01t = stSet.enter_context(nc.sbuf_tensor("d01t", [128, NCH * CH], F32))
    ecl = stSet.enter_context(nc.sbuf_tensor("eclt", [128, NCH * CH], F32))
    dpow(ecl[:], "eclt", dram["eclamp"], NCH * CH)
    P.dma("sp", d01t[:], dram["d01"], writes=("d01t",))
    P.tt("dve", dmask[:], ecl[:], d01t[:], ALU.mult,
         reads=("eclt", "d01t"), writes=("dmask",))
    cfet = stSet.enter_context(nc.sbuf_tensor("cfet", [128, 4], F32))
    dpow(cfet[:], "cfe", dram["coefexp"], 4)
    cvt = stSet.enter_context(nc.sbuf_tensor("cvt", [128, 4], F32))
    P.dma("sp", cvt[:], dram["coefval"], writes=("cvt",))
    P.tt("dve", coefs[:], cfet[:], cvt[:], ALU.mult,
         reads=("cfe", "cvt"), writes=("coefs",))
    stSet.close()

    _lnrot = [0]

    def ln_stats_epi(bkA=None, bkB=None, kA="BIGa", kB="BIGb", n_feat=1024.0):
        bkA = BA if bkA is None else bkA
        bkB = BB if bkB is None else bkB
        j = _lnrot[0] % 2
        _lnrot[0] += 1
        ra = lnrep[j][:, 0:512]; rb = lnrep[j][:, 512:1024]
        rak, rbk = f"lnra{j}", f"lnrb{j}"
        me = stat[0:1, 0:512]; va = stat[0:1, 512:1024]
        rs = stat[0:1, 1024:1536]; mr = va
        P.act(me, bkA[0:1, 0:512], AF.Copy, scale=1.0 / n_feat,
              reads=(kA,), writes=("st_me",))
        P.act(va, me, AF.Square, reads=("st_me",), writes=("st_va",))
        P.stt("dve", va, bkB[0:1, 0:512], 1.0 / n_feat, va, ALU.mult,
              ALU.subtract, reads=(kB, "st_va"), writes=("st_va",))
        P.act(rs, va, AF.Ln, bias=epsc[0:1, 0:1], reads=("st_va", "epsc"),
              writes=("st_rs",))
        P.act(rs, rs, AF.Exp, scale=-0.5, reads=("st_rs",), writes=("st_rs",))
        P.tt("dve", mr, me, rs, ALU.mult, reads=("st_me", "st_rs"), writes=("st_va",))
        P.mmg([(bkA[:, 0:512], onesf[0:1, :], rs, None)], reads=("st_rs",), writes=(kA,))
        P.mmg([(bkB[:, 0:512], onesf[0:1, :], mr, None)], reads=("st_va",), writes=(kB,))
        P.act(ra, bkA[:, 0:512], AF.Copy, reads=(kA,), writes=(rak,))
        P.act(rb, bkB[:, 0:512], AF.Copy, reads=(kB,), writes=(rbk,))
        return ra, rb, rak, rbk

    stA = ExitStack()
    sba = lambda nm, shp, dt=F32: stA.enter_context(nc.sbuf_tensor(nm, shp, dt))
    Mst = sba("Mst", [128, FCH * D], F32)
    Mbf = sba("Mbf", [128, FCH * D], BF16)
    Wqs = sba("Wqs", [128, FCH * D], BF16)
    Wks = sba("Wks", [128, FCH * D], BF16)
    Wvs = sba("Wvs", [128, FCH * D], BF16)
    xt = [sba(f"xt{i}", [128, 512], F32) for i in range(1)]
    xcb = [sba(f"xcb{i}", [128, 512], BF16) for i in range(2)]
    sqc = [sba(f"sqc{i}", [128, 512], BF16) for i in range(2)]
    xab = sba("xab", [128, FCH * CH], BF16)
    qb = sba("q_b", [128, FCH * CH], BF16)
    kb = sba("k_b", [128, FCH * CH], BF16)
    vb = sba("v_b", [128, FCH * CH], BF16)
    ktl = sba("ktl", [128, FCH * CH], BF16)
    qtl = sba("qtl", [128, FCH * CH], BF16)
    VT = sba("VTt", [128, NCH * D], BF16)
    KT = sba("KTt", [128, NCH * D], BF16)
    Pb = sba("Pb", [128, 4 * CH], BF16)
    PT = sba("PTt", [128, 4 * CH], BF16)
    zb = sba("zb", [128, 512], F32)
    zb2 = sba("zb2", [128, 512], F32)
    oste = [sba(f"oste{i}", [128, 512], F32) for i in range(2)]

    P.dma("pool", Wqs[:], dram["Wq"].rearrange("(c p) o -> p c o", p=128),
          writes=("Wqs",))
    P.dma("pool", Wks[:], dram["Wk"].rearrange("(c p) o -> p c o", p=128),
          writes=("Wks",))
    P.dma("pool", Wvs[:], dram["Wv"].rearrange("(c p) o -> p c o", p=128),
          writes=("Wvs",))

    def wsl(wt, kch, och):
        return wt[:, kch * D + och * 128: kch * D + och * 128 + 128]

    for c in range(NCH):
        col0 = c * CH
        xbufs = [xt[0], oste[0], oste[1]]
        xkeys_ = ["xt0", "oste0", "oste1"]
        for f in range(FCH):
            bkey = xkeys_[f % 3]
            ck, sk = f"xcb{f % 2}", f"sqc{f % 2}"
            P.dma("sp", xbufs[f % 3][:], drr("xT", f)[:, col0:col0 + CH],
                  writes=(bkey,))
            P.act(xcb[f % 2][:], xbufs[f % 3][:], AF.Copy, reads=(bkey,),
                  writes=(ck,))
            P.tt("dve", sqc[f % 2][:], xbufs[f % 3][:], xbufs[f % 3][:],
                 ALU.mult, reads=(bkey,), writes=(sk,))
            P.mm1(H[0][0:1, 0:512], onesb[:, 0:1], xcb[f % 2][:],
                  start=(f == 0), stop=(f == FCH - 1),
                  reads=(ck, "onesb"), writes=("H0",))
            P.mm1(H[1][0:1, 0:512], onesb[:, 0:1], sqc[f % 2][:],
                  start=(f == 0), stop=(f == FCH - 1),
                  reads=(sk, "onesb"), writes=("H1",))
        ra, rb, rak, rbk = ln_stats_epi(H[0], H[1], "H0", "H1")
        for f in range(FCH):
            bkey = xkeys_[f % 3]
            P.dma("act", xbufs[f % 3][:], drr("xT", f)[:, col0:col0 + CH],
                  writes=(bkey,))
            P.tt("pool", zb[:], xbufs[f % 3][:], ra, ALU.mult,
                 reads=(bkey, rak), writes=("zb",))
            P.tt("pool", xab[:, f * CH:(f + 1) * CH], zb[:], rb, ALU.subtract,
                 reads=("zb", rbk), writes=(f"xab.{f}",))
        xareads = tuple(f"xab.{f}" for f in range(FCH))

        def gkey(i):
            return f"G{i}"

        for och in range(FCH):
            g = G[och % 2]
            P.mmg([(g[:, 0:CH], wsl(Wqs, kch, och),
                    xab[:, kch * CH:(kch + 1) * CH], None) for kch in range(FCH)],
                  reads=("Wqs",) + xareads, writes=(gkey(och % 2),))
            P.ts("dve", qb[:, och * CH:(och + 1) * CH], g[:, 0:CH],
                 biast["bq"][:, och:och + 1], ALU.add,
                 reads=(gkey(och % 2), "bq"), writes=(f"qb.{och}",))
        for och in range(FCH):
            g = G[och % 2]
            P.mmg([(g[:, 0:CH], wsl(Wks, kch, och),
                    xab[:, kch * CH:(kch + 1) * CH], None) for kch in range(FCH)],
                  reads=("Wks",) + xareads, writes=(gkey(och % 2),))
            P.ts("dve", zb2[:], g[:, 0:CH], biast["bk"][:, och:och + 1], ALU.add,
                 reads=(gkey(och % 2), "bk"), writes=("zb2",))
            P.ts("dve", zb[:], zb2[:], 0.0, ALU.min, reads=("zb2",), writes=("zb",))
            P.act(zb[:], zb[:], AF.Exp, reads=("zb",), writes=("zb",))
            P.stt("dve", kb[:, och * CH:(och + 1) * CH], zb2[:], 0.0, zb[:],
                  ALU.max, ALU.add, reads=("zb2", "zb"), writes=(f"kb.{och}",))
        for och in range(FCH):
            g = G[och % 2]
            P.mmg([(g[:, 0:CH], wsl(Wvs, kch, och),
                    xab[:, kch * CH:(kch + 1) * CH], None) for kch in range(FCH)],
                  reads=("Wvs",) + xareads, writes=(gkey(och % 2),))
            P.ts("dve", vb[:, och * CH:(och + 1) * CH], g[:, 0:CH],
                 biast["bv"][:, och:och + 1], ALU.add,
                 reads=(gkey(och % 2), "bv"), writes=(f"vb.{och}",))

        for f in range(FCH):
            sl = slice(f * CH, (f + 1) * CH)
            P.tt("dve", qtl[:, sl], qb[:, sl], qvec[:], ALU.mult,
                 reads=(f"qb.{f}", "qvec"), writes=(f"qtl.{f}",))
            P.tt("dve", ktl[:, sl], kb[:, sl], kvec[:], ALU.mult,
                 reads=(f"kb.{f}", "kvec"), writes=(f"ktl.{f}",))
        P.dma("act", dram["sp_qt"].rearrange("(ff p) t -> p ff t", p=128)[:, :, col0:col0 + CH],
              qtl.ap().rearrange("p (ff t) -> p ff t", t=CH),
              reads=tuple(f"qtl.{f}" for f in range(FCH)), writes=("d_qt",))

        for jt in range(4):
            for half in range(2):
                g = G[2 + half]
                gk = gkey(2 + half)
                parts = []
                rd = []
                for dq in range(4):
                    dch = half * 4 + dq
                    parts.append((g[:, dq * 128:(dq + 1) * 128],
                                  vb[:, dch * CH + jt * 128: dch * CH + jt * 128 + 128],
                                  ident[:], None))
                    rd.append(f"vb.{dch}")
                P.mmg(parts, reads=tuple(rd) + ("identb",), writes=(gk,))
                P.cp("dve", VT[:, jt * D + half * 512: jt * D + half * 512 + 512],
                     g[:, 0:512], reads=(gk,), writes=(f"VT.{jt}.{half}",))
        for jt in range(4):
            for half in range(2):
                g = G[2 + half]
                gk = gkey(2 + half)
                parts = []
                rd = []
                for dq in range(4):
                    dch = half * 4 + dq
                    parts.append((g[:, dq * 128:(dq + 1) * 128],
                                  ktl[:, dch * CH + jt * 128: dch * CH + jt * 128 + 128],
                                  ident[:], None))
                    rd.append(f"ktl.{dch}")
                P.mmg(parts, reads=tuple(rd) + ("identb",), writes=(gk,))
                P.cp("dve", KT[:, jt * D + half * 512: jt * D + half * 512 + 512],
                     g[:, 0:512], reads=(gk,), writes=(f"KT.{jt}.{half}",))

        def sec_out(c=c, col0=col0):
            for qt in range(4):
                g = G[qt % 2]
                P.mmg([(g[:, 0:CH], qb[:, kch * CH + qt * 128: kch * CH + qt * 128 + 128],
                        kb[:, kch * CH:(kch + 1) * CH], None) for kch in range(FCH)],
                      reads=tuple(f"qb.{f}" for f in range(FCH)) +
                            tuple(f"kb.{f}" for f in range(FCH)),
                      writes=(gkey(qt % 2),))
                P.tt("dve", Pb[:, qt * CH:(qt + 1) * CH], g[:, 0:CH],
                     dmask[:, qt * CH:(qt + 1) * CH], ALU.mult,
                     reads=(gkey(qt % 2), "dmask"), writes=(f"Pb.{qt}",))
            for mt in range(4):
                g = G[2 + mt % 2]
                gk = gkey(2 + mt % 2)
                parts = []
                for qt in range(4):
                    parts.append((g[:, qt * 128:(qt + 1) * 128],
                                  Pb[:, qt * CH + mt * 128: qt * CH + mt * 128 + 128],
                                  ident[:], None))
                P.mmg(parts, reads=tuple(f"Pb.{qt}" for qt in range(4)) + ("identb",),
                      writes=(gk,))
                P.cp("dve", PT[:, mt * CH:(mt + 1) * CH], g[:, 0:512],
                     reads=(gk,), writes=(f"PT.{mt}",))

            for ech in range(FCH):
                gps = G[ech % 2]
                gk = gkey(ech % 2)
                parts = []
                rd = []
                if c > 0:
                    for dch in range(FCH):
                        parts.append((gps[:, 0:CH],
                                      Mbf[:, dch * D + ech * 128: dch * D + ech * 128 + 128],
                                      qtl[:, dch * CH:(dch + 1) * CH], None))
                        rd += ["Mbf", f"qtl.{dch}"]
                for mt in range(4):
                    parts.append((gps[:, 0:CH],
                                  VT[:, mt * D + ech * 128: mt * D + ech * 128 + 128],
                                  PT[:, mt * CH:(mt + 1) * CH], None))
                    rd += [f"VT.{mt}.0", f"VT.{mt}.1", f"PT.{mt}"]
                P.mmg(parts, reads=tuple(set(rd)), writes=(gk,))
                ok_ = f"xcb{ech % 2}"
                P.act(xcb[ech % 2][:], gps[:, 0:CH], AF.Copy, reads=(gk,),
                      writes=(ok_,))
                P.dma("sp", drr("sp_outT", ech)[:, col0:col0 + CH], xcb[ech % 2][:],
                      reads=(ok_,), writes=("d_outT",))

        def sec_state(c=c):
            for dch in range(FCH):
                for half in range(2):
                    bkey = "BIGa" if half == 0 else "BIGb"
                    bap = BA if half == 0 else BB
                    parts = []
                    rd = []
                    for jt in range(4):
                        parts.append((bap,
                                      KT[:, jt * D + dch * 128: jt * D + dch * 128 + 128],
                                      VT[:, jt * D + half * 512: jt * D + half * 512 + 512],
                                      None))
                        rd += [f"KT.{jt}.{dch // 4}", f"VT.{jt}.{half}"]
                    P.mmg(parts, reads=tuple(set(rd)), writes=(bkey,))
                    msl = slice(dch * D + half * 512, dch * D + half * 512 + 512)
                    if c == 0:
                        P.cp("dve", Mst[:, msl], bap, reads=(bkey,),
                             writes=(f"Mst.{dch}.{half}",))
                    else:
                        P.stt("dve", Mst[:, msl], Mst[:, msl], dCt[:, 0:1], bap,
                              ALU.mult, ALU.add,
                              reads=(f"Mst.{dch}.{half}", "dC", bkey),
                              writes=(f"Mst.{dch}.{half}",))

        if c < NCH - 1:
            sec_out()
            sec_state()
            for dch in range(FCH):
                P.act(Mbf[:, dch * D:(dch + 1) * D], Mst[:, dch * D:(dch + 1) * D],
                      AF.Copy, reads=(f"Mst.{dch}.0", f"Mst.{dch}.1"),
                      writes=("Mbf",))
        else:
            sec_state()
            for dch in range(FCH):
                P.act(Wvs[:, dch * D:(dch + 1) * D],
                      Mst[:, dch * D:(dch + 1) * D],
                      AF.Copy, reads=(f"Mst.{dch}.0", f"Mst.{dch}.1"),
                      writes=("Wvs", "Mb2"))
            P.dma("sp", dram["a_in"].rearrange("(dd p) e -> p dd e", p=128),
                  Wvs.ap().rearrange("p (dd e) -> p dd e", e=D),
                  reads=("Mb2",), writes=("d_a_in",))
            P.coll("AllGather", ALU.bypass, [[0, 1, 2, 3], [4, 5, 6, 7]],
                   [dram["a_in"]], [dram["a_out"]], sem="cc1",
                   reads=("d_a_in",), writes=("d_a_out",))
            sec_out()


    if "A" in dbg:
        da = dout("dbg_A", [D, D])
        P.dma("sp", da.rearrange("(dd p) e -> p dd e", p=128),
              Mst.ap().rearrange("p (dd e) -> p dd e", e=D),
              reads=tuple(f"Mst.{d_}.{h_}" for d_ in range(FCH) for h_ in range(2)),
              writes=("d_dbgA",))
    if "outT" in dbg:
        do_ = dout("dbg_outT", [D, TPC], BF16)
        P.dma("sp", do_.rearrange("(ff p) t -> p ff t", p=128),
              dram["sp_outT"].rearrange("(ff p) t -> p ff t", p=128),
              reads=("d_outT",), writes=("d_dbgoT",))

    if upto == "attn1":
        P.barrier()
        P.emit()
        stA.close()
        es.close()
        return nc, dram

    dint("d_minit", [D, D], BF16)
    dint("d_xsb", [D, TW], BF16)
    P.barrier()
    stA.close()
    stDer.close()

    warena = ExitStack()
    wqkvb = warena.enter_context(nc.sbuf_tensor("wqkvb", [128, FCH * 3 * INNER], BF16))
    wob = warena.enter_context(nc.sbuf_tensor("wob", [128, 4 * D], BF16))

    stMI = ExitStack()
    minitb = stMI.enter_context(nc.sbuf_tensor("minitb", [128, FCH * D], BF16))
    st2p = ExitStack()
    slotb = st2p.enter_context(nc.sbuf_tensor("slotb", [128, 4 * D], F32))
    macc = st2p.enter_context(nc.sbuf_tensor("macc", [128, D], F32))
    for dch in range(FCH):
        for sl_ in range(4):
            P.dma("pool", slotb[:, sl_ * D:(sl_ + 1) * D],
                  dram["a_out"][sl_ * D + dch * 128: sl_ * D + dch * 128 + 128, :],
                  reads=("d_a_out",), writes=(f"slot.{sl_}",))
        P.ts("dve", macc[:], slotb[:, 0:D], coefs[:, 0:1], ALU.mult,
             reads=("slot.0", "coefs"), writes=("macc",))
        for sl_ in range(1, 3):
            P.stt("dve", macc[:], slotb[:, sl_ * D:(sl_ + 1) * D],
                  coefs[:, sl_:sl_ + 1], macc[:], ALU.mult, ALU.add,
                  reads=(f"slot.{sl_}", "coefs", "macc"), writes=("macc",))
        P.stt("dve", minitb[:, dch * D:(dch + 1) * D], slotb[:, 3 * D:4 * D],
              coefs[:, 3:4], macc[:],
              ALU.mult, ALU.add, reads=("slot.3", "coefs", "macc"),
              writes=("minitb",))
    P.dma("pool", wqkvb[:], dram["wqkv0"].rearrange("(c p) o -> p c o", p=128),
          writes=("wqkvb",))
    P.dma("pool", wob[:], dram["wo0"].rearrange("(c p) o -> p c o", p=128),
          writes=("wob",))

    if "minit" in dbg:
        dmn = dout("dbg_minit", [D, D], BF16)
        P.dma("sp", dmn.rearrange("(dd p) e -> p dd e", p=128),
              minitb.ap().rearrange("p (dd e) -> p dd e", e=D),
              reads=("minitb",), writes=("dbmn",))
    st3 = ExitStack()
    s3 = lambda nm, shp, dt=F32: st3.enter_context(nc.sbuf_tensor(nm, shp, dt))
    qhb = [s3(f"qhb{i}", [128, FCH * 512], BF16) for i in range(2)]
    oob2 = s3("oob2", [128, FCH * 512], BF16)
    oob = s3("oob", [128, FCH * 512], BF16)
    Wos = s3("Wos", [128, FCH * D], BF16)
    aof = [s3(f"aof{i}", [128, FCH * 512], BF16) for i in range(2)]
    sqb3 = [s3(f"sqb3{i}", [128, FCH * 512], BF16) for i in range(2)]
    x1f = s3("x1f", [128, FCH * 512], F32)
    xff = s3("xff", [128, FCH * 512], F32)
    xfb = s3("xfb", [128, 512], BF16)
    x1b = s3("x1b", [128, 512], BF16)
    sq1 = s3("sq1", [128, 512], BF16)
    xres = [s3(f"xres{i}", [128, 512], F32) for i in range(2)]
    zb3 = s3("zb3", [128, 512], F32)
    zb3b = s3("zb3b", [128, 512], F32)
    zcol = s3("zcol", [128, 1], BF16)
    P.memset("dve", zcol[:], 0.0, writes=("zcol",))
    P.dma("pool", Wos[:], dram["Wo"].rearrange("(c p) o -> p c o", p=128),
          writes=("Wos",))

    def stageX(tt):
        j = tt % 2
        col0 = tt * 512
        q_, qk_ = qhb[j], f"qhb{j}"
        P.dma("sp", q_.ap().rearrange("p (ff t) -> p ff t", t=512),
              dram["sp_qt"].rearrange("(ff p) t -> p ff t", p=128)[:, :, col0:col0 + 512],
              reads=("d_qt",), writes=(qk_,))
        P.ts("dve", q_[:], q_[:], chs[:, tt:tt + 1], ALU.mult,
             reads=(qk_, "chs"), writes=(qk_,))
        for ech in range(FCH):
            gk = f"G{ech % 2}"
            g = G[ech % 2]
            P.mmg([(g[:, 0:512],
                    minitb[:, dch * D + ech * 128: dch * D + ech * 128 + 128],
                    q_[:, dch * 512:(dch + 1) * 512], None) for dch in range(FCH)],
                  reads=("minitb", qk_), writes=(gk,))
            sl = slice(ech * 512, (ech + 1) * 512)
            P.dma("act", oob2[:, sl], drr("sp_outT", ech)[:, col0:col0 + 512],
                  reads=("d_outT",), writes=(f"oo2.{ech}",))
            P.tt("dve", oob[:, sl], oob2[:, sl], g[:, 0:512], ALU.add,
                 reads=(f"oo2.{ech}", gk), writes=(f"oob.{ech}",))
        for och in range(FCH):
            hk = f"H{och % 2}"
            h = H[och % 2]
            P.mmg([(h[:, 0:512], Wos[:, ech * D + och * 128: ech * D + och * 128 + 128],
                    oob[:, ech * 512:(ech + 1) * 512], None) for ech in range(FCH)],
                  reads=("Wos",) + tuple(f"oob.{e_}" for e_ in range(FCH)),
                  writes=(hk,))
            sl = slice(och * 512, (och + 1) * 512)
            P.ts("dve", aof[j][:, sl], h[:, 0:512], biast["bo"][:, och:och + 1],
                 ALU.add, reads=(hk, "bo"), writes=(f"aof{j}.{och}",))
            P.tt("dve", sqb3[j][:, sl], aof[j][:, sl], aof[j][:, sl], ALU.mult,
                 reads=(f"aof{j}.{och}",), writes=(f"sqb3{j}.{och}",))

    def stageY(tt):
        j = tt % 2
        col0 = tt * 512
        for och in range(FCH):
            sl = slice(och * 512, (och + 1) * 512)
            P.mm1(BIG[0:1, 0:512], onesb[:, 0:1], aof[j][:, sl],
                  start=(och == 0), stop=(och == FCH - 1),
                  reads=(f"aof{j}.{och}", "onesb"), writes=("BIGa",))
            P.mm1(BIG[0:1, 512:1024], onesb[:, 0:1], sqb3[j][:, sl],
                  start=(och == 0), stop=(och == FCH - 1),
                  reads=(f"sqb3{j}.{och}", "onesb"), writes=("BIGb",))
        ra, rb, rak, rbk = ln_stats_epi()
        for f in range(FCH):
            sl = slice(f * 512, (f + 1) * 512)
            xk = f"xres{f % 2}"
            P.dma("sp", xres[f % 2][:], drr("xT", f)[:, col0:col0 + 512],
                  writes=(xk,))
            zeng = "pool" if f % 3 == 0 else "dve"
            z, zk = (zb3, "zb3") if zeng == "pool" else (zb3b, "zb3b")
            P.tt(zeng, z[:], aof[j][:, sl], ra, ALU.mult,
                 reads=(f"aof{j}.{f}", rak), writes=(zk,))
            P.tt(zeng, z[:], z[:], rb, ALU.subtract,
                 reads=(zk, rbk), writes=(zk,))
            P.tt(zeng, x1f[:, sl], xres[f % 2][:], z[:], ALU.add,
                 reads=(xk, zk), writes=(f"x1f.{f}",))
            P.dma("pool", drr("sp_x1", f)[:, col0:col0 + 512], x1f[:, sl],
                  reads=(f"x1f.{f}",), writes=("d_x1",))
        for f in range(FCH):
            sl = slice(f * 512, (f + 1) * 512)
            P.act(x1b[:], x1f[:, sl], AF.Copy, reads=(f"x1f.{f}",),
                  writes=("x1b",))
            P.tt("dve", sq1[:], x1f[:, sl], x1f[:, sl], ALU.mult,
                 reads=(f"x1f.{f}",), writes=("sq1",))
            P.mm1(BIG[0:1, 0:512], onesb[:, 0:1], x1b[:],
                  start=(f == 0), stop=(f == FCH - 1),
                  reads=("x1b", "onesb"), writes=("BIGa",))
            P.mm1(BIG[0:1, 512:1024], onesb[:, 0:1], sq1[:],
                  start=(f == 0), stop=(f == FCH - 1),
                  reads=("sq1", "onesb"), writes=("BIGb",))
        ra, rb, rak, rbk = ln_stats_epi()
        for f in range(FCH):
            sl = slice(f * 512, (f + 1) * 512)
            zeng = "pool" if f % 3 == 0 else "dve"
            z, zk = (zb3, "zb3") if zeng == "pool" else (zb3b, "zb3b")
            P.tt(zeng, z[:], x1f[:, sl], ra, ALU.mult,
                 reads=(f"x1f.{f}", rak), writes=(zk,))
            P.tt(zeng, xff[:, sl], z[:], rb, ALU.subtract,
                 reads=(zk, rbk), writes=(f"xff.{f}",))
            P.dma("act", drr("sp_xin0", f)[:, col0:col0 + 512], xff[:, sl],
                  reads=(f"xff.{f}",), writes=("d_xin0",))
            P.act(xfb[:], xff[:, sl], AF.Copy, reads=(f"xff.{f}",),
                  writes=("xfb",))
            P.dma("sp", dram["d_xsb"][f * 128:(f + 1) * 128,
                                      QCOL0 + col0:QCOL0 + col0 + 512],
                  xfb[:], reads=("xfb",), writes=("d_xsb_own",))

    for tt in range(NCH + 1):
        if tt < NCH:
            stageX(tt)
        if tt >= 1:
            stageY(tt - 1)

    if "x1" in dbg:
        dx = dout("dbg_x1", [D, TPC])
        P.dma("sp", dx, dram["sp_x1"], reads=("d_x1",), writes=("d_dbg1",))
        dx2 = dout("dbg_xf", [D, TPC])
        P.dma("sp", dx2, dram["sp_xin0"], reads=("d_xin0",), writes=("d_dbg2",))

    P.dma("pool", dram["h2_in"], dram["sp_xin0"][:, TPC - HALO:TPC],
          reads=("d_xin0",), writes=("d_h2in",))
    P.barrier()
    P.coll("AllGather", ALU.bypass, [[0, 1, 2, 3], [4, 5, 6, 7]],
           [dram["h2_in"]], [dram["h2_out"]], sem="cc2",
           reads=("d_h2in",), writes=("d_h2out",))
    st3.close()
    st2p.close()
    stMI.close()

    stXB = ExitStack()

    def halo_fill(hout_key, hout_ap, stHF, pfx):
        hsl = stHF.enter_context(nc.sbuf_tensor("hsl" + pfx, [128, 4 * HALO], F32))
        halof = stHF.enter_context(nc.sbuf_tensor("hf" + pfx, [128, HALO], F32))
        halob = stHF.enter_context(nc.sbuf_tensor("hb" + pfx, [128, HALO], BF16))
        for f in range(FCH):
            for sl_ in range(4):
                P.dma("pool", hsl[:, sl_ * HALO:(sl_ + 1) * HALO],
                      hout_ap[sl_ * D + f * 128: sl_ * D + f * 128 + 128, :],
                      reads=(hout_key,), writes=(f"hsl.{sl_}",))
            P.ts("dve", halof[:], hsl[:, 0:HALO], hselt[:, 0:1], ALU.mult,
                 reads=("hsl.0", "hsel"), writes=("halof",))
            for sl_ in range(1, 4):
                P.stt("dve", halof[:], hsl[:, sl_ * HALO:(sl_ + 1) * HALO],
                      hselt[:, sl_:sl_ + 1], halof[:], ALU.mult, ALU.add,
                      reads=(f"hsl.{sl_}", "hsel", "halof"), writes=("halof",))
            P.act(halob[:], halof[:], AF.Copy, reads=("halof",), writes=("halob",))
            P.dma("sp", dram["d_xsb"][f * 128:(f + 1) * 128, 1:1 + HALO],
                  halob[:], reads=("halob",), writes=("d_xsb_halo",))
            P.cp("dve", xh1[:, f:f + 1], halof[:, HALO - 1:HALO],
                 reads=("halof",), writes=("xh1",))

    def emit_layer(l, xs_in_d, xs_out_d, write_next_xsb, hkey, hout):
        LQ = 2561
        stH1 = ExitStack()
        h2b = stH1.enter_context(nc.sbuf_tensor("h2b" + str(l), [128, FCH * TPC],
                                                BF16, side="right"))
        stL1 = ExitStack()
        sL1 = lambda nm, shp, dt=BF16: stL1.enter_context(
            nc.sbuf_tensor(nm + str(l), shp, dt, side="right"))
        oTb = sL1("oTb", [128, 4 * TPC])
        xmx = sL1("xmx", [128, FCH], F32)
        stL2 = ExitStack()
        sL2 = lambda nm, shp, dt=BF16: stL2.enter_context(nc.sbuf_tensor(nm + str(l), shp, dt))
        qrb = sL2("qrb", [128, 4 * LQ])
        krb = sL2("krb", [128, 4 * LQ])
        stVR = ExitStack()
        vrb = stVR.enter_context(nc.sbuf_tensor("vrb" + str(l), [128, 4 * LQ], BF16))

        stq = ExitStack()
        sQ = lambda nm, shp, dt=F32: stq.enter_context(nc.sbuf_tensor(nm + str(l), shp, dt))
        xst = [sQ(f"xst{i}", [128, FCH * 513], BF16) for i in range(2)]
        qraw = sQ("qraw", [128, 2 * 512], F32)
        sqcs = sQ("sqcs", [128, 2 * 512], BF16)
        invs = sQ("invs", [128, 512], F32)
        tblb = [sQ(f"tblb{i}", [128, 2 * 512], F32) for i in range(2)]
        rt0 = sQ("rt0", [128, 512], F32)
        rt1 = sQ("rt1", [128, 512], F32)

        TILES = [(1027, 512), (1539, 512), (2051, 512),
                 (515, 512), (3, 512), (2, 1)]
        sides = [("q", 0, "csq", "ssq", "qsl", "qsh"),
                 ("k", 4, "csk", "ssk", "ksl", "ksh")]
        for ti, (c0, w) in enumerate(TILES):
            if ti == 3:
                halo_fill(hkey, hout, stq, f"l{l}")
            if c0 >= 1027:
                xrd = ("d_xsb_own",)
            elif c0 == 515:
                xrd = ("d_xsb_own", "d_xsb_halo")
            else:
                xrd = ("d_xsb_halo",)
            nb = ti % len(xst)
            xk = f"xst{nb}"
            P.dma("sp", xst[nb].ap().rearrange("p (ff t) -> p ff t", t=513)[:, :, 0:w + 1],
                  dram["d_xsb"].rearrange("(ff p) t -> p ff t", p=128)[:, :, c0 - 1:c0 + w],
                  reads=xrd, writes=(xk,))

            def rhs(kch, sh, nb=nb):
                base = kch * 513
                return xst[nb][:, base + 1 - sh: base + 1 - sh + w]

            for sname, obase, csn, ssn, sln, shn in sides:
                for grp in range(2):
                    for hi in range(2):
                        och = obase + 2 * grp + hi
                        g = G[hi]
                        gk = f"G{hi}"
                        P.mmg([(g[:, 0:w],
                                wqkvb[:, kch * 3 * INNER + och * 128: kch * 3 * INNER + och * 128 + 128],
                                rhs(kch, 1 if kch >= 4 else 0), None)
                               for kch in range(FCH)],
                              reads=("wqkvb", xk), writes=(gk,))
                        P.act(qraw[:, hi * 512:hi * 512 + w], g[:, 0:w], AF.Copy,
                              reads=(gk,), writes=(f"qraw.{hi}",))
                        P.tt("dve", sqcs[:, hi * 512:hi * 512 + w],
                             qraw[:, hi * 512:hi * 512 + w],
                             qraw[:, hi * 512:hi * 512 + w], ALU.mult,
                             reads=(f"qraw.{hi}",), writes=(f"sqcs.{hi}",))
                    P.mm1(H[0][:, 0:w], hm2[:, :], sqcs[:, 0:w], start=True,
                          stop=False, reads=("sqcs.0", "hm2b"), writes=("H0",))
                    P.mm1(H[0][:, 0:w], hm2[:, :], sqcs[:, 512:512 + w],
                          start=False, stop=True, reads=("sqcs.1", "hm2b"),
                          writes=("H0",))
                    P.ts("dve", invs[:, 0:w], H[0][:, 0:w], 1e-24, ALU.max,
                         reads=("H0",), writes=("invs",))
                    P.act(invs[:, 0:w], invs[:, 0:w], AF.Ln, reads=("invs",),
                          writes=("invs",))
                    P.act(invs[:, 0:w], invs[:, 0:w], AF.Exp, scale=-0.5,
                          reads=("invs",), writes=("invs",))
                    P.ts("dve", qraw[:, 0:w], qraw[:, 0:w],
                         qst[sln][:, l:l + 1], ALU.mult,
                         reads=("qraw.0", sln), writes=("qraw.0",))
                    P.ts("dve", qraw[:, 512:512 + w], qraw[:, 512:512 + w],
                         qst[shn][:, l:l + 1], ALU.mult,
                         reads=("qraw.1", shn), writes=("qraw.1",))
                    P.tt("dve", qraw[:, 0:w], qraw[:, 0:w], invs[:, 0:w],
                         ALU.mult, reads=("qraw.0", "invs"), writes=("qraw.0",))
                    P.tt("dve", qraw[:, 512:512 + w], qraw[:, 512:512 + w],
                         invs[:, 0:w], ALU.mult, reads=("qraw.1", "invs"),
                         writes=("qraw.1",))
                    tk = f"tblb{(2 * ti + grp) % 2}"
                    tb = tblb[(2 * ti + grp) % 2]
                    P.dma("sp", tb[:, 0:w], dram[csn][:, c0 - 2:c0 - 2 + w],
                          writes=(tk,), slow=(w == 1))
                    P.dma("sp", tb[:, 512:512 + w], dram[ssn][:, c0 - 2:c0 - 2 + w],
                          reads=(tk,), writes=(tk,), slow=(w == 1))
                    dst = qrb if sname == "q" else krb
                    dk = ("qrb" if sname == "q" else "krb")
                    qc1 = (2 * grp) * LQ + (c0 - 2)
                    qc2 = (2 * grp + 1) * LQ + (c0 - 2)
                    P.tt("dve", rt0[:, 0:w], qraw[:, 0:w], tb[:, 0:w], ALU.mult,
                         reads=("qraw.0", tk), writes=("rt0",))
                    P.tt("dve", rt1[:, 0:w], qraw[:, 512:512 + w],
                         tb[:, 512:512 + w], ALU.mult,
                         reads=("qraw.1", tk), writes=("rt1",))
                    P.tt("dve", dst[:, qc1:qc1 + w], rt0[:, 0:w], rt1[:, 0:w],
                         ALU.subtract, reads=("rt0", "rt1"),
                         writes=(f"{dk}.{grp}.lo.{ti}",))
                    P.tt("dve", rt0[:, 0:w], qraw[:, 512:512 + w], tb[:, 0:w],
                         ALU.mult, reads=("qraw.1", tk), writes=("rt0",))
                    P.tt("dve", rt1[:, 0:w], qraw[:, 0:w], tb[:, 512:512 + w],
                         ALU.mult, reads=("qraw.0", tk), writes=("rt1",))
                    P.tt("dve", dst[:, qc2:qc2 + w], rt0[:, 0:w], rt1[:, 0:w],
                         ALU.add, reads=("rt0", "rt1"),
                         writes=(f"{dk}.{grp}.hi.{ti}",))
            for vi in range(4):
                och = 8 + vi
                g = G[2 + vi % 2]
                gk = f"G{2 + vi % 2}"
                P.mmg([(g[:, 0:w],
                        wqkvb[:, kch * 3 * INNER + och * 128: kch * 3 * INNER + och * 128 + 128],
                        rhs(kch, 1 if kch >= 4 else 0), None)
                       for kch in range(FCH)],
                      reads=("wqkvb", xk), writes=(gk,))
                P.act(vrb[:, vi * LQ + (c0 - 2): vi * LQ + (c0 - 2) + w],
                      g[:, 0:w], AF.Copy, reads=(gk,), writes=(f"vrb.{vi}.{ti}",))
        P.barrier()
        stq.close()
        stVT = ExitStack()
        VTb = stVT.enter_context(nc.sbuf_tensor("VTb" + str(l), [128, 20 * 512],
                                                BF16, side="right"))
        SVTb = stVT.enter_context(nc.sbuf_tensor("SVTb" + str(l), [128, 5 * 512],
                                                 BF16, side="right"))
        if l == 0:
            P.dma("pool", wqkvb[:], dram["wqkv1"].rearrange("(c p) o -> p c o", p=128),
                  writes=("wqkvb",))

        vkeys = tuple(f"vrb.{vi}.{ti}" for vi in range(4) for ti in range(6))
        if l == 0 and "vrb" in dbg:
            dv0 = dout("dbg_vrb0", [128, 4 * LQ], BF16)
            P.dma("sp", dv0, vrb[:], reads=vkeys, writes=("dbv0",))
        qkeys = lambda dk: tuple(f"{dk}.{g_}.{h_}.{ti}" for g_ in range(2)
                                 for h_ in ("lo", "hi") for ti in range(6))
        for k in range(20):
            g = G[k % 2]
            gk = f"G{k % 2}"
            P.mmg([(g[:, vi * 128:(vi + 1) * 128],
                    vrb[:, vi * LQ + 1 + 128 * k: vi * LQ + 1 + 128 * k + 128],
                    ident[:], None) for vi in range(4)],
                  reads=vkeys + ("identb",), writes=(gk,))
            P.cp("dve", VTb[:, k * 512:(k + 1) * 512], g[:, 0:512],
                 reads=(gk,), writes=(f"VTb.{k}",))
        for mtx in range(5):
            g = G[2 + mtx % 2]
            gk = f"G{2 + mtx % 2}"
            P.mmg([(g[:, vi * 128:(vi + 1) * 128],
                    vrb[:, vi * LQ + 128 * mtx: vi * LQ + 128 * mtx + 128],
                    ident[:], None) for vi in range(4)],
                  reads=vkeys + ("identb",), writes=(gk,))
            P.cp("dve", SVTb[:, mtx * 512:(mtx + 1) * 512], g[:, 0:512],
                 reads=(gk,), writes=(f"SVTb.{mtx}",))
        P.barrier()
        stVR.close()

        P.barrier()
        stAt = ExitStack()
        sA = lambda nm, shp, dt=BF16: stAt.enter_context(
            nc.sbuf_tensor(nm + str(l), shp, dt, side="right"))
        SsbP = [sA(f"SsbP{i}", [128, 1280], F32) for i in range(2)]
        maskv = sA("maskv", [128, 4 * 1280])
        maskc = sA("maskc", [128, 1280])
        masksp = sA("masksp", [128, 1280])
        P.dma("sp", maskc[:], dram["maskb"][4 * 128:5 * 128, :], writes=("maskc",))
        P.dma("sp", maskv.ap().rearrange("p (s m) -> p s m", m=1280),
              dram["maskb"].rearrange("(s p) m -> p s m", p=128)[:, 0:4, :],
              writes=("maskv",))
        P.dma("sp", masksp[:], dram["maskspec"], writes=("masksp",))
        PsbP = [sA(f"PsbP{i}", [128, 1280]) for i in range(4)]
        PTsbP = [sA(f"PTsbP{i}", [128, 1280]) for i in range(2)]
        dgbP = [sA(f"dgbP{i}", [128, 256]) for i in range(4)]
        sml = sA("sml", [128, 16], F32)
        oxb = sA("oxb", [128, 512])

        tasks = []
        for qt in [-1] + list(range(16)):
            for h in range(8):
                tasks.append((qt, h))
        NT = len(tasks)
        NG = NT // 2
        qk_r = qkeys("qrb") + qkeys("krb")
        scores2 = ARENA.ap().rearrange("p (s c) -> p s c", c=1024)[:, 0:2, 0:640]
        SCKEYS = ("SCA", "BKA", "SCB", "BKB")

        def tinfo(i):
            qt, h = tasks[i]
            sl_ = i % 4
            p = i % 2
            spec = (qt < 0)
            kwc = 0 if spec else 1 + 128 * qt
            qc = 385 if spec else 513 + 128 * qt
            if spec:
                mask, mkk = masksp[:], "masksp"
            elif qt < 4:
                mask, mkk = maskv[:, qt * 1280:(qt + 1) * 1280], "maskv"
            else:
                mask, mkk = maskc[:], "maskc"
            grp = h // 4
            hb = 32 * (h % 4)
            return dict(sl=sl_, p=p, spec=spec, h=h, qt=qt, qtidx=i // 8,
                        hb=hb, mask=mask, mkk=mkk,
                        qlo=qrb[hb:hb + 32, (2 * grp) * LQ + qc: (2 * grp) * LQ + qc + 128],
                        qhi=qrb[hb:hb + 32, (2 * grp + 1) * LQ + qc: (2 * grp + 1) * LQ + qc + 128],
                        klo=krb[hb:hb + 32, (2 * grp) * LQ + kwc: (2 * grp) * LQ + kwc + 640],
                        khi=krb[hb:hb + 32, (2 * grp + 1) * LQ + kwc: (2 * grp + 1) * LQ + kwc + 640])

        def sc_g(p):
            return ARENA[:, 1024 * p:1024 * p + 512], ("SCA", "SCB")[p]

        def sc_t(p):
            return ARENA[:, 1024 * p + 512:1024 * p + 640], ("BKA", "BKB")[p]

        def pt_g(p):
            return ARENA[:, 2048 + 512 * p:2560 + 512 * p], ("PTA", "PTB")[p]

        def pt_t(p):
            return ARENA[:, 1024 * p + 640:1024 * p + 768], ("BKA", "BKB")[p]

        def stage_A(g):
            ts_ = [tinfo(2 * g), tinfo(2 * g + 1)]
            parts = []
            for t in ts_:
                parts.append((sc_g(t["p"])[0], t["qlo"],
                              t["klo"][:, 0:512], True, False, (t["hb"], 0)))
            for t in ts_:
                parts.append((sc_g(t["p"])[0], t["qhi"],
                              t["khi"][:, 0:512], False, True, (t["hb"], 0)))
            for t in ts_:
                parts.append((sc_t(t["p"])[0], t["qlo"],
                              t["klo"][:, 512:640], True, False, (t["hb"], 0)))
            for t in ts_:
                parts.append((sc_t(t["p"])[0], t["qhi"],
                              t["khi"][:, 512:640], False, True, (t["hb"], 0)))
            P.mseq(parts, reads=qk_r, writes=SCKEYS)

        def stage_BC(g):
            j = g % 2
            ts_ = [tinfo(2 * g), tinfo(2 * g + 1)]
            assert ts_[0]["mkk"] == ts_[1]["mkk"]
            sk = f"SsbP{j}"
            nmk = f"nmP{j}"
            nmP = sml[:, 2 * j:2 * j + 2]
            sv = SsbP[j].ap().rearrange("p (s c) -> p s c", c=640)
            P.tt("dve", sv, scores2,
                 ts_[0]["mask"].rearrange("p (s c) -> p s c", c=640),
                 ALU.add, reads=SCKEYS + (ts_[0]["mkk"],), writes=(sk,))
            P.red(nmP, sv, ALU.max, reads=(sk,), writes=(nmk,))
            P.ts("dve", nmP, nmP, -1.0, ALU.mult, reads=(nmk,), writes=(nmk,))
            jj = g % 4
            for p, t in enumerate(ts_):
                sl_ = t["sl"]
                sm1 = sml[:, 4 + sl_:5 + sl_]
                P.act(PsbP[jj][:, 640 * p:640 * p + 640],
                      SsbP[j][:, 640 * p:640 * p + 640], AF.Exp,
                      bias=sml[:, 2 * j + p:2 * j + p + 1], accum_out=sm1,
                      reads=(sk, nmk), writes=(f"PsbP{jj}", f"sm.{sl_}"))
            j2 = (2 * g) % 4
            P.recip(sml[:, 8 + j2:10 + j2], sml[:, 4 + j2:6 + j2],
                    reads=(f"sm.{ts_[0]['sl']}", f"sm.{ts_[1]['sl']}"),
                    writes=(f"rc.{ts_[0]['sl']}", f"rc.{ts_[1]['sl']}"))
            for p, t in enumerate(ts_):
                sl_ = t["sl"]
                rc = sml[:, 8 + sl_:9 + sl_]
                P.ts("dve", dgbP[jj][:, 128 * p:128 * p + 128], ident[:], rc,
                     ALU.mult, reads=(f"rc.{sl_}", "identb"),
                     writes=(f"dgbP{jj}",))

        def stage_D(g):
            ts_ = [tinfo(2 * g), tinfo(2 * g + 1)]
            j = g % 2
            jj = g % 4
            parts = []
            wr_ = []
            for p, t in enumerate(ts_):
                wr_ += [pt_g(p)[1], pt_t(p)[1]]
                for mtp in range(4):
                    parts.append((pt_g(p)[0][:, mtp * 128:(mtp + 1) * 128],
                                  PsbP[jj][:, 640 * p + mtp * 128: 640 * p + mtp * 128 + 128],
                                  dgbP[jj][:, 128 * p:128 * p + 128],
                                  True, True, None))
                parts.append((pt_t(p)[0], PsbP[jj][:, 640 * p + 512:640 * p + 640],
                              dgbP[jj][:, 128 * p:128 * p + 128],
                              True, True, None))
            P.mseq(parts, reads=(f"PsbP{jj}", f"dgbP{jj}"), writes=tuple(wr_))
            ptk = f"PTP{j}"
            P.act(PTsbP[j].ap().rearrange("p (s c) -> p s c", c=640)[:, :, 0:512],
                  ARENA[:, 2048:3072].rearrange("p (s c) -> p s c", c=512),
                  AF.Copy, reads=("PTA", "PTB"), writes=(ptk,))
            P.act(PTsbP[j].ap().rearrange("p (s c) -> p s c", c=640)[:, :, 512:640],
                  ARENA.ap().rearrange("p (s c) -> p s c", c=1024)[:, 0:2, 640:768],
                  AF.Copy, reads=("BKA", "BKB"), writes=(ptk,))

        def stage_E(g):
            ts_ = [tinfo(2 * g), tinfo(2 * g + 1)]
            j = g % 2
            parts = []
            rd_ = [f"PTP{j}"]
            wr_ = []
            for mtp in range(5):
                for t in ts_:
                    h = t["h"]
                    p = t["p"]
                    rb_ = 64 * p
                    cb = 128 * (h // 2)
                    if t["spec"]:
                        vt = SVTb[:, mtp * 512 + 64 * h: mtp * 512 + 64 * h + 64]
                        rd_.append(f"SVTb.{mtp}")
                    else:
                        vt = VTb[:, (t["qt"] + mtp) * 512 + 64 * h:
                                 (t["qt"] + mtp) * 512 + 64 * h + 64]
                        rd_.append(f"VTb.{t['qt'] + mtp}")
                    wr_.append(f"H{p}")
                    parts.append((H[p][rb_:rb_ + 64, cb:cb + 128], vt,
                                  PTsbP[j][:, 640 * p + mtp * 128: 640 * p + mtp * 128 + 128],
                                  mtp == 0, mtp == 4, None))
            P.mseq(parts, reads=tuple(set(rd_)), writes=tuple(set(wr_)))
            if g % 4 == 3:
                if ts_[0]["spec"]:
                    P.cp("dve", oxb[0:64, :], H[0][0:64, 0:512],
                         reads=("H0",), writes=("oxb",))
                    P.cp("dve", oxb[64:128, :], H[1][64:128, 0:512],
                         reads=("H1",), writes=("oxb",))
                else:
                    qt = ts_[0]["qt"]
                    dst = oTb.ap().rearrange("p (c t) -> p c t", t=TPC)
                    P.cp("dve", dst[0:64, :, 128 * qt:128 * qt + 128],
                         H[0].rearrange("p (c t) -> p c t", t=128)[0:64],
                         reads=("H0",), writes=("oTb",))
                    P.cp("dve", dst[64:128, :, 128 * qt:128 * qt + 128],
                         H[1].rearrange("p (c t) -> p c t", t=128)[64:128],
                         reads=("H1",), writes=("oTb",))

        for g in range(NG + 4):
            if 3 <= g < NG + 3:
                stage_D(g - 3)
            if 4 <= g:
                stage_E(g - 4)
            if g < NG:
                stage_A(g)
                stage_BC(g)

        for och in range(FCH):
            P.mmg([(H[0][:, och:och + 1],
                    wob[:, ch * D + och * 128: ch * D + och * 128 + 128],
                    oxb[:, ch * 128 + 127: ch * 128 + 128], None)
                   for ch in range(4)],
                  reads=("wob", "oxb"), writes=("H0",))
        P.tt("dve", xmx[:], H[0][:, 0:FCH], xh1[:], ALU.add,
             reads=("H0", "xh1"), writes=("xmx",))
        P.ts("dve", xmx[:], xmx[:], validt[:, 0:1], ALU.mult,
             reads=("xmx", "validp"), writes=("xmx",))
        if l == 0 and "lyr" in dbg:
            dq = dout("dbg_qrb", [128, 4 * LQ], BF16)
            dk2 = dout("dbg_krb", [128, 4 * LQ], BF16)
            dv2 = dout("dbg_vrb", [128, 4 * LQ], BF16)
            do2 = dout("dbg_oTb", [128, 4 * TPC], BF16)
            P.dma("sp", dq, qrb[:], reads=qkeys("qrb"), writes=("dbq",))
            P.dma("sp", dk2, krb[:], reads=qkeys("krb"), writes=("dbk",))
            P.dma("sp", dv2, vrb[:], reads=vkeys, writes=("dbv",))
            P.dma("sp", do2, oTb[:], reads=("oTb",), writes=("dbo",))
        P.barrier()
        stAt.close()
        stVT.close()
        stL2.close()

        stH2 = ExitStack()
        sH = lambda nm, shp, dt=F32: stH2.enter_context(nc.sbuf_tensor(nm + str(l), shp, dt))
        xsm_sb = sH("xsm_sb", [128, FCH * 513])
        xsit = [sH(f"xsit{i}", [128, 512]) for i in range(4)]
        h2c = sH("h2c", [128, 512], BF16)
        sqh = sH("sqh", [128, 512], BF16)
        zb4 = sH("zb4", [128, 512])
        zb4b = sH("zb4b", [128, 512])
        for f in range(FCH):
            P.cp("dve", xsm_sb[:, f * 513:f * 513 + 1], xmx[:, f:f + 1],
                 reads=("xmx",), writes=(f"xsm.{f}",))

        def hslice(f):
            base = f * 513
            return xsm_sb[:, base + (1 if f < 4 else 0): base + (1 if f < 4 else 0) + 512]

        for tt in range(4):
            col0 = tt * 512
            for f in range(FCH):
                hk = f"H{f % 2}"
                h = H[f % 2]
                P.mmg([(h[:, 0:512],
                        wob[:, ch * D + f * 128: ch * D + f * 128 + 128],
                        oTb[:, ch * TPC + col0: ch * TPC + col0 + 512], None)
                       for ch in range(4)],
                      reads=("wob", "oTb"), writes=(hk,))
                xk = f"xsit{f % 4}"
                P.dma("sp", xsit[f % 4][:],
                      xs_in_d[f * 128:(f + 1) * 128, col0:col0 + 512],
                      writes=(xk,))
                mk = f"xsm.{f}"
                P.tt("dve", xsm_sb[:, f * 513 + 1: f * 513 + 513],
                     xsit[f % 4][:], h[:, 0:512], ALU.add,
                     reads=(xk, hk, mk), writes=(mk,))
                P.dma("act", dram["sp_xsm"][f * 128:(f + 1) * 128, 1 + col0:1 + col0 + 512],
                      xsm_sb[:, f * 513 + 1: f * 513 + 513],
                      reads=(mk,), writes=("d_xsm",))
                P.act(h2c[:], hslice(f), AF.Copy, reads=(mk,), writes=("h2c",))
                P.tt("dve", sqh[:], hslice(f), hslice(f), ALU.mult,
                     reads=(mk,), writes=("sqh",))
                P.mm1(G[2][0:1, 0:512], onesb[:, 0:1], h2c[:],
                      start=(f == 0), stop=(f == FCH - 1),
                      reads=("h2c", "onesb"), writes=("G2",))
                P.mm1(G[3][0:1, 0:512], onesb[:, 0:1], sqh[:],
                      start=(f == 0), stop=(f == FCH - 1),
                      reads=("sqh", "onesb"), writes=("G3",))
            ra, rb, rak, rbk = ln_stats_epi(G[2], G[3], "G2", "G3")
            for f in range(FCH):
                mk = f"xsm.{f}"
                zeng = "pool" if f % 3 == 0 else "dve"
                z, zk = (zb4, "zb4") if zeng == "pool" else (zb4b, "zb4b")
                P.tt(zeng, z[:], hslice(f), ra, ALU.mult,
                     reads=(mk, rak), writes=(zk,))
                P.tt(zeng, z[:], z[:], rb, ALU.subtract,
                     reads=(zk, rbk), writes=(zk,))
                P.ts(zeng, h2b[:, f * TPC + tt * 512: f * TPC + tt * 512 + 512],
                     z[:], lngt[:, l * FCH + f:l * FCH + f + 1], ALU.mult,
                     reads=(zk, "lng", "lnb"), writes=(f"h2b.{f}.{tt}",),
                     s2=lnbt[:, l * FCH + f:l * FCH + f + 1], op1=ALU.add)
                if tt < 3:
                    P.cp("dve", xsm_sb[:, f * 513:f * 513 + 1],
                         xsm_sb[:, f * 513 + 512:f * 513 + 513],
                         reads=(mk,), writes=(mk,))
        if l == 0 and "lyr" in dbg:
            dxm = dout("dbg_xsm", [D, 1 + TPC])
            P.dma("sp", dram["sp_xsm"].rearrange("(ff p) t -> p ff t", p=128)[:, :, 0:1],
                  xmx.ap().rearrange("p (ff o) -> p ff o", o=1),
                  reads=("xmx",), writes=("d_xsm",), slow=True)
            P.dma("sp", dxm, dram["sp_xsm"], reads=("d_xsm",), writes=("dbxm",))
        P.barrier()
        stH2.close()
        stL1.close()
        if l == 0:
            P.dma("pool", wob[:], dram["wo1"].rearrange("(c p) o -> p c o", p=128),
                  writes=("wob",))

        stW = ExitStack()
        sW = lambda nm, shp, dt=BF16: sW_st.enter_context(nc.sbuf_tensor(nm + str(l), shp, dt))
        sW_st = stW
        hgelu = stW.enter_context(nc.sbuf_tensor("hgelu" + str(l), [128, NBLK * TPC],
                                                 BF16, side="right"))
        hh = [sW(f"hh{i}", [128, 512], F32) for i in range(3)]
        glt = sW("glt", [128, 512])
        abt = sW("abt", [128, 512])
        w1b = [sW(f"w1b{i}", [128, FCH * 256]) for i in range(2)]
        w2b = [sW(f"w2b{i}", [128, NBLK * 128]) for i in range(2)]
        tf = sW("tf", [128, 512], F32)
        xsot = sW("xsot", [128, 512], F32)
        xsob = sW("xsob", [128, 512])

        h2keys = tuple(f"h2b.{f}.{tt}" for f in range(FCH) for tt in range(4))
        for i in range(NBLK):
            wk = f"w1b{i % 2}"
            P.dma("pool", w1b[i % 2].ap().rearrange("p (c o) -> p c o", o=256),
                  dram[f"w1_{l}"].rearrange("(c p) o -> p c o", p=128)[:, :, 256 * i:256 * i + 256],
                  writes=(wk,))
            for tt in range(4):
                ga = G[(2 * tt) % 4]
                gg = G[(2 * tt + 1) % 4]
                gak = f"G{(2 * tt) % 4}"
                ggk = f"G{(2 * tt + 1) % 4}"
                P.mmg([(ga[:, 0:512], w1b[i % 2][:, kch * 256: kch * 256 + 128],
                        h2b[:, kch * TPC + tt * 512: kch * TPC + tt * 512 + 512],
                        None) for kch in range(FCH)],
                      reads=(wk,) + h2keys, writes=(gak,))
                P.mmg([(gg[:, 0:512], w1b[i % 2][:, kch * 256 + 128: kch * 256 + 256],
                        h2b[:, kch * TPC + tt * 512: kch * TPC + tt * 512 + 512],
                        None) for kch in range(FCH)],
                      reads=(wk,) + h2keys, writes=(ggk,))
                P.act(glt[:], gg[:, 0:512], AF.Gelu,
                      bias=b1tt[:, l * 44 + 2 * i + 1:l * 44 + 2 * i + 2],
                      reads=(ggk, "b1t"), writes=("glt",))
                P.ts("dve", abt[:], ga[:, 0:512],
                     b1tt[:, l * 44 + 2 * i:l * 44 + 2 * i + 1], ALU.add,
                     reads=(gak, "b1t"), writes=("abt",))
                P.tt("dve", hgelu[:, i * TPC + tt * 512: i * TPC + tt * 512 + 512],
                     abt[:], glt[:], ALU.mult, reads=("abt", "glt"),
                     writes=(f"hg.{i}.{tt}",))
        hgkeys = tuple(f"hg.{i}.{tt}" for i in range(NBLK) for tt in range(4))
        for och in range(FCH):
            wk = f"w2b{och % 2}"
            P.dma("pool", w2b[och % 2].ap().rearrange("p (c o) -> p c o", o=128),
                  dram[f"w2_{l}"].rearrange("(c p) o -> p c o", p=128)[:, :, och * 128:(och + 1) * 128],
                  writes=(wk,))
            for tt in range(4):
                col0 = tt * 512
                hk = f"H{tt % 2}"
                h = H[tt % 2]
                P.mmg([(h[:, 0:512], w2b[och % 2][:, ki * 128:(ki + 1) * 128],
                        hgelu[:, ki * TPC + col0: ki * TPC + col0 + 512], None)
                       for ki in range(NBLK)],
                      reads=(wk,) + hgkeys, writes=(hk,))
                P.ts("dve", tf[:], h[:, 0:512],
                     b2tt[:, l * FCH + och:l * FCH + och + 1], ALU.add,
                     reads=(hk, "b2t"), writes=("tf",))
                xk = f"hh{(och * 4 + tt) % 3}"
                P.dma("sp", hh[(och * 4 + tt) % 3][:],
                      dram["sp_xsm"][och * 128:(och + 1) * 128, 1 + col0:1 + col0 + 512],
                      reads=("d_xsm",), writes=(xk,))
                P.tt("dve", xsot[:], tf[:], hh[(och * 4 + tt) % 3][:], ALU.add,
                     reads=("tf", xk), writes=("xsot",))
                P.dma("sp", xs_out_d[och * 128:(och + 1) * 128, col0:col0 + 512],
                      xsot[:], reads=("xsot",), writes=("d_xsout",))
                if write_next_xsb:
                    P.act(xsob[:], xsot[:], AF.Copy, reads=("xsot",),
                          writes=("xsob",))
                    P.dma("sp", dram["d_xsb"][och * 128:(och + 1) * 128,
                                              QCOL0 + col0:QCOL0 + col0 + 512],
                          xsob[:], reads=("xsob",), writes=("d_xsb_own",))
        P.barrier()
        stW.close()
        stH1.close()

    emit_layer(0, dram["sp_xin0"], dram["sp_xin1"], True,
               "d_h2out", dram["h2_out"])
    if upto == "layer1":
        if "xs1" in dbg:
            dx = dout("dbg_xs1", [D, TPC])
            P.dma("sp", dx, dram["sp_xin1"], reads=("d_xsout",), writes=("dbg3",))
        P.barrier()
        P.emit()
        stXB.close()
        warena.close()
        es.close()
        return nc, dram

    P.dma("pool", dram["h3_in"], dram["sp_xin1"][:, TPC - HALO:TPC],
          reads=("d_xsout",), writes=("d_h3in",))
    P.coll("AllGather", ALU.bypass, [[0, 1, 2, 3], [4, 5, 6, 7]],
           [dram["h3_in"]], [dram["h3_out"]], sem="cc3",
           reads=("d_h3in",), writes=("d_h3out",))

    emit_layer(1, dram["sp_xin1"], dram["sp_xs2"], False,
               "d_h3out", dram["h3_out"])

    stF = ExitStack()
    sF = lambda nm, shp, dt=F32: stF.enter_context(nc.sbuf_tensor(nm, shp, dt))
    xsf2 = [sF(f"xsf{i}", [128, FCH * 512]) for i in range(2)]
    x1t = [sF(f"x1t{i}", [128, 512]) for i in range(4)]
    xcb2 = [sF(f"xcb2{i}", [128, 512], BF16) for i in range(2)]
    sqf2 = [sF(f"sqf2{i}", [128, 512], BF16) for i in range(2)]
    zb5 = sF("zb5", [128, 512])
    zb5b = sF("zb5b", [128, 512])
    ytb = sF("ytb", [128, 512])
    yt = sF("yt", [128, 512])

    def load_xs(tt):
        j = tt % 2
        col0 = tt * 512
        for f in range(FCH):
            P.dma("act", xsf2[j][:, f * 512:(f + 1) * 512],
                  dram["sp_xs2"][f * 128:(f + 1) * 128, col0:col0 + 512],
                  reads=("d_xsout",), writes=(f"xsf{j}.{f}",))

    load_xs(0)
    for tt in range(4):
        col0 = tt * 512
        j = tt % 2
        xsf = xsf2[j]
        for f in range(FCH):
            sl = slice(f * 512, (f + 1) * 512)
            fk = f"xsf{j}.{f}"
            ck, sk = f"xcb2{f % 2}", f"sqf2{f % 2}"
            P.act(xcb2[f % 2][:], xsf[:, sl], AF.Copy, reads=(fk,),
                  writes=(ck,))
            P.tt("dve", sqf2[f % 2][:], xsf[:, sl], xsf[:, sl], ALU.mult,
                 reads=(fk,), writes=(sk,))
            P.mm1(BIG[0:1, 0:512], onesb[:, 0:1], xcb2[f % 2][:],
                  start=(f == 0), stop=(f == FCH - 1),
                  reads=(ck, "onesb"), writes=("BIGa",))
            P.mm1(BIG[0:1, 512:1024], onesb[:, 0:1], sqf2[f % 2][:],
                  start=(f == 0), stop=(f == FCH - 1),
                  reads=(sk, "onesb"), writes=("BIGb",))
        ra, rb, rak, rbk = ln_stats_epi()
        if tt < 3:
            load_xs(tt + 1)
        for f in range(FCH):
            sl = slice(f * 512, (f + 1) * 512)
            xk = f"x1t{f % 4}"
            P.dma("pool", x1t[f % 4][:], dram["sp_x1"][f * 128:(f + 1) * 128, col0:col0 + 512],
                  reads=("d_x1",), writes=(xk,))
            zeng = "pool" if f % 3 == 0 else "dve"
            z, zk = (zb5, "zb5") if zeng == "pool" else (zb5b, "zb5b")
            y_, yk = (yt, "yt") if zeng == "pool" else (ytb, "ytb")
            P.tt(zeng, z[:], xsf[:, sl], ra, ALU.mult,
                 reads=(f"xsf{j}.{f}", rak), writes=(zk,))
            P.tt(zeng, z[:], z[:], rb, ALU.subtract,
                 reads=(zk, rbk), writes=(zk,))
            P.tt(zeng, y_[:], x1t[f % 4][:], z[:], ALU.add,
                 reads=(xk, zk), writes=(yk,))
            P.dma("sp", dram["y"][f * 128:(f + 1) * 128, col0:col0 + 512],
                  y_[:], reads=(yk,), writes=("d_y",))
    stF.close()

    P.barrier()
    P.emit()
    stXB.close()
    warena.close()
    es.close()
    return nc, dram


_NC_CACHE = {}


def kernel(**inputs):
    in_maps = [host_inputs(inputs, c) for c in range(8)]
    if "all" not in _NC_CACHE:
        _NC_CACHE["all"] = build_nc()
    nc, dram = _NC_CACHE["all"]
    res = run_bass_kernel_spmd(nc, in_maps, core_ids=list(range(8)))
    out = np.empty((B, N, D), np.float32)
    for c in range(8):
        b, s = c // 4, c % 4
        out[b, s * TPC:(s + 1) * TPC, :] = res.results[c]["y"].T
    return out
